# version 04
# baseline (speedup 1.0000x reference)
import numpy as np
import ml_dtypes
import sys
if '/opt/trn_rl_repo' not in sys.path:
    sys.path.insert(0, '/opt/trn_rl_repo')
from contextlib import ExitStack
import concourse.bass as bass
import concourse.mybir as mybir
from concourse.bass_utils import run_bass_kernel_spmd

F32 = mybir.dt.float32
BF16 = mybir.dt.bfloat16
I32 = mybir.dt.int32
AF = mybir.ActivationFunctionType
ALU = mybir.AluOpType
BF = ml_dtypes.bfloat16

H = 4096
DIN = 8192
NCORE = 8
CHUNK = 2048
EPS_LN = 1e-5
EPS_RMS = 1e-6

ENGS = ("sync", "gpsimd", "tensor", "vector", "scalar")
SEMS = ("si", "gj", "gw", "gw2", "q0", "q1",
        "t0", "t1", "t2", "t3", "u0", "u1", "u2", "u3",
        "x0", "x1", "sw0", "sw1", "sw2", "sw3", "sx1", "sxc", "sdt",
        "sbc", "sy", "so", "b0", "b1", "g0", "g1", "gb0", "gb1",
        "g", "tn", "v", "a")


class P:
    def __init__(self, marks, emit):
        self.marks = marks
        self.emit = emit
        self.sems = None
        self.cnt = {s: 0 for s in SEMS}
        self.streams = {e: [] for e in ENGS}
        self.lastw = {}

    def do(self, eng, fn, inc=None):
        incs = [inc] if isinstance(inc, tuple) else (inc or [])
        for (s, k) in incs:
            self.cnt[s] += k
        if self.emit:
            def run(e, fn=fn, incs=tuple(incs)):
                ins = fn(e)
                for (s, k) in incs:
                    ins.then_inc(self.sems[s], k)
            self.streams[eng].append(run)

    def mark(self, tag, sem):
        self.marks[tag] = (sem, self.cnt[sem])

    def wait(self, eng, tag):
        if self.emit and tag in self.marks:
            sem, val = self.marks[tag]
            if val > 0:
                self.streams[eng].append(
                    lambda e, s=sem, v=val: e.wait_ge(self.sems[s], v))

    def wait_val(self, eng, sem, val):
        if self.emit and val > 0:
            self.streams[eng].append(
                lambda e, s=sem, v=val: e.wait_ge(self.sems[s], v))

    def wait_self(self, eng, sem):
        self.wait_val(eng, sem, self.cnt[sem])

    def val(self, sem):
        return self.cnt[sem]


def ttiles(n):
    out, o = [], 0
    while o < n:
        w = min(512, n - o)
        out.append((o, w))
        o += w
    return out


def _mk_sems(nc, stack):
    return {n: stack.enter_context(nc.semaphore(n)) for n in SEMS}


def finish(nc, p, stack):
    for s in SEMS:
        if s not in ("g", "tn", "v", "a"):
            p.wait_val("gpsimd", s, p.val(s))
    if p.emit:
        block = stack.enter_context(nc.Block())
        for eng in ENGS:
            ops = p.streams[eng]
            if ops:
                getattr(block, eng)(lambda e, ops=tuple(ops): [f(e) for f in ops])



def emit_consts(p, t, nc, stack, values):
    if p.emit:
        cst = stack.enter_context(nc.sbuf_tensor("cst", [128, len(values)], F32))
        t["cst"] = cst
        for i, v in enumerate(values):
            nc.const_aps.register(F32, v, cst[:, i:i + 1])
    for i, v in enumerate(values):
        p.do("gpsimd", lambda e, i=i, v=v: e.memset(t["cst"][:, i:i + 1], v),
             inc=("g", 1) if i == len(values) - 1 else None)
    p.mark("csts", "g")
    p.wait("scalar", "csts")
    p.wait("vector", "csts")



def emit_dequant(p, t):
    for nt in range(32):
        pb = nt % 2
        p.wait("sync", f"dq_unp_{nt - 2}")
        p.do("sync", lambda e, nt=nt, pb=pb: e.dma_start(
            out=t["sqw"][pb][:], in_=t["qw_d"][nt * 128:(nt + 1) * 128, :]),
            inc=(f"q{pb}", 16))
        p.mark(f"dq_ld_{nt}", f"q{pb}")
        p.wait("vector", f"dq_ld_{nt}")
        for wb in range(4):
            p.wait("vector", f"dq_u{wb}_{nt - 1}")
        for hh in range(2):
            for (j, sj, op, sc) in ((0, 0, ALU.bitwise_and, 63),
                                    (1, 0, ALU.logical_shift_right, 6),
                                    (2, 1, ALU.bitwise_and, 63),
                                    (3, 2, ALU.bitwise_and, 63)):
                p.do("vector", lambda e, pb=pb, j=j, sj=sj, op=op, sc=sc, hh=hh:
                     e.tensor_single_scalar(
                         out=t["wvi"][:].rearrange("q (g f) -> q g f", f=4)[:, :, j],
                         in_=t["sqw"][pb][:, hh * 1536:(hh + 1) * 1536]
                         .rearrange("q (g f) -> q g f", f=3)[:, :, sj],
                         scalar=sc, op=op), inc=("v", 1))
            p.wait_self("vector", "v")
            p.do("vector", lambda e: e.tensor_copy(out=t["wvf"][:], in_=t["wvi"][:]),
                 inc=("v", 1))
            p.wait_self("vector", "v")
            p.do("vector", lambda e, nt=nt, hh=hh: e.tensor_scalar(
                out=t["wnb"][:, hh * 2048:(hh + 1) * 2048], in0=t["wvf"][:],
                scalar1=t["qzp"][:, nt:nt + 1], scalar2=t["qscale"][:, nt:nt + 1],
                op0=ALU.subtract, op1=ALU.mult), inc=("v", 1))
        p.mark(f"dq_unp_{nt}", "v")
        p.mark(f"dq_w_{nt}", "v")
        p.wait("sync", f"dq_w_{nt}")
        for kb in range(32):
            si = nt * 32 + kb
            wb = si % 4
            p.wait_val("sync", f"u{wb}", p.val(f"u{wb}"))
            p.do("sync", lambda e, kb=kb, wb=wb: e.dma_start_transpose(
                out=t["wtt"][wb][:], in_=t["wnb"][:, kb * 128:(kb + 1) * 128]),
                inc=(f"t{wb}", 16))
            p.wait_val("scalar", f"t{wb}", p.val(f"t{wb}"))
            p.do("scalar", lambda e, kb=kb, nt=nt, wb=wb: e.dma_start(
                out=t["wt_dram"][kb * 128:(kb + 1) * 128, nt * 128:(nt + 1) * 128],
                in_=t["wtt"][wb][:]), inc=(f"u{wb}", 16))
        for wb in range(4):
            p.mark(f"dq_u{wb}_{nt}", f"u{wb}")
    for wb in range(4):
        p.mark(f"dq_done{wb}", f"u{wb}")



def emit_ln(p, t, pre, W, n_kt, src_fn, out_fn, norm_guard, row_guard,
            e_guard, src_guard_fn=None):
    for kt in range(n_kt):
        pb = kt % 2
        p.wait("sync", f"{pre}mm_{kt - 2}")
        if src_guard_fn is not None:
            p.wait("sync", src_guard_fn(kt))
        p.do("sync", lambda e, kt=kt, pb=pb, W=W: e.dma_start(
            out=t["xsrc"][pb][:, :W], in_=src_fn(kt)), inc=(f"x{pb}", 16))
        p.mark(f"{pre}ld_{kt}", f"x{pb}")
        p.wait("scalar", f"{pre}ld_{kt}")
        p.wait("scalar", f"{pre}mm_{kt - 1}")
        p.do("scalar", lambda e, pb=pb, W=W: e.activation(
            out=t["xb"][:, :W], in_=t["xsrc"][pb][:, :W], func=AF.Copy),
            inc=("a", 1))
        p.mark(f"{pre}xb_{kt}", "a")
        p.wait("vector", f"{pre}xb_{kt}")
        p.wait("vector", f"{pre}mm_{kt - 1}")
        p.do("vector", lambda e, W=W: e.tensor_mul(
            out=t["sqb"][:, :W], in0=t["xb"][:, :W], in1=t["xb"][:, :W]),
            inc=("v", 1))
        p.mark(f"{pre}sq_{kt}", "v")
        if kt == 0:
            p.wait("tensor", row_guard)
        p.wait("tensor", f"{pre}xb_{kt}")
        p.wait("tensor", f"{pre}sq_{kt}")
        p.do("tensor", lambda e, kt=kt, W=W: e.matmul(
            t["ps_row"][0:1, 0:W], t["onesb"][:], t["xb"][:, :W],
            start=(kt == 0), stop=(kt == n_kt - 1)))
        p.do("tensor", lambda e, kt=kt, W=W: e.matmul(
            t["ps_row"][0:1, 512:512 + W], t["onesb"][:], t["sqb"][:, :W],
            start=(kt == 0), stop=(kt == n_kt - 1)), inc=("tn", 1))
        p.mark(f"{pre}mm_{kt}", "tn")
    p.mark(f"{pre}stats", "tn")
    p.wait("vector", f"{pre}stats")
    p.do("vector", lambda e, W=W: e.tensor_scalar(
        out=t["rows"][0:1, :W], in0=t["ps_row"][0:1, 0:W],
        scalar1=1.0 / (128 * n_kt), scalar2=None, op0=ALU.mult), inc=("v", 1))
    p.wait_self("vector", "v")
    p.do("vector", lambda e, W=W: e.tensor_mul(
        out=t["rows"][0:1, 512:512 + W], in0=t["rows"][0:1, :W],
        in1=t["rows"][0:1, :W]), inc=("v", 1))
    p.wait_self("vector", "v")
    p.do("vector", lambda e, W=W: e.scalar_tensor_tensor(
        out=t["rows"][0:1, 512:512 + W], in0=t["ps_row"][0:1, 512:512 + W],
        scalar=1.0 / (128 * n_kt), in1=t["rows"][0:1, 512:512 + W],
        op0=ALU.mult, op1=ALU.subtract), inc=("v", 1))
    p.mark(f"{pre}var", "v")
    p.wait("scalar", f"{pre}var")
    p.do("scalar", lambda e, W=W: e.activation(
        out=t["rows"][0:1, 512:512 + W], in_=t["rows"][0:1, 512:512 + W],
        func=AF.Ln, bias=EPS_LN), inc=("a", 1))
    p.wait_self("scalar", "a")
    p.do("scalar", lambda e, W=W: e.activation(
        out=t["rows"][0:1, 512:512 + W], in_=t["rows"][0:1, 512:512 + W],
        func=AF.Exp, scale=-0.5), inc=("a", 1))
    p.mark(f"{pre}rstd", "a")
    p.wait("vector", f"{pre}rstd")
    p.do("vector", lambda e, W=W: e.tensor_mul(
        out=t["rows"][0:1, :W], in0=t["rows"][0:1, :W],
        in1=t["rows"][0:1, 512:512 + W]), inc=("v", 1))
    p.wait_self("vector", "v")
    p.do("vector", lambda e, W=W: e.tensor_copy(
        out=t["rowb"][0:1, 0:W], in_=t["rows"][0:1, 512:512 + W]))
    p.do("vector", lambda e, W=W: e.tensor_copy(
        out=t["rowb"][0:1, 512:512 + W], in_=t["rows"][0:1, :W]), inc=("v", 1))
    p.mark(f"{pre}rowb", "v")
    p.wait("tensor", f"{pre}rowb")
    p.wait("tensor", e_guard)
    p.do("tensor", lambda e, W=W: e.matmul(
        t["ps_e"][:, 0:W], t["ones1"][:], t["rowb"][0:1, 0:W],
        start=True, stop=True), inc=("tn", 1))
    p.mark(f"{pre}bc0", "tn")
    p.wait("scalar", f"{pre}bc0")
    p.do("scalar", lambda e, W=W: e.activation(
        out=t["bc32"][:, 0:W], in_=t["ps_e"][:, 0:W], func=AF.Copy), inc=("a", 1))
    p.mark(f"{pre}bce0", "a")
    p.wait("tensor", f"{pre}bce0")
    p.do("tensor", lambda e, W=W: e.matmul(
        t["ps_e"][:, 0:W], t["ones1"][:], t["rowb"][0:1, 512:512 + W],
        start=True, stop=True), inc=("tn", 1))
    p.mark(f"{pre}bc1", "tn")
    p.wait("scalar", f"{pre}bc1")
    p.do("scalar", lambda e, W=W: e.activation(
        out=t["bc32"][:, 512:512 + W], in_=t["ps_e"][:, 0:W], func=AF.Copy),
        inc=("a", 1))
    p.mark(f"{pre}bce1", "a")
    for kt in range(n_kt):
        pb = kt % 2
        p.wait("sync", f"{pre}n_{kt - 2}")
        if kt < 2:
            p.wait("sync", f"{pre}xb_{n_kt - 1}")
        if src_guard_fn is not None:
            p.wait("sync", src_guard_fn(kt))
        p.do("sync", lambda e, kt=kt, pb=pb, W=W: e.dma_start(
            out=t["xsrc"][pb][:, :W], in_=src_fn(kt)), inc=(f"x{pb}", 16))
        p.mark(f"{pre}l2_{kt}", f"x{pb}")
        p.wait("vector", f"{pre}l2_{kt}")
        p.wait("vector", f"{pre}bce1")
        if kt == 0 and norm_guard:
            p.wait("vector", norm_guard)
        p.wait("vector", f"{pre}n_{kt - 1}")
        p.do("vector", lambda e, pb=pb, W=W: e.tensor_mul(
            out=t["tmp32"][:, :W], in0=t["xsrc"][pb][:, :W], in1=t["bc32"][:, 0:W]),
            inc=("v", 1))
        p.wait_self("vector", "v")
        p.do("vector", lambda e, kt=kt, W=W: e.tensor_sub(
            out=out_fn(kt), in0=t["tmp32"][:, :W], in1=t["bc32"][:, 512:512 + W]),
            inc=("v", 1))
        p.mark(f"{pre}n_{kt}", "v")
    p.mark(f"{pre}done", "v")



def emit_ffn(p, t, pre, W, wdram, evac_fn, rhs_guard, war_guards=(),
             wt_guards=()):
    for nq in range(8):
        for k in range(32):
            pb = k % 4
            qe = "sync" if k % 2 == 0 else "scalar"
            if p.lastw.get(pb):
                p.wait(qe, p.lastw[pb])
            if nq == 0 and k < 4:
                for wg in wt_guards:
                    p.wait(qe, wg)
            p.do(qe, lambda e, nq=nq, k=k, pb=pb: e.dma_start(
                out=t["wstr"][pb][:], in_=wdram(k, nq)), inc=(f"sw{pb}", 16))
            p.mark(f"{pre}w_{nq}_{k}", f"sw{pb}")
            p.wait("tensor", f"{pre}w_{nq}_{k}")
            if k == 0:
                p.wait("tensor", rhs_guard)
                if nq == 0:
                    for wg in war_guards:
                        p.wait("tensor", wg)
                for ns in range(4):
                    p.wait("tensor", f"{pre}free_{nq - 1}_{ns}")
            for ns in range(4):
                p.do("tensor", lambda e, k=k, ns=ns, pb=pb, W=W: e.matmul(
                    t["ps"][ns][:, :W], t["wstr"][pb][:, ns * 128:(ns + 1) * 128],
                    t["xnsl"][:, k * 512:k * 512 + W],
                    start=(k == 0), stop=(k == 31)),
                    inc=("tn", 1) if ns == 3 else None)
            p.mark(f"{pre}c_{nq}_{k}", "tn")
            p.lastw[pb] = f"{pre}c_{nq}_{k}"
        p.mark(f"{pre}g_{nq}", "tn")
        for ns in range(4):
            evac_fn(p, nq, ns, nq * 4 + ns)
    p.mark(f"{pre}mmdone", "tn")



def build_A(marks, emit, n_real=CHUNK, nc=None, stack=None):
    n_tok = n_real + 3
    tts = ttiles(n_tok)
    p = P(marks, emit)
    t = {}
    if emit:
        dram, sb, psa = nc.dram_tensor, nc.sbuf_tensor, nc.psum_tensor
        t["x_d"] = dram("x", [H, n_tok], F32, kind="ExternalInput").ap()
        t["wip_d"] = dram("wip", [H, 2 * DIN], F32, kind="ExternalInput").ap()
        t["qw_d"] = dram("qw", [H, 3072], I32, kind="ExternalInput").ap()
        t["qscale_d"] = dram("qscale", [128, 32], F32, kind="ExternalInput").ap()
        t["qzp_d"] = dram("qzp", [128, 32], F32, kind="ExternalInput").ap()
        t["fbias_d"] = dram("fbias", [128, 32], F32, kind="ExternalInput").ap()
        t["xprojT_d"] = dram("xprojT", [DIN, 288], F32, kind="ExternalInput").ap()
        t["dtprojT_d"] = dram("dtprojT", [256, DIN], F32, kind="ExternalInput").ap()
        t["dtbias_d"] = dram("dtbias", [128, 64], F32, kind="ExternalInput").ap()
        t["dtlnw_d"] = dram("dtlnw", [128, 2], F32, kind="ExternalInput").ap()
        t["bclnw_d"] = dram("bclnw", [32, 1], F32, kind="ExternalInput").ap()
        t["convw_d"] = dram("convw", [128, 256], F32, kind="ExternalInput").ap()
        t["convb_d"] = dram("convb", [128, 64], F32, kind="ExternalInput").ap()
        t["bpat_d"] = dram("bpat_i", [32, 2], F32, kind="ExternalInput").ap()
        t["hmask_d"] = dram("hmask", [128, 4], F32, kind="ExternalInput").ap()
        t["bpat2_d"] = dram("bpat2_i", [2, 32], F32, kind="ExternalInput").ap()
        t["x1_o"] = dram("x1o", [H, n_real], F32, kind="ExternalOutput").ap()
        t["dt_o"] = dram("dto", [DIN, n_real], F32, kind="ExternalOutput").ap()
        t["xc_o"] = dram("xco", [DIN, n_real], BF16, kind="ExternalOutput").ap()
        t["sz_o"] = dram("szo", [DIN, n_real], BF16, kind="ExternalOutput").ap()
        t["bc_o"] = dram("bco", [32, n_real], BF16, kind="ExternalOutput").ap()
        t["wt_dram"] = dram("wt_dram", [H, H], BF16, kind="Internal").ap()
        t["wip16"] = dram("wip16", [H, 2 * DIN], BF16, kind="Internal").ap()
        t["x1f"] = dram("x1f", [H, n_tok], F32, kind="Internal").ap()
        e = stack.enter_context
        t["sqw"] = [e(sb(f"sqw{i}", [128, 3072], I32)) for i in range(2)]
        t["wvi"] = e(sb("wvi", [128, 2048], I32))
        t["wvf"] = e(sb("wvf", [128, 2048], F32))
        t["wnb"] = e(sb("wnb", [128, 4096], BF16))
        t["wtt"] = [e(sb(f"wtt{i}", [128, 128], BF16)) for i in range(4)]
        t["qscale"] = e(sb("qscales", [128, 32], F32))
        t["qzp"] = e(sb("qzps", [128, 32], F32))
        t["hb"] = e(sb("hbs", [128, 32], F32))
        t["onesb"] = e(sb("onesb", [128, 1], BF16))
        t["ones1"] = e(sb("ones1", [1, 128], BF16))
        t["bpat"] = e(sb("bpat", [32, 2], BF16))
        t["hmask"] = e(sb("hmask_s", [128, 4], F32))
        t["bpat2"] = e(sb("bpat2", [2, 32], BF16))
        t["xsrc"] = [e(sb(f"xsrc{i}", [128, 512], F32)) for i in range(2)]
        t["xb"] = e(sb("xbuf", [128, 512], BF16))
        t["sqb"] = e(sb("sqbuf", [128, 512], BF16))
        t["tmp32"] = e(sb("tmp32", [128, 512], F32))
        t["rows"] = e(sb("rows", [1, 1024], F32))
        t["rows2"] = e(sb("rows2", [2, 512], F32))
        t["rowb"] = e(sb("rowb", [1, 1024], BF16))
        t["rows2b"] = e(sb("rows2b", [2, 512], BF16))
        t["bc32"] = e(sb("bc32", [128, 1024], F32))
        t["wstr"] = [e(sb(f"wstr{i}", [128, 512], BF16)) for i in range(4)]
        t["wcf32"] = [e(sb(f"wcf32{i}", [128, 1024], F32)) for i in range(2)]
        t["wcb16"] = [e(sb(f"wcb16{i}", [128, 1024], BF16)) for i in range(2)]
        t["xnsl"] = e(sb("xnsl", [128, 32 * 512], BF16))
        t["t1"] = e(sb("t1buf", [128, 512], F32))
        t["x1t"] = e(sb("x1t", [128, 512], F32))
        t["xtb"] = e(sb("xtb", [128, 515], F32))
        t["tails"] = e(sb("tails", [128, 64 * 3], F32))
        t["accs"] = [e(sb(f"accs{i}", [128, 512], F32)) for i in range(2)]
        t["xcb"] = e(sb("xcbuf", [128, 512], BF16))
        t["ljar"] = e(sb("ljar", [128, 64 * 288], BF16))
        t["xpj"] = t["ljar"]
        t["convw"] = e(sb("convw_s", [128, 256], F32))
        t["convb"] = e(sb("convb_s", [128, 64], F32))
        t["ssm0"] = e(sb("ssm0", [128, n_real], F32))
        t["ssm1"] = e(sb("ssm1", [128, n_real], F32))
        t["ssm2"] = e(sb("ssm2", [32, n_real], F32))
        t["dtln"] = e(sb("dtln", [128, 2], F32))
        t["bcln"] = e(sb("bcln", [32, 1], F32))
        t["dtrn"] = e(sb("dtrn", [128, 2 * 512], BF16))
        t["bcb16"] = e(sb("bcb16", [32, 512], BF16))
        t["dtpj"] = t["ljar"]
        t["dtbias"] = e(sb("dtbias_s", [128, 64], F32))
        t["sps"] = [e(sb(f"sps{i}", [128, 1024], F32)) for i in range(2)]
        t["spr"] = e(sb("spr", [128, 1024], F32))
        t["ps"] = [e(psa(f"psum{i}", [128, 512], F32)) for i in range(5)]
        t["ps_x2"] = e(psa("ps_x2", [32, 512], F32))
        t["ps_row"] = e(psa("ps_row", [2, 1024], F32))
        t["ps_e"] = t["ps"][4]
        p.sems = _mk_sems(nc, stack)

    emit_consts(p, t, nc, stack, (EPS_LN, EPS_RMS, 1.0, -0.5, -1.0, 0.5))
    for (name, dst) in (("qscale_d", "qscale"), ("qzp_d", "qzp"),
                        ("fbias_d", "hb"), ("dtbias_d", "dtbias"),
                        ("dtlnw_d", "dtln"), ("bclnw_d", "bcln"),
                        ("convw_d", "convw"), ("convb_d", "convb")):
        p.do("sync", lambda e, name=name, dst=dst: e.dma_start(
            out=t[dst][:], in_=t[name][:]), inc=("si", 16))
    p.mark("smalls", "si")
    p.do("gpsimd", lambda e: e.dma_start(
        out=t["xpj"][:].rearrange("q (c r) -> q c r", r=288),
        in_=t["xprojT_d"][:].rearrange("(c q) r -> q c r", q=128)), inc=("gj", 16))
    p.do("gpsimd", lambda e: e.memset(t["onesb"][:], 1.0))
    p.do("gpsimd", lambda e: e.memset(t["ones1"][:], 1.0), inc=("g", 1))
    p.do("gpsimd", lambda e: e.dma_start(out=t["bpat"][:], in_=t["bpat_d"][:]),
         inc=("gj", 16))
    p.do("gpsimd", lambda e: e.dma_start(out=t["bpat2"][:], in_=t["bpat2_d"][:]),
         inc=("gj", 16))
    p.do("gpsimd", lambda e: e.dma_start(out=t["hmask"][:], in_=t["hmask_d"][:]),
         inc=("gj", 16))
    p.mark("hmaskld", "gj")
    p.mark("ones", "g")
    p.mark("ljt", "gj")
    p.mark("bpats", "gj")
    p.wait("vector", "smalls")
    p.do("vector", lambda e: e.tensor_scalar(
        out=t["hb"][:], in0=t["hb"][:], scalar1=0.5, scalar2=None,
        op0=ALU.mult), inc=("v", 1))
    p.mark("hb", "v")
    emit_dequant(p, t)
    for ci in range(512):
        kt, ct = ci // 16, ci % 16
        pb = ci % 2
        sname = "gw" if pb == 0 else "gw2"
        p.wait("sync", f"wipc_{ci - 2}")
        p.do("sync", lambda e, kt=kt, ct=ct, pb=pb: e.dma_start(
            out=t["wcf32"][pb][:], in_=t["wip_d"][kt * 128:(kt + 1) * 128,
                                                  ct * 1024:(ct + 1) * 1024]),
            inc=(sname, 16))
        p.mark(f"wipl_{ci}", sname)
        p.wait("vector", f"wipl_{ci}")
        p.wait("vector", f"wipst_{ci - 2}")
        p.do("vector", lambda e, pb=pb: e.tensor_copy(
            out=t["wcb16"][pb][:], in_=t["wcf32"][pb][:]), inc=("v", 1))
        p.mark(f"wipc_{ci}", "v")
        p.wait("scalar", f"wipc_{ci}")
        p.do("scalar", lambda e, kt=kt, ct=ct, pb=pb: e.dma_start(
            out=t["wip16"][kt * 128:(kt + 1) * 128, ct * 1024:(ct + 1) * 1024],
            in_=t["wcb16"][pb][:]), inc=(f"gb{pb}", 16))
        p.mark(f"wipst_{ci}", f"gb{pb}")
    p.mark("wip16", "gb0")
    p.mark("wip16b", "gb1")

    p.wait("tensor", "ones")
    p.wait("vector", "ones")
    for ti, (tt0, w) in enumerate(tts):
        pre = f"T{ti}."
        lo = max(tt0, 3)
        wo = tt0 + w - lo

        emit_ln(p, t, pre + "ln1.", w, 32,
                lambda kt, tt0=tt0, w=w: t["x_d"][kt * 128:(kt + 1) * 128,
                                                  tt0:tt0 + w],
                lambda kt, w=w: t["xnsl"][:, kt * 512:kt * 512 + w],
                f"T{ti - 1}.ipdone",
                f"T{ti - 1}.ln2.var", f"T{ti - 1}.ln2.bce1")

        if ti == 0:
            for wb in range(4):
                p.wait("tensor", f"dq_done{wb}")
            p.wait("tensor", "hb")

        def evac_ffn1(p, nq, ns, ev, pre=pre, tt0=tt0, w=w, lo=lo, wo=wo):
            n = ev
            p.wait("vector", f"{pre}f1.g_{nq}")
            p.wait("vector", f"{pre}f1.std_{ev - 1}")
            p.do("vector", lambda e, ns=ns, n=n, w=w: e.tensor_scalar(
                out=t["t1"][:, :w], in0=t["ps"][ns][:, :w],
                scalar1=0.5, scalar2=t["hb"][:, n:n + 1],
                op0=ALU.mult, op1=ALU.add), inc=("v", 1))
            p.mark(f"{pre}f1.free_{nq}_{ns}", "v")
            p.mark(f"{pre}f1.t1_{ev}", "v")
            xp = ev % 2
            if ev < 2:
                p.wait("sync", f"{pre}ln1.done")
            p.wait("sync", f"{pre}f1.x1_{ev - 2}")
            p.do("sync", lambda e, n=n, tt0=tt0, w=w, xp=xp: e.dma_start(
                out=t["xsrc"][xp][:, :w], in_=t["x_d"][n * 128:(n + 1) * 128,
                                                       tt0:tt0 + w]),
                inc=(f"x{xp}", 16))
            p.mark(f"{pre}f1.xl_{ev}", f"x{xp}")
            p.wait("vector", f"{pre}f1.xl_{ev}")
            p.wait_self("vector", "v")
            p.do("vector", lambda e, w=w, xp=xp: e.tensor_add(
                out=t["x1t"][:, :w], in0=t["t1"][:, :w], in1=t["xsrc"][xp][:, :w]),
                inc=("v", 1))
            p.mark(f"{pre}f1.x1_{ev}", "v")
            p.wait("sync", f"{pre}f1.x1_{ev}")
            p.do("sync", lambda e, n=n, tt0=tt0, w=w: e.dma_start(
                out=t["x1f"][n * 128:(n + 1) * 128, tt0:tt0 + w],
                in_=t["x1t"][:, :w]), inc=("sx1", 16))
            if wo > 0:
                p.do("sync", lambda e, n=n, lo=lo, wo=wo, tt0=tt0: e.dma_start(
                    out=t["x1_o"][n * 128:(n + 1) * 128, lo - 3:lo - 3 + wo],
                    in_=t["x1t"][:, lo - tt0:lo - tt0 + wo]), inc=("sx1", 16))
            p.mark(f"{pre}f1.std_{ev}", "sx1")
            p.mark(f"{pre}f1.stdk_{ev}", "sx1")

        emit_ffn(p, t, pre + "f1.", w,
                 lambda k, nq: t["wt_dram"][k * 128:(k + 1) * 128,
                                            nq * 512:(nq + 1) * 512],
                 evac_ffn1, f"{pre}ln1.done",
                 (f"T{ti - 1}.ipdone", f"T{ti - 1}.ssmev"),
                 tuple(f"dq_done{i}" for i in range(4)) if ti == 0 else ())
        p.mark(f"{pre}ffn1t", "tn")

        emit_ln(p, t, pre + "ln2.", w, 32,
                lambda kt, tt0=tt0, w=w: t["x1f"][kt * 128:(kt + 1) * 128,
                                                  tt0:tt0 + w],
                lambda kt, w=w: t["xnsl"][:, kt * 512:kt * 512 + w],
                f"{pre}ffn1t",
                f"{pre}ln1.var", f"{pre}ln1.bce1",
                src_guard_fn=lambda kt, pre=pre: f"{pre}f1.stdk_{kt}")

        for cq in range(32):
            for k in range(32):
                pb = k % 4
                qe = "sync" if k % 2 == 0 else "scalar"
                if p.lastw.get(pb):
                    p.wait(qe, p.lastw[pb])
                if cq == 0 and k < 4 and ti == 0:
                    p.wait(qe, "wip16")
                    p.wait(qe, "wip16b")
                p.do(qe, lambda e, cq=cq, k=k, pb=pb: e.dma_start(
                    out=t["wstr"][pb][:],
                    in_=t["wip16"][k * 128:(k + 1) * 128,
                                   cq * 512:(cq + 1) * 512]), inc=(f"sw{pb}", 16))
                p.mark(f"{pre}ip.w_{cq}_{k}", f"sw{pb}")
                p.wait("tensor", f"{pre}ip.w_{cq}_{k}")
                if k == 0:
                    if cq == 0:
                        p.wait("tensor", f"{pre}ln2.done")
                        if ti == 0:
                            p.wait("tensor", "ljt")
                    for cs in range(4):
                        p.wait("tensor", f"{pre}ip.free_{cq - 1}_{cs}")
                for cs in range(4):
                    p.do("tensor", lambda e, k=k, cs=cs, pb=pb, w=w: e.matmul(
                        t["ps"][cs][:, :w], t["wstr"][pb][:, cs * 128:(cs + 1) * 128],
                        t["xnsl"][:, k * 512:k * 512 + w],
                        start=(k == 0), stop=(k == 31)),
                        inc=("tn", 1) if cs == 3 else None)
                p.mark(f"{pre}ip.c_{cq}_{k}", "tn")
                p.lastw[pb] = f"{pre}ip.c_{cq}_{k}"
            p.mark(f"{pre}ip.g_{cq}", "tn")
            for cs in range(4):
                c = cq * 4 + cs
                if c < 64:
                    p.wait("scalar", f"{pre}ip.g_{cq}")
                    if c == 0:
                        p.wait("scalar", f"T{ti - 1}.xcd_127")
                    p.wait("scalar", f"{pre}cv.done_{c - 1}")
                    p.do("scalar", lambda e, cs=cs, w=w: e.activation(
                        out=t["xtb"][:, 3:3 + w], in_=t["ps"][cs][:, :w],
                        func=AF.Copy), inc=("a", 1))
                    p.mark(f"{pre}ip.free_{cq}_{cs}", "a")
                    if ti > 0:
                        p.wait_self("scalar", "a")
                        p.do("scalar", lambda e, c=c: e.activation(
                            out=t["xtb"][:, 0:3], in_=t["tails"][:, c * 3:c * 3 + 3],
                            func=AF.Copy), inc=("a", 1))
                    p.mark(f"{pre}xt_{c}", "a")
                    if ti == 0:
                        p.wait("vector", f"{pre}xt_{c}")
                        p.wait("vector", "hmaskld")
                        p.do("vector", lambda e: e.tensor_mul(
                            out=t["xtb"][:, 3:6], in0=t["xtb"][:, 3:6],
                            in1=t["hmask"][:, 0:3]), inc=("v", 1))
                        p.mark(f"{pre}xt_{c}", "v")
                        p.wait("scalar", f"{pre}xt_{c}")
                    b0 = lo - tt0
                    p.wait("scalar", f"{pre}cv.acc_{c - 1}")
                    p.wait_self("scalar", "a")
                    p.do("scalar", lambda e, c=c, b0=b0, wo=wo: e.activation(
                        out=t["accs"][0][:, :wo], in_=t["xtb"][:, b0:b0 + wo],
                        func=AF.Copy, scale=t["convw"][:, c * 4:c * 4 + 1]),
                        inc=("a", 1))
                    p.mark(f"{pre}cv0_{c}", "a")
                    p.wait("vector", f"{pre}cv0_{c}")
                    p.wait("vector", f"{pre}cv.vdone_{c - 1}")
                    for j in (1, 2, 3):
                        jsrc, jdst = (j + 1) % 2, j % 2
                        if j > 1:
                            p.wait_self("vector", "v")
                        p.do("vector", lambda e, c=c, j=j, b0=b0, wo=wo,
                             jsrc=jsrc, jdst=jdst: e.scalar_tensor_tensor(
                                 out=t["accs"][jdst][:, :wo],
                                 in0=t["xtb"][:, b0 + j:b0 + j + wo],
                                 scalar=t["convw"][:, c * 4 + j:c * 4 + j + 1],
                                 in1=t["accs"][jsrc][:, :wo],
                                 op0=ALU.mult, op1=ALU.add), inc=("v", 1))
                    p.mark(f"{pre}cvt_{c}", "v")
                    p.mark(f"{pre}cv.vdone_{c}", "v")
                    p.wait("scalar", f"{pre}cvt_{c}")
                    if c == 0:
                        p.wait("scalar", f"T{ti - 1}.xcd_127")
                    p.wait("scalar", f"{pre}xcd_{c - 1}")
                    p.wait("scalar", f"{pre}xpj_{c - 1}")
                    p.do("scalar", lambda e, c=c, wo=wo: e.activation(
                        out=t["xcb"][:, :wo], in_=t["accs"][1][:, :wo],
                        func=AF.Silu, bias=t["convb"][:, c:c + 1]), inc=("a", 1))
                    p.mark(f"{pre}xc_{c}", "a")
                    p.mark(f"{pre}cv.acc_{c}", "a")
                    p.wait("sync", f"{pre}xc_{c}")
                    p.do("sync", lambda e, c=c, lo=lo, wo=wo: e.dma_start(
                        out=t["xc_o"][c * 128:(c + 1) * 128, lo - 3:lo - 3 + wo],
                        in_=t["xcb"][:, :wo]), inc=("sxc", 16))
                    p.mark(f"{pre}xcdma_{c}", "sxc")
                    p.wait("tensor", f"{pre}xc_{c}")
                    for (ei, dst, pst, elo, ehi) in (
                            (0, "ssm0", "ps_e", 0, 128),
                            (1, "ssm1", "ps_e", 128, 256),
                            (2, "ssm2", "ps_x2", 256, 288)):
                        if ei == 0:
                            p.wait("tensor", f"{pre}xa1_{c - 1}")
                        elif ei == 1:
                            p.wait("tensor", f"{pre}xa0_{c}")
                        else:
                            p.wait("tensor", f"{pre}xa2_{c - 1}")
                        p.do("tensor", lambda e, c=c, wo=wo, pst=pst, elo=elo,
                             ehi=ehi: e.matmul(
                                 t[pst][:, :wo] if pst == "ps_x2" else
                                 t[pst][:, :wo],
                                 t["xpj"][:].rearrange("q (c r) -> q c r", r=288)
                                 [:, c, elo:ehi],
                                 t["xcb"][:, :wo], start=True, stop=True),
                             inc=("tn", 1))
                        p.mark(f"{pre}xm{ei}_{c}", "tn")
                        p.wait("vector", f"{pre}xm{ei}_{c}")
                        if c > 0:
                            p.wait_self("vector", "v")
                        if c == 0:
                            p.do("vector", lambda e, dst=dst, lo=lo, wo=wo,
                                 pst=pst: e.tensor_copy(
                                     out=t[dst][:, lo - 3:lo - 3 + wo],
                                     in_=t[pst][:, :wo]), inc=("v", 1))
                        else:
                            p.do("vector", lambda e, dst=dst, lo=lo, wo=wo,
                                 pst=pst: e.tensor_tensor(
                                     out=t[dst][:, lo - 3:lo - 3 + wo],
                                     in0=t[pst][:, :wo],
                                     in1=t[dst][:, lo - 3:lo - 3 + wo],
                                     op=ALU.add), inc=("v", 1))
                        p.mark(f"{pre}xa{ei}_{c}", "v")
                    p.mark(f"{pre}xpj_{c}", "v")
                    if ti < len(tts) - 1:
                        p.wait("vector", f"{pre}xt_{c}")
                        p.do("vector", lambda e, c=c, w=w: e.tensor_copy(
                            out=t["tails"][:, c * 3:c * 3 + 3],
                            in_=t["xtb"][:, w:w + 3]), inc=("v", 1))
                    p.mark(f"{pre}cv.done_{c}", "v")
                else:
                    p.wait("scalar", f"{pre}ip.g_{cq}")
                    p.wait("scalar", f"{pre}xcd_{c - 1}")
                    p.wait("scalar", f"{pre}xpj_{c - 1}")
                    p.do("scalar", lambda e, cs=cs, lo=lo, tt0=tt0, wo=wo:
                         e.activation(out=t["xcb"][:, :wo],
                                      in_=t["ps"][cs][:, lo - tt0:lo - tt0 + wo],
                                      func=AF.Silu), inc=("a", 1))
                    p.mark(f"{pre}ip.free_{cq}_{cs}", "a")
                    p.mark(f"{pre}z_{c}", "a")
                    p.wait("sync", f"{pre}z_{c}")
                    p.do("sync", lambda e, c=c, lo=lo, wo=wo: e.dma_start(
                        out=t["sz_o"][(c - 64) * 128:(c - 63) * 128,
                                      lo - 3:lo - 3 + wo],
                        in_=t["xcb"][:, :wo]), inc=("sxc", 16))
                p.mark(f"{pre}xcd_{c}", "sxc")
        p.mark(f"{pre}ipdone", "tn")
        p.mark(f"T{ti}.ssmev", "v")
    p.mark("ssm_done", "v")

    n_tt_last = len(tts) - 1
    p.wait("gpsimd", f"T{n_tt_last}.ipdone")
    p.do("gpsimd", lambda e: e.dma_start(
        out=t["dtpj"][:, 0:2 * DIN].rearrange("q (c d) -> q c d", d=DIN),
        in_=t["dtprojT_d"][:].rearrange("(c q) d -> q c d", q=128)), inc=("gj", 16))
    p.mark("dtpjld", "gj")

    n_rt = (n_real + 511) // 512
    for rt in range(n_rt):
        o = rt * 512
        w = min(512, n_real - o)
        pre = f"R{rt}."
        p.wait("scalar", "ssm_done")
        p.wait("scalar", f"R{rt - 1}.m2")
        p.do("scalar", lambda e, o=o, w=w: e.activation(
            out=t["xb"][:, :w], in_=t["ssm0"][:, o:o + w], func=AF.Square),
            inc=("a", 1))
        p.mark(f"{pre}s0", "a")
        if rt == 0:
            p.wait("tensor", "ssm_done")
            p.wait("tensor", f"T{len(tts) - 1}.ln2.var")
            p.wait("tensor", f"T{len(tts) - 1}.ln2.bce1")
        p.wait("tensor", f"{pre}s0")
        p.wait("tensor", f"R{rt - 1}.mean")
        p.do("tensor", lambda e, w=w: e.matmul(
            t["ps_row"][0:1, 0:w], t["onesb"][:], t["xb"][:, :w],
            start=True, stop=False), inc=("tn", 1))
        p.mark(f"{pre}m0", "tn")
        p.wait("scalar", f"{pre}m0")
        p.do("scalar", lambda e, o=o, w=w: e.activation(
            out=t["xb"][:, :w], in_=t["ssm1"][:, o:o + w], func=AF.Square),
            inc=("a", 1))
        p.mark(f"{pre}s1", "a")
        p.wait("tensor", f"{pre}s1")
        p.do("tensor", lambda e, w=w: e.matmul(
            t["ps_row"][0:1, 0:w], t["onesb"][:], t["xb"][:, :w],
            start=False, stop=True), inc=("tn", 1))
        p.mark(f"{pre}m1", "tn")
        p.wait("scalar", f"{pre}m1")
        p.do("scalar", lambda e, o=o, w=w: e.activation(
            out=t["xb"][0:32, :w], in_=t["ssm2"][:, o:o + w], func=AF.Square),
            inc=("a", 1))
        p.mark(f"{pre}s2", "a")
        p.wait("tensor", f"{pre}s2")
        if rt == 0:
            p.wait("tensor", "bpats")
        p.do("tensor", lambda e, w=w: e.matmul(
            t["ps_row"][0:2, 512:512 + w], t["bpat"][:], t["xb"][0:32, :w],
            start=True, stop=True), inc=("tn", 1))
        p.mark(f"{pre}m2", "tn")
        p.wait("vector", f"{pre}m1")
        p.wait("vector", f"{pre}m2")
        p.wait("vector", f"R{rt - 1}.dtrn")
        p.do("vector", lambda e, w=w: e.tensor_scalar(
            out=t["rows"][0:1, :w], in0=t["ps_row"][0:1, 0:w],
            scalar1=1.0 / 256, scalar2=None, op0=ALU.mult), inc=("v", 1))
        p.do("vector", lambda e, w=w: e.tensor_scalar(
            out=t["rows2"][0:2, :w], in0=t["ps_row"][0:2, 512:512 + w],
            scalar1=1.0 / 16, scalar2=None, op0=ALU.mult), inc=("v", 1))
        p.mark(f"{pre}mean", "v")
        p.wait("scalar", f"{pre}mean")
        p.do("scalar", lambda e, w=w: e.activation(
            out=t["rows"][0:1, :w], in_=t["rows"][0:1, :w], func=AF.Ln,
            bias=EPS_RMS), inc=("a", 1))
        p.do("scalar", lambda e, w=w: e.activation(
            out=t["rows2"][0:2, :w], in_=t["rows2"][0:2, :w], func=AF.Ln,
            bias=EPS_RMS), inc=("a", 1))
        p.wait_self("scalar", "a")
        p.do("scalar", lambda e, w=w: e.activation(
            out=t["rows"][0:1, :w], in_=t["rows"][0:1, :w], func=AF.Exp,
            scale=-0.5), inc=("a", 1))
        p.do("scalar", lambda e, w=w: e.activation(
            out=t["rows2"][0:2, :w], in_=t["rows2"][0:2, :w], func=AF.Exp,
            scale=-0.5), inc=("a", 1))
        p.mark(f"{pre}rstd", "a")
        p.wait("vector", f"{pre}rstd")
        p.do("vector", lambda e, w=w: e.tensor_copy(
            out=t["rowb"][0:1, 0:w], in_=t["rows"][0:1, :w]))
        p.do("vector", lambda e, w=w: e.tensor_copy(
            out=t["rows2b"][:, :w], in_=t["rows2"][0:2, :w]), inc=("v", 1))
        p.mark(f"{pre}rowb", "v")
        p.wait("tensor", f"{pre}rowb")
        p.wait("tensor", f"R{rt - 1}.dtrn")
        p.do("tensor", lambda e, w=w: e.matmul(
            t["ps_e"][:, 0:w], t["ones1"][:], t["rowb"][0:1, 0:w],
            start=True, stop=True), inc=("tn", 1))
        p.mark(f"{pre}bcmm", "tn")
        p.wait("scalar", f"{pre}bcmm")
        p.wait("scalar", f"R{rt - 1}.dtrn")
        p.do("scalar", lambda e, w=w: e.activation(
            out=t["bc32"][:, 0:w], in_=t["ps_e"][:, 0:w], func=AF.Copy),
            inc=("a", 1))
        p.mark(f"{pre}bcev", "a")
        for kt in range(2):
            p.wait("vector", f"{pre}bcev")
            p.do("vector", lambda e, kt=kt, o=o, w=w: e.scalar_tensor_tensor(
                out=t["dtrn"][:, kt * 512:kt * 512 + w],
                in0=t["ssm0" if kt == 0 else "ssm1"][:, o:o + w],
                scalar=t["dtln"][:, kt:kt + 1], in1=t["bc32"][:, 0:w],
                op0=ALU.mult, op1=ALU.mult), inc=("v", 1))
        p.mark(f"{pre}dtrn", "v")
        p.wait("tensor", f"{pre}rowb")
        p.wait("tensor", f"R{rt - 1}.bcn")
        p.do("tensor", lambda e, w=w: e.matmul(
            t["ps_x2"][:, :w], t["bpat2"][:], t["rows2b"][:, :w],
            start=True, stop=True), inc=("tn", 1))
        p.mark(f"{pre}bcbc", "tn")
        p.wait("vector", f"{pre}bcbc")
        p.wait("vector", f"R{rt - 1}.bcnd")
        p.do("vector", lambda e, o=o, w=w: e.scalar_tensor_tensor(
            out=t["bcb16"][:, :w], in0=t["ssm2"][:, o:o + w],
            scalar=t["bcln"][:, 0:1], in1=t["ps_x2"][:, :w],
            op0=ALU.mult, op1=ALU.mult), inc=("v", 1))
        p.mark(f"{pre}bcn", "v")
        p.wait("sync", f"{pre}bcn")
        p.do("sync", lambda e, o=o, w=w: e.dma_start(
            out=t["bc_o"][:, o:o + w], in_=t["bcb16"][:, :w]), inc=("sbc", 16))
        p.mark(f"{pre}bcnd", "sbc")
        for dc in range(64):
            pp = dc % 2
            p.wait("tensor", f"{pre}dtrn")
            p.wait("tensor", f"{pre}dtev_{dc - 2}")
            if rt == 0 and dc < 2:
                p.wait("tensor", "ssm_done")
                p.wait("tensor", "dtpjld")
            for kt in range(2):
                p.do("tensor", lambda e, dc=dc, kt=kt, pp=pp, w=w: e.matmul(
                    t["ps"][pp][:, :w],
                    t["dtpj"][:, 0:2 * DIN].rearrange("q (a d) -> q a d", d=DIN)
                    [:, kt, dc * 128:(dc + 1) * 128],
                    t["dtrn"][:, kt * 512:kt * 512 + w],
                    start=(kt == 0), stop=(kt == 1)),
                    inc=("tn", 1) if kt == 1 else None)
            p.mark(f"{pre}dtmm_{dc}", "tn")
            po = pp * 512
            p.wait("scalar", f"{pre}dtmm_{dc}")
            p.wait("scalar", f"{pre}dtvd_{dc - 2}")
            if dc < 2:
                p.wait("scalar", f"R{rt - 1}.dtvd_{62 + dc}")
            p.do("scalar", lambda e, dc=dc, pp=pp, w=w, po=po: e.activation(
                out=t["sps"][0][:, po:po + w], in_=t["ps"][pp][:, :w], func=AF.Abs,
                bias=t["dtbias"][:, dc:dc + 1]), inc=("a", 1))
            p.wait_self("scalar", "a")
            p.do("scalar", lambda e, w=w, po=po: e.activation(
                out=t["sps"][1][:, po:po + w], in_=t["sps"][0][:, po:po + w],
                func=AF.Exp, scale=-1.0), inc=("a", 1))
            p.wait_self("scalar", "a")
            p.do("scalar", lambda e, w=w, po=po: e.activation(
                out=t["sps"][0][:, po:po + w], in_=t["sps"][1][:, po:po + w],
                func=AF.Ln, bias=1.0), inc=("a", 1))
            p.do("scalar", lambda e, dc=dc, pp=pp, w=w, po=po: e.activation(
                out=t["spr"][:, po:po + w], in_=t["ps"][pp][:, :w], func=AF.Relu,
                bias=t["dtbias"][:, dc:dc + 1]), inc=("a", 1))
            p.mark(f"{pre}dtev_{dc}", "a")
            p.mark(f"{pre}dtsp_{dc}", "a")
            p.wait("vector", f"{pre}dtsp_{dc}")
            p.wait("vector", f"{pre}dtvd_{dc - 2}")
            if dc < 2:
                p.wait("vector", f"R{rt - 1}.dtvd_{62 + dc}")
            p.do("vector", lambda e, w=w, po=po: e.tensor_add(
                out=t["sps"][1][:, po:po + w], in0=t["sps"][0][:, po:po + w],
                in1=t["spr"][:, po:po + w]), inc=("v", 1))
            p.mark(f"{pre}dtv_{dc}", "v")
            p.wait("sync", f"{pre}dtv_{dc}")
            p.do("sync", lambda e, dc=dc, o=o, w=w, po=po: e.dma_start(
                out=t["dt_o"][dc * 128:(dc + 1) * 128, o:o + w],
                in_=t["sps"][1][:, po:po + w]),
                inc=("sdt" if pp == 0 else "so", 16))
            p.mark(f"{pre}dtvd_{dc}", "sdt" if pp == 0 else "so")
        p.mark(f"{pre}fin", "sdt")
        p.mark(f"{pre}fin2", "so")

    finish(nc, p, stack)
    return p


def make_A(n_real=CHUNK):
    marks = {}
    build_A(marks, emit=False, n_real=n_real)
    nc = bass.Bass("TRN2", target_bir_lowering=False)
    with ExitStack() as stack:
        build_A(marks, emit=True, n_real=n_real, nc=nc, stack=stack)
    return nc



def build_B(marks, emit, n_t=16384, batch_starts=(0, 8192), nc=None, stack=None):
    TC = 512
    n_ch = n_t // TC
    p = P(marks, emit)
    t = {}
    if emit:
        dram, sb = nc.dram_tensor, nc.sbuf_tensor
        t["dt_d"] = dram("dt", [1024, n_t], F32, kind="ExternalInput").ap()
        t["xc_d"] = dram("xc", [1024, n_t], BF16, kind="ExternalInput").ap()
        t["sz_d"] = dram("sz", [1024, n_t], BF16, kind="ExternalInput").ap()
        t["bc_d"] = dram("bc", [32, n_t], BF16, kind="ExternalInput").ap()
        t["al_d"] = dram("al", [8, 128, 16], F32, kind="ExternalInput").ap()
        t["dp_d"] = dram("dp", [128, 8], F32, kind="ExternalInput").ap()
        t["y_o"] = dram("yo", [1024, n_t], BF16, kind="ExternalOutput").ap()
        e = stack.enter_context
        t["dt"] = [e(sb(f"dtt{i}", [128, TC], F32)) for i in range(2)]
        t["xcf"] = [e(sb(f"xcf{i}", [128, TC], F32)) for i in range(2)]
        t["szf"] = [e(sb(f"szf{i}", [128, TC], F32)) for i in range(2)]
        t["xcs"] = [e(sb(f"xcs{i}", [128, TC], BF16)) for i in range(2)]
        t["szs"] = [e(sb(f"szs{i}", [128, TC], BF16)) for i in range(2)]
        t["u"] = e(sb("ubuf", [128, TC], BF16))
        t["amat"] = e(sb("amat", [128, 8 * 16], F32))
        t["dmat"] = e(sb("dmat", [128, 8], F32))
        t["Bb"] = [e(sb("Bb0", [128, 16 * TC], BF16))]
        t["Cb"] = [e(sb("Cb0", [128, 16 * TC], BF16))]
        t["av"] = [e(sb(f"av{i}", [128, 8 * TC], BF16)) for i in range(4)]
        t["bv"] = [e(sb(f"bv{i}", [128, 8 * TC], BF16)) for i in range(4)]
        t["hv"] = [e(sb(f"hv{i}", [128, 8 * TC], BF16)) for i in range(4)]
        t["hc"] = e(sb("hcbuf", [128, 8 * TC], F32))
        t["tr"] = e(sb("trbuf", [128, 4 * TC], F32))
        t["yh"] = e(sb("yhbuf", [128, 2 * TC], F32))
        t["yt"] = e(sb("ytbuf", [128, TC], F32))
        t["dxc"] = e(sb("dxcbuf", [128, TC], F32))
        t["yb"] = e(sb("ybbuf", [128, TC], BF16))
        t["hlast"] = e(sb("hlast", [128, 8 * 16], F32))
        p.sems = _mk_sems(nc, stack)

    emit_consts(p, t, nc, stack, (EPS_LN, EPS_RMS, 1.0, -0.5, -1.0, 0.5))
    p.do("sync", lambda e: e.dma_start(
        out=t["amat"][:].rearrange("q (b n) -> q b n", n=16), in_=t["al_d"][:]),
        inc=("si", 16))
    p.do("sync", lambda e: e.dma_start(out=t["dmat"][:], in_=t["dp_d"][:]),
         inc=("si", 16))
    p.mark("smalls", "si")
    p.wait("scalar", "smalls")
    p.do("scalar", lambda e: e.activation(
        out=t["amat"][:], in_=t["amat"][:], func=AF.Exp), inc=("a", 1))
    p.wait_self("scalar", "a")
    p.do("scalar", lambda e: e.activation(
        out=t["amat"][:], in_=t["amat"][:], func=AF.Copy, scale=-1.0),
        inc=("a", 1))
    p.mark("aneg", "a")

    for ch in range(n_ch):
        tc0 = ch * TC
        cb = 0
        p.wait("sync", f"bcused_{ch - 1}")
        p.wait("sync", f"bcusedg_{ch - 1}")
        for (nm, roff) in (("Bb", 0), ("Cb", 16)):
            for hh in range(2):
                p.do("sync", lambda e, nm=nm, roff=roff, hh=hh, tc0=tc0, cb=cb:
                     e.dma_start(
                         out=t[nm][cb][:].rearrange("q (n c) -> q n c", c=TC)
                         [:, hh * 8:(hh + 1) * 8, :],
                         in_=t["bc_d"][roff + hh * 8:roff + (hh + 1) * 8,
                                       tc0:tc0 + TC][None, :, :]
                         .broadcast_to([128, 8, TC])), inc=(f"gb{cb}", 16))
        p.mark(f"bcb_{ch}", f"gb{cb}")
        for blk in range(8):
            it = ch * 8 + blk
            pb = it % 2
            p.wait("sync", f"u_{it - 2}")
            p.wait("sync", f"a_{it - 2}_1")
            p.wait("sync", f"ydone_{it - 2}")
            p.do("sync", lambda e, blk=blk, tc0=tc0, pb=pb: e.dma_start(
                out=t["dt"][pb][:], in_=t["dt_d"][blk * 128:(blk + 1) * 128,
                                                  tc0:tc0 + TC]), inc=(f"b{pb}", 16))
            p.mark(f"ld_{it}", f"b{pb}")
            p.wait("scalar", f"cast_{it - 2}")
            p.do("scalar", lambda e, blk=blk, tc0=tc0, pb=pb: e.dma_start(
                out=t["xcs"][pb][:], in_=t["xc_d"][blk * 128:(blk + 1) * 128,
                                                   tc0:tc0 + TC]), inc=(f"g{pb}", 16))
            p.do("scalar", lambda e, blk=blk, tc0=tc0, pb=pb: e.dma_start(
                out=t["szs"][pb][:], in_=t["sz_d"][blk * 128:(blk + 1) * 128,
                                                   tc0:tc0 + TC]), inc=(f"g{pb}", 16))
            p.mark(f"ldg_{it}", f"g{pb}")
            p.wait("scalar", f"ldg_{it}")
            p.wait("scalar", f"u_{it - 2}")
            p.wait("scalar", f"ydone_{it - 2}")
            p.do("scalar", lambda e, pb=pb: e.activation(
                out=t["xcf"][pb][:], in_=t["xcs"][pb][:], func=AF.Copy),
                inc=("a", 1))
            p.do("scalar", lambda e, pb=pb: e.activation(
                out=t["szf"][pb][:], in_=t["szs"][pb][:], func=AF.Copy),
                inc=("a", 1))
            p.mark(f"cast_{it}", "a")
            for hf in range(2):
                ab = pb * 2 + hf
                p.wait("scalar", f"ld_{it}")
                if hf == 0:
                    p.wait("scalar", "aneg")
                p.wait("scalar", f"scans_{it - 2}_{hf}")
                for n in range(8):
                    nn = hf * 8 + n
                    p.do("scalar", lambda e, blk=blk, nn=nn, n=n, ab=ab, pb=pb:
                         e.activation(
                             out=t["av"][ab][:, n * TC:(n + 1) * TC],
                             in_=t["dt"][pb][:], func=AF.Exp,
                             scale=t["amat"][:, blk * 16 + nn:blk * 16 + nn + 1]),
                         inc=("a", 1) if n == 7 else None)
                p.mark(f"a_{it}_{hf}", "a")
            p.mark(f"a_{it}_1", "a")
            p.wait("vector", f"ld_{it}")
            p.wait("vector", f"cast_{it}")
            p.wait("vector", f"b_{it - 1}_1")
            p.do("vector", lambda e, pb=pb: e.tensor_tensor(
                out=t["u"][:], in0=t["dt"][pb][:], in1=t["xcf"][pb][:],
                op=ALU.mult), inc=("v", 1))
            p.mark(f"u_{it}", "v")
            for hf in range(2):
                ab = pb * 2 + hf
                p.wait("vector", f"bcb_{ch}")
                p.wait("vector", f"scans_{it - 2}_{hf}")
                p.wait_self("vector", "v")
                p.do("vector", lambda e, ab=ab, cb=cb, hf=hf: e.tensor_tensor(
                    out=t["bv"][ab][:].rearrange("q (n c) -> q n c", c=TC),
                    in0=t["u"][:][:, None, :].broadcast_to([128, 8, TC]),
                    in1=t["Bb"][cb][:].rearrange("q (n c) -> q n c", c=TC)
                    [:, hf * 8:(hf + 1) * 8, :], op=ALU.mult), inc=("v", 1))
                p.mark(f"b_{it}_{hf}", "v")
                p.wait("vector", f"a_{it}_{hf}")
                p.wait("vector", f"hcd_{it - 2}_{hf}")
                p.wait_self("vector", "v")
                for n in range(8):
                    nn = hf * 8 + n
                    if tc0 in batch_starts:
                        p.do("vector", lambda e, ab=ab, n=n: e.tensor_tensor_scan(
                            out=t["hv"][ab][:, n * TC:(n + 1) * TC],
                            data0=t["av"][ab][:, n * TC:(n + 1) * TC],
                            data1=t["bv"][ab][:, n * TC:(n + 1) * TC],
                            initial=0.0, op0=ALU.mult, op1=ALU.add),
                            inc=("v", 1) if n == 7 else None)
                    else:
                        p.do("vector", lambda e, ab=ab, n=n, nn=nn, blk=blk:
                             e.tensor_tensor_scan(
                                 out=t["hv"][ab][:, n * TC:(n + 1) * TC],
                                 data0=t["av"][ab][:, n * TC:(n + 1) * TC],
                                 data1=t["bv"][ab][:, n * TC:(n + 1) * TC],
                                 initial=t["hlast"][:, blk * 16 + nn:
                                                    blk * 16 + nn + 1],
                                 op0=ALU.mult, op1=ALU.add),
                             inc=("v", 1) if n == 7 else None)
                p.mark(f"scans_{it}_{hf}", "v")
                p.wait_self("vector", "v")
                p.do("vector", lambda e, ab=ab, blk=blk, hf=hf: e.tensor_copy(
                    out=t["hlast"][:, blk * 16 + hf * 8:blk * 16 + hf * 8 + 8],
                    in_=t["hv"][ab][:].rearrange("q (n c) -> q n c", c=TC)
                    [:, :, TC - 1]), inc=("v", 1))
                p.mark(f"hl_{it}_{hf}", "v")
                p.wait("gpsimd", f"scans_{it}_{hf}")
                p.wait("gpsimd", f"hl_{it}_{hf}")
                p.wait("gpsimd", f"tree_{it}_0" if hf == 1
                       else f"tree_{it - 1}_1")
                p.do("gpsimd", lambda e, ab=ab, cb=cb, hf=hf: e.tensor_tensor(
                    out=t["hc"][:].rearrange("q (n c) -> q n c", c=TC),
                    in0=t["hv"][ab][:].rearrange("q (n c) -> q n c", c=TC),
                    in1=t["Cb"][cb][:].rearrange("q (n c) -> q n c", c=TC)
                    [:, hf * 8:(hf + 1) * 8, :], op=ALU.mult), inc=("g", 1))
                p.mark(f"hcd_{it}_{hf}", "g")
                p.wait_self("gpsimd", "g")
                p.do("gpsimd", lambda e: e.tensor_add(
                    out=t["tr"][:], in0=t["hc"][:, 0:4 * TC],
                    in1=t["hc"][:, 4 * TC:8 * TC]), inc=("g", 1))
                p.mark(f"trg_{it}_{hf}", "g")
                p.wait("vector", f"trg_{it}_{hf}")
                p.do("vector", lambda e: e.tensor_add(
                    out=t["tr"][:, 0:2 * TC], in0=t["tr"][:, 0:2 * TC],
                    in1=t["tr"][:, 2 * TC:4 * TC]), inc=("v", 1))
                p.wait_self("vector", "v")
                p.do("vector", lambda e, hf=hf: e.tensor_add(
                    out=t["yh"][:, hf * TC:(hf + 1) * TC], in0=t["tr"][:, 0:TC],
                    in1=t["tr"][:, TC:2 * TC]), inc=("v", 1))
                p.mark(f"tree_{it}_{hf}", "v")
            p.wait_self("vector", "v")
            p.do("vector", lambda e: e.tensor_add(
                out=t["yt"][:], in0=t["yh"][:, 0:TC], in1=t["yh"][:, TC:2 * TC]),
                inc=("v", 1))
            p.do("vector", lambda e, pb=pb, blk=blk: e.tensor_scalar(
                out=t["dxc"][:], in0=t["xcf"][pb][:],
                scalar1=t["dmat"][:, blk:blk + 1], scalar2=None, op0=ALU.mult),
                inc=("v", 1))
            p.wait_self("vector", "v")
            p.do("vector", lambda e: e.tensor_add(
                out=t["yt"][:], in0=t["yt"][:], in1=t["dxc"][:]), inc=("v", 1))
            p.wait_self("vector", "v")
            p.wait("vector", f"ydma_{it - 1}")
            p.do("vector", lambda e, pb=pb: e.tensor_tensor(
                out=t["yb"][:], in0=t["yt"][:], in1=t["szf"][pb][:],
                op=ALU.mult), inc=("v", 1))
            p.mark(f"ydone_{it}", "v")
            p.wait("sync", f"ydone_{it}")
            p.do("sync", lambda e, blk=blk, tc0=tc0: e.dma_start(
                out=t["y_o"][blk * 128:(blk + 1) * 128, tc0:tc0 + TC],
                in_=t["yb"][:]), inc=("sy", 16))
            p.mark(f"ydma_{it}", "sy")
        p.mark(f"bcused_{ch}", "v")
        p.mark(f"bcusedg_{ch}", "g")

    finish(nc, p, stack)
    return p


def make_B(n_t=16384, batch_starts=(0, 8192)):
    marks = {}
    build_B(marks, emit=False, n_t=n_t, batch_starts=batch_starts)
    nc = bass.Bass("TRN2", target_bir_lowering=False)
    with ExitStack() as stack:
        build_B(marks, emit=True, n_t=n_t, batch_starts=batch_starts,
                nc=nc, stack=stack)
    return nc



def build_C(marks, emit, n_real=CHUNK, nc=None, stack=None):
    tts = ttiles(n_real)
    p = P(marks, emit)
    t = {}
    if emit:
        dram, sb, psa = nc.dram_tensor, nc.sbuf_tensor, nc.psum_tensor
        t["y_d"] = dram("y", [DIN, n_real], BF16, kind="ExternalInput").ap()
        t["x1_d"] = dram("x1", [H, n_real], F32, kind="ExternalInput").ap()
        t["wop_d"] = dram("wop", [DIN, H], F32, kind="ExternalInput").ap()
        t["qw_d"] = dram("qw", [H, 3072], I32, kind="ExternalInput").ap()
        t["qscale_d"] = dram("qscale", [128, 32], F32, kind="ExternalInput").ap()
        t["qzp_d"] = dram("qzp", [128, 32], F32, kind="ExternalInput").ap()
        t["fbias_d"] = dram("fbias", [128, 32], F32, kind="ExternalInput").ap()
        t["out_o"] = dram("outo", [H, n_real], F32, kind="ExternalOutput").ap()
        t["wt_dram"] = dram("wt_dram", [H, H], BF16, kind="Internal").ap()
        t["wop16"] = dram("wop16", [DIN, H], BF16, kind="Internal").ap()
        t["x2f"] = dram("x2f", [H, n_real], F32, kind="Internal").ap()
        e = stack.enter_context
        t["sqw"] = [e(sb(f"sqw{i}", [128, 3072], I32)) for i in range(2)]
        t["wvi"] = e(sb("wvi", [128, 2048], I32))
        t["wvf"] = e(sb("wvf", [128, 2048], F32))
        t["wnb"] = e(sb("wnb", [128, 4096], BF16))
        t["wtt"] = [e(sb(f"wtt{i}", [128, 128], BF16)) for i in range(4)]
        t["qscale"] = e(sb("qscales", [128, 32], F32))
        t["qzp"] = e(sb("qzps", [128, 32], F32))
        t["hb"] = e(sb("hbs", [128, 32], F32))
        t["onesb"] = e(sb("onesb", [128, 1], BF16))
        t["ones1"] = e(sb("ones1", [1, 128], BF16))
        t["xsrc"] = [e(sb(f"xsrc{i}", [128, 512], F32)) for i in range(2)]
        t["xb"] = e(sb("xbuf", [128, 512], BF16))
        t["sqb"] = e(sb("sqbuf", [128, 512], BF16))
        t["tmp32"] = e(sb("tmp32", [128, 512], F32))
        t["rows"] = e(sb("rows", [1, 1024], F32))
        t["rowb"] = e(sb("rowb", [1, 1024], BF16))
        t["bc32"] = e(sb("bc32", [128, 1024], F32))
        t["wstr"] = [e(sb(f"wstr{i}", [128, 512], BF16)) for i in range(4)]
        t["wcf32"] = [e(sb(f"wcf32{i}", [128, 1024], F32)) for i in range(2)]
        t["wcb16"] = [e(sb(f"wcb16{i}", [128, 1024], BF16)) for i in range(2)]
        t["xnsl"] = e(sb("xnsl", [128, 32 * 512], BF16))
        t["ysl"] = e(sb("ysl", [128, 64 * 512], BF16))
        t["t1"] = e(sb("t1buf", [128, 512], F32))
        t["x1t"] = e(sb("x1t", [128, 512], F32))
        t["ps"] = [e(psa(f"psum{i}", [128, 512], F32)) for i in range(5)]
        t["ps_row"] = e(psa("ps_row", [2, 1024], F32))
        t["ps_e"] = t["ps"][4]
        p.sems = _mk_sems(nc, stack)

    emit_consts(p, t, nc, stack, (EPS_LN, EPS_RMS, 1.0, -0.5, -1.0, 0.5))
    for (name, dst) in (("qscale_d", "qscale"), ("qzp_d", "qzp"),
                        ("fbias_d", "hb")):
        p.do("sync", lambda e, name=name, dst=dst: e.dma_start(
            out=t[dst][:], in_=t[name][:]), inc=("si", 16))
    p.mark("smalls", "si")
    p.do("gpsimd", lambda e: e.memset(t["onesb"][:], 1.0))
    p.do("gpsimd", lambda e: e.memset(t["ones1"][:], 1.0), inc=("g", 1))
    p.mark("ones", "g")
    p.wait("vector", "smalls")
    p.do("vector", lambda e: e.tensor_scalar(
        out=t["hb"][:], in0=t["hb"][:], scalar1=0.5, scalar2=None,
        op0=ALU.mult), inc=("v", 1))
    p.mark("hb", "v")
    emit_dequant(p, t)
    for ci in range(256):
        kt, ct = ci // 4, ci % 4
        pb = ci % 2
        sname = "gw" if pb == 0 else "gw2"
        p.wait("sync", f"wipc_{ci - 2}")
        p.do("sync", lambda e, kt=kt, ct=ct, pb=pb: e.dma_start(
            out=t["wcf32"][pb][:], in_=t["wop_d"][kt * 128:(kt + 1) * 128,
                                                  ct * 1024:(ct + 1) * 1024]),
            inc=(sname, 16))
        p.mark(f"wipl_{ci}", sname)
        p.wait("vector", f"wipl_{ci}")
        p.wait("vector", f"wipst_{ci - 2}")
        p.do("vector", lambda e, pb=pb: e.tensor_copy(
            out=t["wcb16"][pb][:], in_=t["wcf32"][pb][:]), inc=("v", 1))
        p.mark(f"wipc_{ci}", "v")
        p.wait("scalar", f"wipc_{ci}")
        p.do("scalar", lambda e, kt=kt, ct=ct, pb=pb: e.dma_start(
            out=t["wop16"][kt * 128:(kt + 1) * 128, ct * 1024:(ct + 1) * 1024],
            in_=t["wcb16"][pb][:]), inc=(f"gb{pb}", 16))
        p.mark(f"wipst_{ci}", f"gb{pb}")
    p.mark("wop16", "gb0")
    p.mark("wop16b", "gb1")

    p.wait("tensor", "ones")
    p.wait("vector", "ones")
    for ti, (tt0, w) in enumerate(tts):
        pre = f"T{ti}."
        p.wait("sync", f"T{ti - 1}.opdone")
        for kt in range(64):
            p.do("sync", lambda e, kt=kt, tt0=tt0, w=w: e.dma_start(
                out=t["ysl"][:, kt * 512:kt * 512 + w],
                in_=t["y_d"][kt * 128:(kt + 1) * 128, tt0:tt0 + w]),
                inc=("sy", 16))
        p.mark(f"{pre}yld", "sy")
        for hq in range(8):
            for k in range(64):
                pb = k % 4
                qe = "sync" if k % 2 == 0 else "scalar"
                if p.lastw.get(pb):
                    p.wait(qe, p.lastw[pb])
                if ti == 0 and hq == 0 and k < 4:
                    p.wait(qe, "wop16")
                    p.wait(qe, "wop16b")
                p.do(qe, lambda e, hq=hq, k=k, pb=pb: e.dma_start(
                    out=t["wstr"][pb][:], in_=t["wop16"][k * 128:(k + 1) * 128,
                                                         hq * 512:(hq + 1) * 512]),
                    inc=(f"sw{pb}", 16))
                p.mark(f"{pre}op.w_{hq}_{k}", f"sw{pb}")
                p.wait("tensor", f"{pre}op.w_{hq}_{k}")
                if k == 0:
                    p.wait("tensor", f"{pre}yld")
                    if ti == 0 and hq == 0:
                        p.wait("tensor", "ones")
                    for ns in range(4):
                        p.wait("tensor", f"{pre}op.free_{hq - 1}_{ns}")
                for ns in range(4):
                    p.do("tensor", lambda e, k=k, ns=ns, pb=pb, w=w: e.matmul(
                        t["ps"][ns][:, :w], t["wstr"][pb][:, ns * 128:(ns + 1) * 128],
                        t["ysl"][:, k * 512:k * 512 + w],
                        start=(k == 0), stop=(k == 63)),
                        inc=("tn", 1) if ns == 3 else None)
                p.mark(f"{pre}op.c_{hq}_{k}", "tn")
                p.lastw[pb] = f"{pre}op.c_{hq}_{k}"
            p.mark(f"{pre}op.g_{hq}", "tn")
            for ns in range(4):
                ev = hq * 4 + ns
                p.wait("sync", f"{pre}op.x2_{ev - 1}")
                p.do("sync", lambda e, ev=ev, tt0=tt0, w=w: e.dma_start(
                    out=t["xsrc"][0][:, :w], in_=t["x1_d"][ev * 128:(ev + 1) * 128,
                                                           tt0:tt0 + w]),
                    inc=("x0", 16))
                p.mark(f"{pre}op.xl_{ev}", "x0")
                p.wait("vector", f"{pre}op.g_{hq}")
                p.wait("vector", f"{pre}op.xl_{ev}")
                p.wait("vector", f"{pre}op.x2d_{ev - 1}")
                p.do("vector", lambda e, ns=ns, w=w: e.tensor_add(
                    out=t["x1t"][:, :w], in0=t["ps"][ns][:, :w],
                    in1=t["xsrc"][0][:, :w]), inc=("v", 1))
                p.mark(f"{pre}op.free_{hq}_{ns}", "v")
                p.mark(f"{pre}op.x2_{ev}", "v")
                p.wait("sync", f"{pre}op.x2_{ev}")
                p.do("sync", lambda e, ev=ev, tt0=tt0, w=w: e.dma_start(
                    out=t["x2f"][ev * 128:(ev + 1) * 128, tt0:tt0 + w],
                    in_=t["x1t"][:, :w]), inc=("sx1", 16))
                p.mark(f"{pre}op.x2d_{ev}", "sx1")
                p.mark(f"{pre}op.x2dk_{ev}", "sx1")
        p.mark(f"{pre}opdone", "tn")

        emit_ln(p, t, pre + "ln3.", w, 32,
                lambda kt, tt0=tt0, w=w: t["x2f"][kt * 128:(kt + 1) * 128,
                                                  tt0:tt0 + w],
                lambda kt, w=w: t["xnsl"][:, kt * 512:kt * 512 + w],
                f"T{ti - 1}.f2.mmdone",
                f"T{ti - 1}.ln3.var", f"T{ti - 1}.ln3.bce1",
                src_guard_fn=lambda kt, pre=pre: f"{pre}op.x2dk_{kt}")

        def evac_ffn2(p, nq, ns, ev, pre=pre, tt0=tt0, w=w):
            n = ev
            p.wait("vector", f"{pre}f2.g_{nq}")
            p.wait("vector", f"{pre}f2.od_{ev - 1}")
            p.do("vector", lambda e, ns=ns, n=n, w=w: e.tensor_scalar(
                out=t["t1"][:, :w], in0=t["ps"][ns][:, :w],
                scalar1=0.5, scalar2=t["hb"][:, n:n + 1],
                op0=ALU.mult, op1=ALU.add), inc=("v", 1))
            p.mark(f"{pre}f2.free_{nq}_{ns}", "v")
            if ev == 0:
                p.wait("sync", f"{pre}ln3.done")
            p.wait("sync", f"{pre}f2.o_{ev - 1}")
            p.do("sync", lambda e, n=n, tt0=tt0, w=w: e.dma_start(
                out=t["xsrc"][0][:, :w], in_=t["x2f"][n * 128:(n + 1) * 128,
                                                      tt0:tt0 + w]),
                inc=("x0", 16))
            p.mark(f"{pre}f2.xl_{ev}", "x0")
            p.wait("vector", f"{pre}f2.xl_{ev}")
            p.wait_self("vector", "v")
            p.do("vector", lambda e, w=w: e.tensor_add(
                out=t["x1t"][:, :w], in0=t["t1"][:, :w], in1=t["xsrc"][0][:, :w]),
                inc=("v", 1))
            p.mark(f"{pre}f2.o_{ev}", "v")
            p.wait("sync", f"{pre}f2.o_{ev}")
            p.do("sync", lambda e, n=n, tt0=tt0, w=w: e.dma_start(
                out=t["out_o"][n * 128:(n + 1) * 128, tt0:tt0 + w],
                in_=t["x1t"][:, :w]), inc=("so", 16))
            p.mark(f"{pre}f2.od_{ev}", "so")

        if ti == 0:
            for wb in range(4):
                p.wait("tensor", f"dq_done{wb}")
            p.wait("tensor", "hb")
        emit_ffn(p, t, pre + "f2.", w,
                 lambda k, nq: t["wt_dram"][k * 128:(k + 1) * 128,
                                            nq * 512:(nq + 1) * 512],
                 evac_ffn2, f"{pre}ln3.done", (),
                 tuple(f"dq_done{i}" for i in range(4)) if ti == 0 else ())

    finish(nc, p, stack)
    return p


def make_C(n_real=CHUNK):
    marks = {}
    build_C(marks, emit=False, n_real=n_real)
    nc = bass.Bass("TRN2", target_bir_lowering=False)
    with ExitStack() as stack:
        build_C(marks, emit=True, n_real=n_real, nc=nc, stack=stack)
    return nc



def _col32(v):
    return np.ascontiguousarray(v.reshape(-1, 128).T)


def prep_host(inp):
    g = lambda k: np.asarray(inp[k])
    d = {}
    x0 = g("L0_attn_hidden_states")
    d["xt"] = np.ascontiguousarray(x0.reshape(2 * 8192, H).T)
    d["wip"] = np.ascontiguousarray(g("L0_attn_in_proj_w"))
    d["qw"] = np.ascontiguousarray(g("L0_ffn_qweight"))
    d["qscale"] = _col32(g("L0_ffn_weight_scale").reshape(-1))
    d["qzp"] = _col32(g("L0_ffn_weight_zero_point").reshape(-1))
    d["fbias"] = _col32(g("L0_ffn_bias"))
    d["xprojT"] = np.ascontiguousarray(g("L0_attn_x_proj_w").T)
    d["dtprojT"] = np.ascontiguousarray(g("L0_attn_dt_proj_w").T)
    d["dtbias"] = np.ascontiguousarray(g("L0_attn_dt_bias").reshape(64, 128).T)
    d["dtlnw"] = np.ascontiguousarray(g("L0_attn_dt_ln_w").reshape(2, 128).T)
    d["bclnw"] = np.concatenate([g("L0_attn_B_ln_w"),
                                 g("L0_attn_C_ln_w")]).reshape(32, 1)
    d["convw"] = np.ascontiguousarray(
        g("L0_attn_conv_w").reshape(64, 128, 4).transpose(1, 0, 2).reshape(128, 256))
    d["convb"] = np.ascontiguousarray(g("L0_attn_conv_b").reshape(64, 128).T)
    d["wopT"] = np.ascontiguousarray(g("L0_attn_out_proj_w").T)
    d["al"] = g("L0_attn_A_log")
    d["dp"] = g("L0_attn_D_param")
    return d


def a_inmap(d, c):
    lo = c * CHUNK
    if c % 4 == 0:
        xc = np.concatenate([np.zeros((H, 3), np.float32),
                             d["xt"][:, lo:lo + CHUNK]], axis=1)
    else:
        xc = np.ascontiguousarray(d["xt"][:, lo - 3:lo + CHUNK])
    bpat = np.zeros((32, 2), np.float32)
    bpat[0:16, 0] = 1.0
    bpat[16:32, 1] = 1.0
    hmask = np.ones((128, 4), np.float32)
    if c % 4 == 0:
        hmask[:, 0:3] = 0.0
    return dict(x=xc, hmask=hmask, wip=d["wip"], qw=d["qw"], qscale=d["qscale"],
                qzp=d["qzp"], fbias=d["fbias"], xprojT=d["xprojT"],
                dtprojT=d["dtprojT"], dtbias=d["dtbias"], dtlnw=d["dtlnw"],
                bclnw=d["bclnw"], convw=d["convw"], convb=d["convb"],
                bpat_i=bpat, bpat2_i=np.ascontiguousarray(bpat.T))


def b_inmap(d, ab, j):
    sl = slice(j * 1024, (j + 1) * 1024)
    return dict(dt=np.ascontiguousarray(ab["dt"][sl]),
                xc=np.ascontiguousarray(ab["xc"][sl]),
                sz=np.ascontiguousarray(ab["sz"][sl]),
                bc=ab["bc"],
                al=np.ascontiguousarray(d["al"][sl].reshape(8, 128, 16)),
                dp=np.ascontiguousarray(d["dp"][sl].reshape(8, 128).T))


def c_inmap(d, ab, y_all, c):
    cs = slice(c * CHUNK, (c + 1) * CHUNK)
    return dict(y=np.ascontiguousarray(y_all[:, cs]),
                x1=np.ascontiguousarray(ab["x1"][:, cs]),
                wop=d["wopT"], qw=d["qw"], qscale=d["qscale"],
                qzp=d["qzp"], fbias=d["fbias"])


def kernel(**inp):
    d = prep_host(inp)
    ncA = make_A()
    resA = run_bass_kernel_spmd(ncA, [a_inmap(d, c) for c in range(NCORE)],
                                core_ids=list(range(NCORE)))
    ab = dict(
        x1=np.concatenate([r["x1o"] for r in resA.results], axis=1),
        dt=np.concatenate([r["dto"] for r in resA.results], axis=1),
        xc=np.concatenate([r["xco"] for r in resA.results], axis=1),
        sz=np.concatenate([r["szo"] for r in resA.results], axis=1),
        bc=np.concatenate([r["bco"] for r in resA.results], axis=1),
    )
    ncB = make_B()
    resB = run_bass_kernel_spmd(ncB, [b_inmap(d, ab, j) for j in range(NCORE)],
                                core_ids=list(range(NCORE)))
    y_all = np.concatenate([r["yo"] for r in resB.results], axis=0)
    ncC = make_C()
    resC = run_bass_kernel_spmd(ncC, [c_inmap(d, ab, y_all, c)
                                      for c in range(NCORE)],
                                core_ids=list(range(NCORE)))
    out = np.concatenate([r["outo"] for r in resC.results], axis=1)
    return np.ascontiguousarray(out.T).reshape(2, 8192, H).astype(np.float32)


# version 05
# speedup vs baseline: 1.1088x; 1.1088x over previous
import numpy as np
import ml_dtypes
import sys
if '/opt/trn_rl_repo' not in sys.path:
    sys.path.insert(0, '/opt/trn_rl_repo')
from contextlib import ExitStack
import concourse.bass as bass
import concourse.mybir as mybir
from concourse.bass_utils import run_bass_kernel_spmd

F32 = mybir.dt.float32
BF16 = mybir.dt.bfloat16
I32 = mybir.dt.int32
AF = mybir.ActivationFunctionType
ALU = mybir.AluOpType
BF = ml_dtypes.bfloat16

H = 4096
DIN = 8192
NCORE = 8
CHUNK = 2048
EPS_LN = 1e-5
EPS_RMS = 1e-6

ENGS = ("sync", "gpsimd", "tensor", "vector", "scalar")
SEMS = ("si", "gj", "gw", "gw2", "q0", "q1",
        "t0", "t1", "t2", "t3", "u0", "u1", "u2", "u3",
        "x0", "x1", "sw0", "sw1", "sw2", "sw3", "sx1", "sxc", "sdt",
        "sbc", "sy", "so", "b0", "b1", "g0", "g1", "gb0", "gb1",
        "g", "tn", "v", "a")


class P:
    def __init__(self, marks, emit):
        self.marks = marks
        self.emit = emit
        self.sems = None
        self.cnt = {s: 0 for s in SEMS}
        self.streams = {e: [] for e in ENGS}
        self.lastw = {}

    def do(self, eng, fn, inc=None):
        incs = [inc] if isinstance(inc, tuple) else (inc or [])
        for (s, k) in incs:
            self.cnt[s] += k
        if self.emit:
            def run(e, fn=fn, incs=tuple(incs)):
                ins = fn(e)
                for (s, k) in incs:
                    ins.then_inc(self.sems[s], k)
            self.streams[eng].append(run)

    def mark(self, tag, sem):
        self.marks[tag] = (sem, self.cnt[sem])

    def wait(self, eng, tag):
        if self.emit and tag in self.marks:
            sem, val = self.marks[tag]
            if val > 0:
                self.streams[eng].append(
                    lambda e, s=sem, v=val: e.wait_ge(self.sems[s], v))

    def wait_val(self, eng, sem, val):
        if self.emit and val > 0:
            self.streams[eng].append(
                lambda e, s=sem, v=val: e.wait_ge(self.sems[s], v))

    def wait_self(self, eng, sem):
        self.wait_val(eng, sem, self.cnt[sem])

    def val(self, sem):
        return self.cnt[sem]


def ttiles(n):
    out, o = [], 0
    while o < n:
        w = min(512, n - o)
        out.append((o, w))
        o += w
    return out


def _mk_sems(nc, stack):
    return {n: stack.enter_context(nc.semaphore(n)) for n in SEMS}


def finish(nc, p, stack):
    for s in SEMS:
        if s not in ("g", "tn", "v", "a"):
            p.wait_val("gpsimd", s, p.val(s))
    if p.emit:
        block = stack.enter_context(nc.Block())
        for eng in ENGS:
            ops = p.streams[eng]
            if ops:
                getattr(block, eng)(lambda e, ops=tuple(ops): [f(e) for f in ops])



def emit_consts(p, t, nc, stack, values):
    if p.emit:
        cst = stack.enter_context(nc.sbuf_tensor("cst", [128, len(values)], F32))
        t["cst"] = cst
        for i, v in enumerate(values):
            nc.const_aps.register(F32, v, cst[:, i:i + 1])
    for i, v in enumerate(values):
        p.do("gpsimd", lambda e, i=i, v=v: e.memset(t["cst"][:, i:i + 1], v),
             inc=("g", 1) if i == len(values) - 1 else None)
    p.mark("csts", "g")
    p.wait("scalar", "csts")
    p.wait("vector", "csts")



def emit_dequant(p, t):
    for nt in range(32):
        pb = nt % 2
        p.wait("sync", f"dq_unp_{nt - 2}")
        p.do("sync", lambda e, nt=nt, pb=pb: e.dma_start(
            out=t["sqw"][pb][:], in_=t["qw_d"][nt * 128:(nt + 1) * 128, :]),
            inc=(f"q{pb}", 16))
        p.mark(f"dq_ld_{nt}", f"q{pb}")
        p.wait("vector", f"dq_ld_{nt}")
        for wb in range(4):
            p.wait("vector", f"dq_u{wb}_{nt - 1}")
        for hh in range(2):
            for (j, sj, op, sc) in ((0, 0, ALU.bitwise_and, 63),
                                    (1, 0, ALU.logical_shift_right, 6),
                                    (2, 1, ALU.bitwise_and, 63),
                                    (3, 2, ALU.bitwise_and, 63)):
                p.do("vector", lambda e, pb=pb, j=j, sj=sj, op=op, sc=sc, hh=hh:
                     e.tensor_single_scalar(
                         out=t["wvi"][:].rearrange("q (g f) -> q g f", f=4)[:, :, j],
                         in_=t["sqw"][pb][:, hh * 1536:(hh + 1) * 1536]
                         .rearrange("q (g f) -> q g f", f=3)[:, :, sj],
                         scalar=sc, op=op), inc=("v", 1))
            p.wait_self("vector", "v")
            p.do("vector", lambda e: e.tensor_copy(out=t["wvf"][:], in_=t["wvi"][:]),
                 inc=("v", 1))
            p.wait_self("vector", "v")
            p.do("vector", lambda e, nt=nt, hh=hh: e.tensor_scalar(
                out=t["wnb"][:, hh * 2048:(hh + 1) * 2048], in0=t["wvf"][:],
                scalar1=t["qzp"][:, nt:nt + 1], scalar2=t["qscale"][:, nt:nt + 1],
                op0=ALU.subtract, op1=ALU.mult), inc=("v", 1))
        p.mark(f"dq_unp_{nt}", "v")
        p.mark(f"dq_w_{nt}", "v")
        p.wait("sync", f"dq_w_{nt}")
        for kb in range(32):
            si = nt * 32 + kb
            wb = si % 4
            p.wait_val("sync", f"u{wb}", p.val(f"u{wb}"))
            p.do("sync", lambda e, kb=kb, wb=wb: e.dma_start_transpose(
                out=t["wtt"][wb][:], in_=t["wnb"][:, kb * 128:(kb + 1) * 128]),
                inc=(f"t{wb}", 16))
            p.wait_val("scalar", f"t{wb}", p.val(f"t{wb}"))
            p.do("scalar", lambda e, kb=kb, nt=nt, wb=wb: e.dma_start(
                out=t["wt_dram"][kb * 128:(kb + 1) * 128, nt * 128:(nt + 1) * 128],
                in_=t["wtt"][wb][:]), inc=(f"u{wb}", 16))
        for wb in range(4):
            p.mark(f"dq_u{wb}_{nt}", f"u{wb}")
    for wb in range(4):
        p.mark(f"dq_done{wb}", f"u{wb}")



def emit_ln(p, t, pre, W, n_kt, src_fn, out_fn, norm_guard, row_guard,
            e_guard, src_guard_fn=None):
    for kt in range(n_kt):
        pb = kt % 2
        p.wait("sync", f"{pre}mm_{kt - 2}")
        if src_guard_fn is not None:
            p.wait("sync", src_guard_fn(kt))
        p.do("sync", lambda e, kt=kt, pb=pb, W=W: e.dma_start(
            out=t["xsrc"][pb][:, :W], in_=src_fn(kt)), inc=(f"x{pb}", 16))
        p.mark(f"{pre}ld_{kt}", f"x{pb}")
        p.wait("scalar", f"{pre}ld_{kt}")
        p.wait("scalar", f"{pre}mm_{kt - 1}")
        p.do("scalar", lambda e, pb=pb, W=W: e.activation(
            out=t["xb"][:, :W], in_=t["xsrc"][pb][:, :W], func=AF.Copy),
            inc=("a", 1))
        p.mark(f"{pre}xb_{kt}", "a")
        p.wait("vector", f"{pre}xb_{kt}")
        p.wait("vector", f"{pre}mm_{kt - 1}")
        p.do("vector", lambda e, W=W: e.tensor_mul(
            out=t["sqb"][:, :W], in0=t["xb"][:, :W], in1=t["xb"][:, :W]),
            inc=("v", 1))
        p.mark(f"{pre}sq_{kt}", "v")
        if kt == 0:
            p.wait("tensor", row_guard)
        p.wait("tensor", f"{pre}xb_{kt}")
        p.wait("tensor", f"{pre}sq_{kt}")
        p.do("tensor", lambda e, kt=kt, W=W: e.matmul(
            t["ps_row"][0:1, 0:W], t["onesb"][:], t["xb"][:, :W],
            start=(kt == 0), stop=(kt == n_kt - 1)))
        p.do("tensor", lambda e, kt=kt, W=W: e.matmul(
            t["ps_row"][0:1, 512:512 + W], t["onesb"][:], t["sqb"][:, :W],
            start=(kt == 0), stop=(kt == n_kt - 1)), inc=("tn", 1))
        p.mark(f"{pre}mm_{kt}", "tn")
    p.mark(f"{pre}stats", "tn")
    p.wait("vector", f"{pre}stats")
    p.do("vector", lambda e, W=W: e.tensor_scalar(
        out=t["rows"][0:1, :W], in0=t["ps_row"][0:1, 0:W],
        scalar1=1.0 / (128 * n_kt), scalar2=None, op0=ALU.mult), inc=("v", 1))
    p.wait_self("vector", "v")
    p.do("vector", lambda e, W=W: e.tensor_mul(
        out=t["rows"][0:1, 512:512 + W], in0=t["rows"][0:1, :W],
        in1=t["rows"][0:1, :W]), inc=("v", 1))
    p.wait_self("vector", "v")
    p.do("vector", lambda e, W=W: e.scalar_tensor_tensor(
        out=t["rows"][0:1, 512:512 + W], in0=t["ps_row"][0:1, 512:512 + W],
        scalar=1.0 / (128 * n_kt), in1=t["rows"][0:1, 512:512 + W],
        op0=ALU.mult, op1=ALU.subtract), inc=("v", 1))
    p.mark(f"{pre}var", "v")
    p.wait("scalar", f"{pre}var")
    p.do("scalar", lambda e, W=W: e.activation(
        out=t["rows"][0:1, 512:512 + W], in_=t["rows"][0:1, 512:512 + W],
        func=AF.Ln, bias=EPS_LN), inc=("a", 1))
    p.wait_self("scalar", "a")
    p.do("scalar", lambda e, W=W: e.activation(
        out=t["rows"][0:1, 512:512 + W], in_=t["rows"][0:1, 512:512 + W],
        func=AF.Exp, scale=-0.5), inc=("a", 1))
    p.mark(f"{pre}rstd", "a")
    p.wait("vector", f"{pre}rstd")
    p.do("vector", lambda e, W=W: e.tensor_mul(
        out=t["rows"][0:1, :W], in0=t["rows"][0:1, :W],
        in1=t["rows"][0:1, 512:512 + W]), inc=("v", 1))
    p.wait_self("vector", "v")
    p.do("vector", lambda e, W=W: e.tensor_copy(
        out=t["rowb"][0:1, 0:W], in_=t["rows"][0:1, 512:512 + W]))
    p.do("vector", lambda e, W=W: e.tensor_copy(
        out=t["rowb"][0:1, 512:512 + W], in_=t["rows"][0:1, :W]), inc=("v", 1))
    p.mark(f"{pre}rowb", "v")
    p.wait("tensor", f"{pre}rowb")
    p.wait("tensor", e_guard)
    p.do("tensor", lambda e, W=W: e.matmul(
        t["ps_e"][:, 0:W], t["ones1"][:], t["rowb"][0:1, 0:W],
        start=True, stop=True), inc=("tn", 1))
    p.mark(f"{pre}bc0", "tn")
    p.wait("scalar", f"{pre}bc0")
    p.do("scalar", lambda e, W=W: e.activation(
        out=t["bc32"][:, 0:W], in_=t["ps_e"][:, 0:W], func=AF.Copy), inc=("a", 1))
    p.mark(f"{pre}bce0", "a")
    p.wait("tensor", f"{pre}bce0")
    p.do("tensor", lambda e, W=W: e.matmul(
        t["ps_e"][:, 0:W], t["ones1"][:], t["rowb"][0:1, 512:512 + W],
        start=True, stop=True), inc=("tn", 1))
    p.mark(f"{pre}bc1", "tn")
    p.wait("scalar", f"{pre}bc1")
    p.do("scalar", lambda e, W=W: e.activation(
        out=t["bc32"][:, 512:512 + W], in_=t["ps_e"][:, 0:W], func=AF.Copy),
        inc=("a", 1))
    p.mark(f"{pre}bce1", "a")
    for kt in range(n_kt):
        pb = kt % 2
        p.wait("sync", f"{pre}n_{kt - 2}")
        if kt < 2:
            p.wait("sync", f"{pre}xb_{n_kt - 1}")
        if src_guard_fn is not None:
            p.wait("sync", src_guard_fn(kt))
        p.do("sync", lambda e, kt=kt, pb=pb, W=W: e.dma_start(
            out=t["xsrc"][pb][:, :W], in_=src_fn(kt)), inc=(f"x{pb}", 16))
        p.mark(f"{pre}l2_{kt}", f"x{pb}")
        p.wait("vector", f"{pre}l2_{kt}")
        p.wait("vector", f"{pre}bce1")
        if kt == 0 and norm_guard:
            p.wait("vector", norm_guard)
        p.wait("vector", f"{pre}n_{kt - 1}")
        p.do("vector", lambda e, pb=pb, W=W: e.tensor_mul(
            out=t["tmp32"][:, :W], in0=t["xsrc"][pb][:, :W], in1=t["bc32"][:, 0:W]),
            inc=("v", 1))
        p.wait_self("vector", "v")
        p.do("vector", lambda e, kt=kt, W=W: e.tensor_sub(
            out=out_fn(kt), in0=t["tmp32"][:, :W], in1=t["bc32"][:, 512:512 + W]),
            inc=("v", 1))
        p.mark(f"{pre}n_{kt}", "v")
    p.mark(f"{pre}done", "v")



def emit_ffn(p, t, pre, W, wdram, evac_fn, rhs_guard, war_guards=(),
             wt_guards=()):
    for nq in range(8):
        for k in range(32):
            pb = k % 4
            qe = "sync" if k % 2 == 0 else "scalar"
            if p.lastw.get(pb):
                p.wait(qe, p.lastw[pb])
            if nq == 0 and k < 4:
                for wg in wt_guards:
                    p.wait(qe, wg)
            p.do(qe, lambda e, nq=nq, k=k, pb=pb: e.dma_start(
                out=t["wstr"][pb][:], in_=wdram(k, nq)), inc=(f"sw{pb}", 16))
            p.mark(f"{pre}w_{nq}_{k}", f"sw{pb}")
            p.wait("tensor", f"{pre}w_{nq}_{k}")
            if k == 0:
                p.wait("tensor", rhs_guard)
                if nq == 0:
                    for wg in war_guards:
                        p.wait("tensor", wg)
                for ns in range(4):
                    p.wait("tensor", f"{pre}free_{nq - 1}_{ns}")
            for ns in range(4):
                p.do("tensor", lambda e, k=k, ns=ns, pb=pb, W=W: e.matmul(
                    t["ps"][ns][:, :W], t["wstr"][pb][:, ns * 128:(ns + 1) * 128],
                    t["xnsl"][:, k * 512:k * 512 + W],
                    start=(k == 0), stop=(k == 31)),
                    inc=("tn", 1) if ns == 3 else None)
            p.mark(f"{pre}c_{nq}_{k}", "tn")
            p.lastw[pb] = f"{pre}c_{nq}_{k}"
        p.mark(f"{pre}g_{nq}", "tn")
        for ns in range(4):
            evac_fn(p, nq, ns, nq * 4 + ns)
    p.mark(f"{pre}mmdone", "tn")



def build_A(marks, emit, n_real=CHUNK, nc=None, stack=None):
    n_tok = n_real + 3
    tts = ttiles(n_tok)
    p = P(marks, emit)
    t = {}
    if emit:
        dram, sb, psa = nc.dram_tensor, nc.sbuf_tensor, nc.psum_tensor
        t["x_d"] = dram("x", [H, n_tok], F32, kind="ExternalInput").ap()
        t["wip_d"] = dram("wip", [H, 2 * DIN], F32, kind="ExternalInput").ap()
        t["qw_d"] = dram("qw", [H, 3072], I32, kind="ExternalInput").ap()
        t["qscale_d"] = dram("qscale", [128, 32], F32, kind="ExternalInput").ap()
        t["qzp_d"] = dram("qzp", [128, 32], F32, kind="ExternalInput").ap()
        t["fbias_d"] = dram("fbias", [128, 32], F32, kind="ExternalInput").ap()
        t["xprojT_d"] = dram("xprojT", [DIN, 288], F32, kind="ExternalInput").ap()
        t["dtprojT_d"] = dram("dtprojT", [256, DIN], F32, kind="ExternalInput").ap()
        t["dtbias_d"] = dram("dtbias", [128, 64], F32, kind="ExternalInput").ap()
        t["dtlnw_d"] = dram("dtlnw", [128, 2], F32, kind="ExternalInput").ap()
        t["bclnw_d"] = dram("bclnw", [32, 1], F32, kind="ExternalInput").ap()
        t["convw_d"] = dram("convw", [128, 256], F32, kind="ExternalInput").ap()
        t["convb_d"] = dram("convb", [128, 64], F32, kind="ExternalInput").ap()
        t["bpat_d"] = dram("bpat_i", [32, 2], F32, kind="ExternalInput").ap()
        t["hmask_d"] = dram("hmask", [128, 4], F32, kind="ExternalInput").ap()
        t["bpat2_d"] = dram("bpat2_i", [2, 32], F32, kind="ExternalInput").ap()
        t["x1_o"] = dram("x1o", [H, n_real], F32, kind="ExternalOutput").ap()
        t["dt_o"] = dram("dto", [DIN, n_real], F32, kind="ExternalOutput").ap()
        t["xc_o"] = dram("xco", [DIN, n_real], BF16, kind="ExternalOutput").ap()
        t["sz_o"] = dram("szo", [DIN, n_real], BF16, kind="ExternalOutput").ap()
        t["bc_o"] = dram("bco", [32, n_real], BF16, kind="ExternalOutput").ap()
        t["wt_dram"] = dram("wt_dram", [H, H], BF16, kind="Internal").ap()
        t["wip16"] = dram("wip16", [H, 2 * DIN], BF16, kind="Internal").ap()
        t["x1f"] = dram("x1f", [H, n_tok], F32, kind="Internal").ap()
        e = stack.enter_context
        t["sqw"] = [e(sb(f"sqw{i}", [128, 3072], I32)) for i in range(2)]
        t["wvi"] = e(sb("wvi", [128, 2048], I32))
        t["wvf"] = e(sb("wvf", [128, 2048], F32))
        t["wnb"] = e(sb("wnb", [128, 4096], BF16))
        t["wtt"] = [e(sb(f"wtt{i}", [128, 128], BF16)) for i in range(4)]
        t["qscale"] = e(sb("qscales", [128, 32], F32))
        t["qzp"] = e(sb("qzps", [128, 32], F32))
        t["hb"] = e(sb("hbs", [128, 32], F32))
        t["onesb"] = e(sb("onesb", [128, 1], BF16))
        t["ones1"] = e(sb("ones1", [1, 128], BF16))
        t["bpat"] = e(sb("bpat", [32, 2], BF16))
        t["hmask"] = e(sb("hmask_s", [128, 4], F32))
        t["bpat2"] = e(sb("bpat2", [2, 32], BF16))
        t["xsrc"] = [e(sb(f"xsrc{i}", [128, 512], F32)) for i in range(2)]
        t["xb"] = e(sb("xbuf", [128, 512], BF16))
        t["sqb"] = e(sb("sqbuf", [128, 512], BF16))
        t["tmp32"] = e(sb("tmp32", [128, 512], F32))
        t["rows"] = e(sb("rows", [1, 1024], F32))
        t["rows2"] = e(sb("rows2", [2, 512], F32))
        t["rowb"] = e(sb("rowb", [1, 1024], BF16))
        t["rows2b"] = e(sb("rows2b", [2, 512], BF16))
        t["bc32"] = e(sb("bc32", [128, 1024], F32))
        t["wstr"] = [e(sb(f"wstr{i}", [128, 512], BF16)) for i in range(4)]
        t["wcf32"] = [e(sb(f"wcf32{i}", [128, 1024], F32)) for i in range(2)]
        t["wcb16"] = [e(sb(f"wcb16{i}", [128, 1024], BF16)) for i in range(2)]
        t["xnsl"] = e(sb("xnsl", [128, 32 * 512], BF16))
        t["t1"] = e(sb("t1buf", [128, 512], F32))
        t["x1t"] = e(sb("x1t", [128, 512], F32))
        t["xtb"] = e(sb("xtb", [128, 515], F32))
        t["tails"] = e(sb("tails", [128, 64 * 3], F32))
        t["accs"] = [e(sb(f"accs{i}", [128, 512], F32)) for i in range(2)]
        t["xcb"] = e(sb("xcbuf", [128, 512], BF16))
        t["ljar"] = e(sb("ljar", [128, 64 * 288], BF16))
        t["xpj"] = t["ljar"]
        t["convw"] = e(sb("convw_s", [128, 256], F32))
        t["convb"] = e(sb("convb_s", [128, 64], F32))
        t["ssm0"] = e(sb("ssm0", [128, n_real], F32))
        t["ssm1"] = e(sb("ssm1", [128, n_real], F32))
        t["ssm2"] = e(sb("ssm2", [32, n_real], F32))
        t["dtln"] = e(sb("dtln", [128, 2], F32))
        t["bcln"] = e(sb("bcln", [32, 1], F32))
        t["dtrn"] = e(sb("dtrn", [128, 2 * 512], BF16))
        t["bcb16"] = e(sb("bcb16", [32, 512], BF16))
        t["dtpj"] = t["ljar"]
        t["dtbias"] = e(sb("dtbias_s", [128, 64], F32))
        t["sps"] = [e(sb(f"sps{i}", [128, 1024], F32)) for i in range(2)]
        t["spr"] = e(sb("spr", [128, 1024], F32))
        t["ps"] = [e(psa(f"psum{i}", [128, 512], F32)) for i in range(5)]
        t["ps_x2"] = e(psa("ps_x2", [32, 512], F32))
        t["ps_row"] = e(psa("ps_row", [2, 1024], F32))
        t["ps_e"] = t["ps"][4]
        p.sems = _mk_sems(nc, stack)

    emit_consts(p, t, nc, stack, (EPS_LN, EPS_RMS, 1.0, -0.5, -1.0, 0.5))
    for (name, dst) in (("qscale_d", "qscale"), ("qzp_d", "qzp"),
                        ("fbias_d", "hb"), ("dtbias_d", "dtbias"),
                        ("dtlnw_d", "dtln"), ("bclnw_d", "bcln"),
                        ("convw_d", "convw"), ("convb_d", "convb")):
        p.do("sync", lambda e, name=name, dst=dst: e.dma_start(
            out=t[dst][:], in_=t[name][:]), inc=("si", 16))
    p.mark("smalls", "si")
    p.do("gpsimd", lambda e: e.dma_start(
        out=t["xpj"][:].rearrange("q (c r) -> q c r", r=288),
        in_=t["xprojT_d"][:].rearrange("(c q) r -> q c r", q=128)), inc=("gj", 16))
    p.do("gpsimd", lambda e: e.memset(t["onesb"][:], 1.0))
    p.do("gpsimd", lambda e: e.memset(t["ones1"][:], 1.0), inc=("g", 1))
    p.do("gpsimd", lambda e: e.dma_start(out=t["bpat"][:], in_=t["bpat_d"][:]),
         inc=("gj", 16))
    p.do("gpsimd", lambda e: e.dma_start(out=t["bpat2"][:], in_=t["bpat2_d"][:]),
         inc=("gj", 16))
    p.do("gpsimd", lambda e: e.dma_start(out=t["hmask"][:], in_=t["hmask_d"][:]),
         inc=("gj", 16))
    p.mark("hmaskld", "gj")
    p.mark("ones", "g")
    p.mark("ljt", "gj")
    p.mark("bpats", "gj")
    p.wait("vector", "smalls")
    p.do("vector", lambda e: e.tensor_scalar(
        out=t["hb"][:], in0=t["hb"][:], scalar1=0.5, scalar2=None,
        op0=ALU.mult), inc=("v", 1))
    p.mark("hb", "v")
    emit_dequant(p, t)
    for ci in range(512):
        kt, ct = ci // 16, ci % 16
        pb = ci % 2
        sname = "gw" if pb == 0 else "gw2"
        p.wait("sync", f"wipc_{ci - 2}")
        p.do("sync", lambda e, kt=kt, ct=ct, pb=pb: e.dma_start(
            out=t["wcf32"][pb][:], in_=t["wip_d"][kt * 128:(kt + 1) * 128,
                                                  ct * 1024:(ct + 1) * 1024]),
            inc=(sname, 16))
        p.mark(f"wipl_{ci}", sname)
        p.wait("vector", f"wipl_{ci}")
        p.wait("vector", f"wipst_{ci - 2}")
        p.do("vector", lambda e, pb=pb: e.tensor_copy(
            out=t["wcb16"][pb][:], in_=t["wcf32"][pb][:]), inc=("v", 1))
        p.mark(f"wipc_{ci}", "v")
        p.wait("scalar", f"wipc_{ci}")
        p.do("scalar", lambda e, kt=kt, ct=ct, pb=pb: e.dma_start(
            out=t["wip16"][kt * 128:(kt + 1) * 128, ct * 1024:(ct + 1) * 1024],
            in_=t["wcb16"][pb][:]), inc=(f"gb{pb}", 16))
        p.mark(f"wipst_{ci}", f"gb{pb}")
    p.mark("wip16", "gb0")
    p.mark("wip16b", "gb1")

    p.wait("tensor", "ones")
    p.wait("vector", "ones")
    for ti, (tt0, w) in enumerate(tts):
        pre = f"T{ti}."
        lo = max(tt0, 3)
        wo = tt0 + w - lo

        emit_ln(p, t, pre + "ln1.", w, 32,
                lambda kt, tt0=tt0, w=w: t["x_d"][kt * 128:(kt + 1) * 128,
                                                  tt0:tt0 + w],
                lambda kt, w=w: t["xnsl"][:, kt * 512:kt * 512 + w],
                f"T{ti - 1}.ipdone",
                f"T{ti - 1}.ln2.var", f"T{ti - 1}.ln2.bce1")

        if ti == 0:
            for wb in range(4):
                p.wait("tensor", f"dq_done{wb}")
            p.wait("tensor", "hb")

        def evac_ffn1(p, nq, ns, ev, pre=pre, tt0=tt0, w=w, lo=lo, wo=wo):
            n = ev
            p.wait("vector", f"{pre}f1.g_{nq}")
            p.wait("vector", f"{pre}f1.std_{ev - 1}")
            p.do("vector", lambda e, ns=ns, n=n, w=w: e.tensor_scalar(
                out=t["t1"][:, :w], in0=t["ps"][ns][:, :w],
                scalar1=0.5, scalar2=t["hb"][:, n:n + 1],
                op0=ALU.mult, op1=ALU.add), inc=("v", 1))
            p.mark(f"{pre}f1.free_{nq}_{ns}", "v")
            p.mark(f"{pre}f1.t1_{ev}", "v")
            xp = ev % 2
            if ev < 2:
                p.wait("sync", f"{pre}ln1.done")
            p.wait("sync", f"{pre}f1.x1_{ev - 2}")
            p.do("sync", lambda e, n=n, tt0=tt0, w=w, xp=xp: e.dma_start(
                out=t["xsrc"][xp][:, :w], in_=t["x_d"][n * 128:(n + 1) * 128,
                                                       tt0:tt0 + w]),
                inc=(f"x{xp}", 16))
            p.mark(f"{pre}f1.xl_{ev}", f"x{xp}")
            p.wait("vector", f"{pre}f1.xl_{ev}")
            p.wait_self("vector", "v")
            p.do("vector", lambda e, w=w, xp=xp: e.tensor_add(
                out=t["x1t"][:, :w], in0=t["t1"][:, :w], in1=t["xsrc"][xp][:, :w]),
                inc=("v", 1))
            p.mark(f"{pre}f1.x1_{ev}", "v")
            p.wait("sync", f"{pre}f1.x1_{ev}")
            p.do("sync", lambda e, n=n, tt0=tt0, w=w: e.dma_start(
                out=t["x1f"][n * 128:(n + 1) * 128, tt0:tt0 + w],
                in_=t["x1t"][:, :w]), inc=("sx1", 16))
            if wo > 0:
                p.do("sync", lambda e, n=n, lo=lo, wo=wo, tt0=tt0: e.dma_start(
                    out=t["x1_o"][n * 128:(n + 1) * 128, lo - 3:lo - 3 + wo],
                    in_=t["x1t"][:, lo - tt0:lo - tt0 + wo]), inc=("sx1", 16))
            p.mark(f"{pre}f1.std_{ev}", "sx1")
            p.mark(f"{pre}f1.stdk_{ev}", "sx1")

        emit_ffn(p, t, pre + "f1.", w,
                 lambda k, nq: t["wt_dram"][k * 128:(k + 1) * 128,
                                            nq * 512:(nq + 1) * 512],
                 evac_ffn1, f"{pre}ln1.done",
                 (f"T{ti - 1}.ipdone", f"T{ti - 1}.ssmev"),
                 tuple(f"dq_done{i}" for i in range(4)) if ti == 0 else ())
        p.mark(f"{pre}ffn1t", "tn")

        emit_ln(p, t, pre + "ln2.", w, 32,
                lambda kt, tt0=tt0, w=w: t["x1f"][kt * 128:(kt + 1) * 128,
                                                  tt0:tt0 + w],
                lambda kt, w=w: t["xnsl"][:, kt * 512:kt * 512 + w],
                f"{pre}ffn1t",
                f"{pre}ln1.var", f"{pre}ln1.bce1",
                src_guard_fn=lambda kt, pre=pre: f"{pre}f1.stdk_{kt}")

        for cq in range(32):
            for k in range(32):
                pb = k % 4
                qe = "sync" if k % 2 == 0 else "scalar"
                if p.lastw.get(pb):
                    p.wait(qe, p.lastw[pb])
                if cq == 0 and k < 4 and ti == 0:
                    p.wait(qe, "wip16")
                    p.wait(qe, "wip16b")
                p.do(qe, lambda e, cq=cq, k=k, pb=pb: e.dma_start(
                    out=t["wstr"][pb][:],
                    in_=t["wip16"][k * 128:(k + 1) * 128,
                                   cq * 512:(cq + 1) * 512]), inc=(f"sw{pb}", 16))
                p.mark(f"{pre}ip.w_{cq}_{k}", f"sw{pb}")
                p.wait("tensor", f"{pre}ip.w_{cq}_{k}")
                if k == 0:
                    if cq == 0:
                        p.wait("tensor", f"{pre}ln2.done")
                        if ti == 0:
                            p.wait("tensor", "ljt")
                    for cs in range(4):
                        p.wait("tensor", f"{pre}ip.free_{cq - 1}_{cs}")
                for cs in range(4):
                    p.do("tensor", lambda e, k=k, cs=cs, pb=pb, w=w: e.matmul(
                        t["ps"][cs][:, :w], t["wstr"][pb][:, cs * 128:(cs + 1) * 128],
                        t["xnsl"][:, k * 512:k * 512 + w],
                        start=(k == 0), stop=(k == 31)),
                        inc=("tn", 1) if cs == 3 else None)
                p.mark(f"{pre}ip.c_{cq}_{k}", "tn")
                p.lastw[pb] = f"{pre}ip.c_{cq}_{k}"
            p.mark(f"{pre}ip.g_{cq}", "tn")
            for cs in range(4):
                c = cq * 4 + cs
                if c < 64:
                    p.wait("scalar", f"{pre}ip.g_{cq}")
                    if c == 0:
                        p.wait("scalar", f"T{ti - 1}.xcd_127")
                    p.wait("scalar", f"{pre}cv.done_{c - 1}")
                    p.do("scalar", lambda e, cs=cs, w=w: e.activation(
                        out=t["xtb"][:, 3:3 + w], in_=t["ps"][cs][:, :w],
                        func=AF.Copy), inc=("a", 1))
                    p.mark(f"{pre}ip.free_{cq}_{cs}", "a")
                    if ti > 0:
                        p.wait_self("scalar", "a")
                        p.do("scalar", lambda e, c=c: e.activation(
                            out=t["xtb"][:, 0:3], in_=t["tails"][:, c * 3:c * 3 + 3],
                            func=AF.Copy), inc=("a", 1))
                    p.mark(f"{pre}xt_{c}", "a")
                    if ti == 0:
                        p.wait("vector", f"{pre}xt_{c}")
                        p.wait("vector", "hmaskld")
                        p.do("vector", lambda e: e.tensor_mul(
                            out=t["xtb"][:, 3:6], in0=t["xtb"][:, 3:6],
                            in1=t["hmask"][:, 0:3]), inc=("v", 1))
                        p.mark(f"{pre}xt_{c}", "v")
                        p.wait("scalar", f"{pre}xt_{c}")
                    b0 = lo - tt0
                    p.wait("scalar", f"{pre}cv.acc_{c - 1}")
                    p.wait_self("scalar", "a")
                    p.do("scalar", lambda e, c=c, b0=b0, wo=wo: e.activation(
                        out=t["accs"][0][:, :wo], in_=t["xtb"][:, b0:b0 + wo],
                        func=AF.Copy, scale=t["convw"][:, c * 4:c * 4 + 1]),
                        inc=("a", 1))
                    p.mark(f"{pre}cv0_{c}", "a")
                    p.wait("vector", f"{pre}cv0_{c}")
                    p.wait("vector", f"{pre}cv.vdone_{c - 1}")
                    for j in (1, 2, 3):
                        jsrc, jdst = (j + 1) % 2, j % 2
                        if j > 1:
                            p.wait_self("vector", "v")
                        p.do("vector", lambda e, c=c, j=j, b0=b0, wo=wo,
                             jsrc=jsrc, jdst=jdst: e.scalar_tensor_tensor(
                                 out=t["accs"][jdst][:, :wo],
                                 in0=t["xtb"][:, b0 + j:b0 + j + wo],
                                 scalar=t["convw"][:, c * 4 + j:c * 4 + j + 1],
                                 in1=t["accs"][jsrc][:, :wo],
                                 op0=ALU.mult, op1=ALU.add), inc=("v", 1))
                    p.mark(f"{pre}cvt_{c}", "v")
                    p.mark(f"{pre}cv.vdone_{c}", "v")
                    p.wait("scalar", f"{pre}cvt_{c}")
                    if c == 0:
                        p.wait("scalar", f"T{ti - 1}.xcd_127")
                    p.wait("scalar", f"{pre}xcd_{c - 1}")
                    p.wait("scalar", f"{pre}xpj_{c - 1}")
                    p.do("scalar", lambda e, c=c, wo=wo: e.activation(
                        out=t["xcb"][:, :wo], in_=t["accs"][1][:, :wo],
                        func=AF.Silu, bias=t["convb"][:, c:c + 1]), inc=("a", 1))
                    p.mark(f"{pre}xc_{c}", "a")
                    p.mark(f"{pre}cv.acc_{c}", "a")
                    p.wait("sync", f"{pre}xc_{c}")
                    p.do("sync", lambda e, c=c, lo=lo, wo=wo: e.dma_start(
                        out=t["xc_o"][c * 128:(c + 1) * 128, lo - 3:lo - 3 + wo],
                        in_=t["xcb"][:, :wo]), inc=("sxc", 16))
                    p.mark(f"{pre}xcdma_{c}", "sxc")
                    p.wait("tensor", f"{pre}xc_{c}")
                    for (ei, dst, pst, elo, ehi) in (
                            (0, "ssm0", "ps_e", 0, 128),
                            (1, "ssm1", "ps_e", 128, 256),
                            (2, "ssm2", "ps_x2", 256, 288)):
                        if ei == 0:
                            p.wait("tensor", f"{pre}xa1_{c - 1}")
                        elif ei == 1:
                            p.wait("tensor", f"{pre}xa0_{c}")
                        else:
                            p.wait("tensor", f"{pre}xa2_{c - 1}")
                        p.do("tensor", lambda e, c=c, wo=wo, pst=pst, elo=elo,
                             ehi=ehi: e.matmul(
                                 t[pst][:, :wo] if pst == "ps_x2" else
                                 t[pst][:, :wo],
                                 t["xpj"][:].rearrange("q (c r) -> q c r", r=288)
                                 [:, c, elo:ehi],
                                 t["xcb"][:, :wo], start=True, stop=True),
                             inc=("tn", 1))
                        p.mark(f"{pre}xm{ei}_{c}", "tn")
                        p.wait("vector", f"{pre}xm{ei}_{c}")
                        if c > 0:
                            p.wait_self("vector", "v")
                        if c == 0:
                            p.do("vector", lambda e, dst=dst, lo=lo, wo=wo,
                                 pst=pst: e.tensor_copy(
                                     out=t[dst][:, lo - 3:lo - 3 + wo],
                                     in_=t[pst][:, :wo]), inc=("v", 1))
                        else:
                            p.do("vector", lambda e, dst=dst, lo=lo, wo=wo,
                                 pst=pst: e.tensor_tensor(
                                     out=t[dst][:, lo - 3:lo - 3 + wo],
                                     in0=t[pst][:, :wo],
                                     in1=t[dst][:, lo - 3:lo - 3 + wo],
                                     op=ALU.add), inc=("v", 1))
                        p.mark(f"{pre}xa{ei}_{c}", "v")
                    p.mark(f"{pre}xpj_{c}", "v")
                    if ti < len(tts) - 1:
                        p.wait("vector", f"{pre}xt_{c}")
                        p.do("vector", lambda e, c=c, w=w: e.tensor_copy(
                            out=t["tails"][:, c * 3:c * 3 + 3],
                            in_=t["xtb"][:, w:w + 3]), inc=("v", 1))
                    p.mark(f"{pre}cv.done_{c}", "v")
                else:
                    p.wait("scalar", f"{pre}ip.g_{cq}")
                    p.wait("scalar", f"{pre}xcd_{c - 1}")
                    p.wait("scalar", f"{pre}xpj_{c - 1}")
                    p.do("scalar", lambda e, cs=cs, lo=lo, tt0=tt0, wo=wo:
                         e.activation(out=t["xcb"][:, :wo],
                                      in_=t["ps"][cs][:, lo - tt0:lo - tt0 + wo],
                                      func=AF.Silu), inc=("a", 1))
                    p.mark(f"{pre}ip.free_{cq}_{cs}", "a")
                    p.mark(f"{pre}z_{c}", "a")
                    p.wait("sync", f"{pre}z_{c}")
                    p.do("sync", lambda e, c=c, lo=lo, wo=wo: e.dma_start(
                        out=t["sz_o"][(c - 64) * 128:(c - 63) * 128,
                                      lo - 3:lo - 3 + wo],
                        in_=t["xcb"][:, :wo]), inc=("sxc", 16))
                p.mark(f"{pre}xcd_{c}", "sxc")
        p.mark(f"{pre}ipdone", "tn")
        p.mark(f"T{ti}.ssmev", "v")
    p.mark("ssm_done", "v")

    n_tt_last = len(tts) - 1
    p.wait("gpsimd", f"T{n_tt_last}.ipdone")
    p.do("gpsimd", lambda e: e.dma_start(
        out=t["dtpj"][:, 0:2 * DIN].rearrange("q (c d) -> q c d", d=DIN),
        in_=t["dtprojT_d"][:].rearrange("(c q) d -> q c d", q=128)), inc=("gj", 16))
    p.mark("dtpjld", "gj")

    n_rt = (n_real + 511) // 512
    for rt in range(n_rt):
        o = rt * 512
        w = min(512, n_real - o)
        pre = f"R{rt}."
        p.wait("scalar", "ssm_done")
        p.wait("scalar", f"R{rt - 1}.m2")
        p.do("scalar", lambda e, o=o, w=w: e.activation(
            out=t["xb"][:, :w], in_=t["ssm0"][:, o:o + w], func=AF.Square),
            inc=("a", 1))
        p.mark(f"{pre}s0", "a")
        if rt == 0:
            p.wait("tensor", "ssm_done")
            p.wait("tensor", f"T{len(tts) - 1}.ln2.var")
            p.wait("tensor", f"T{len(tts) - 1}.ln2.bce1")
        p.wait("tensor", f"{pre}s0")
        p.wait("tensor", f"R{rt - 1}.mean")
        p.do("tensor", lambda e, w=w: e.matmul(
            t["ps_row"][0:1, 0:w], t["onesb"][:], t["xb"][:, :w],
            start=True, stop=False), inc=("tn", 1))
        p.mark(f"{pre}m0", "tn")
        p.wait("scalar", f"{pre}m0")
        p.do("scalar", lambda e, o=o, w=w: e.activation(
            out=t["xb"][:, :w], in_=t["ssm1"][:, o:o + w], func=AF.Square),
            inc=("a", 1))
        p.mark(f"{pre}s1", "a")
        p.wait("tensor", f"{pre}s1")
        p.do("tensor", lambda e, w=w: e.matmul(
            t["ps_row"][0:1, 0:w], t["onesb"][:], t["xb"][:, :w],
            start=False, stop=True), inc=("tn", 1))
        p.mark(f"{pre}m1", "tn")
        p.wait("scalar", f"{pre}m1")
        p.do("scalar", lambda e, o=o, w=w: e.activation(
            out=t["xb"][0:32, :w], in_=t["ssm2"][:, o:o + w], func=AF.Square),
            inc=("a", 1))
        p.mark(f"{pre}s2", "a")
        p.wait("tensor", f"{pre}s2")
        if rt == 0:
            p.wait("tensor", "bpats")
        p.do("tensor", lambda e, w=w: e.matmul(
            t["ps_row"][0:2, 512:512 + w], t["bpat"][:], t["xb"][0:32, :w],
            start=True, stop=True), inc=("tn", 1))
        p.mark(f"{pre}m2", "tn")
        p.wait("vector", f"{pre}m1")
        p.wait("vector", f"{pre}m2")
        p.wait("vector", f"R{rt - 1}.dtrn")
        p.do("vector", lambda e, w=w: e.tensor_scalar(
            out=t["rows"][0:1, :w], in0=t["ps_row"][0:1, 0:w],
            scalar1=1.0 / 256, scalar2=None, op0=ALU.mult), inc=("v", 1))
        p.do("vector", lambda e, w=w: e.tensor_scalar(
            out=t["rows2"][0:2, :w], in0=t["ps_row"][0:2, 512:512 + w],
            scalar1=1.0 / 16, scalar2=None, op0=ALU.mult), inc=("v", 1))
        p.mark(f"{pre}mean", "v")
        p.wait("scalar", f"{pre}mean")
        p.do("scalar", lambda e, w=w: e.activation(
            out=t["rows"][0:1, :w], in_=t["rows"][0:1, :w], func=AF.Ln,
            bias=EPS_RMS), inc=("a", 1))
        p.do("scalar", lambda e, w=w: e.activation(
            out=t["rows2"][0:2, :w], in_=t["rows2"][0:2, :w], func=AF.Ln,
            bias=EPS_RMS), inc=("a", 1))
        p.wait_self("scalar", "a")
        p.do("scalar", lambda e, w=w: e.activation(
            out=t["rows"][0:1, :w], in_=t["rows"][0:1, :w], func=AF.Exp,
            scale=-0.5), inc=("a", 1))
        p.do("scalar", lambda e, w=w: e.activation(
            out=t["rows2"][0:2, :w], in_=t["rows2"][0:2, :w], func=AF.Exp,
            scale=-0.5), inc=("a", 1))
        p.mark(f"{pre}rstd", "a")
        p.wait("vector", f"{pre}rstd")
        p.do("vector", lambda e, w=w: e.tensor_copy(
            out=t["rowb"][0:1, 0:w], in_=t["rows"][0:1, :w]))
        p.do("vector", lambda e, w=w: e.tensor_copy(
            out=t["rows2b"][:, :w], in_=t["rows2"][0:2, :w]), inc=("v", 1))
        p.mark(f"{pre}rowb", "v")
        p.wait("tensor", f"{pre}rowb")
        p.wait("tensor", f"R{rt - 1}.dtrn")
        p.do("tensor", lambda e, w=w: e.matmul(
            t["ps_e"][:, 0:w], t["ones1"][:], t["rowb"][0:1, 0:w],
            start=True, stop=True), inc=("tn", 1))
        p.mark(f"{pre}bcmm", "tn")
        p.wait("scalar", f"{pre}bcmm")
        p.wait("scalar", f"R{rt - 1}.dtrn")
        p.do("scalar", lambda e, w=w: e.activation(
            out=t["bc32"][:, 0:w], in_=t["ps_e"][:, 0:w], func=AF.Copy),
            inc=("a", 1))
        p.mark(f"{pre}bcev", "a")
        for kt in range(2):
            p.wait("vector", f"{pre}bcev")
            p.do("vector", lambda e, kt=kt, o=o, w=w: e.scalar_tensor_tensor(
                out=t["dtrn"][:, kt * 512:kt * 512 + w],
                in0=t["ssm0" if kt == 0 else "ssm1"][:, o:o + w],
                scalar=t["dtln"][:, kt:kt + 1], in1=t["bc32"][:, 0:w],
                op0=ALU.mult, op1=ALU.mult), inc=("v", 1))
        p.mark(f"{pre}dtrn", "v")
        p.wait("tensor", f"{pre}rowb")
        p.wait("tensor", f"R{rt - 1}.bcn")
        p.do("tensor", lambda e, w=w: e.matmul(
            t["ps_x2"][:, :w], t["bpat2"][:], t["rows2b"][:, :w],
            start=True, stop=True), inc=("tn", 1))
        p.mark(f"{pre}bcbc", "tn")
        p.wait("vector", f"{pre}bcbc")
        p.wait("vector", f"R{rt - 1}.bcnd")
        p.do("vector", lambda e, o=o, w=w: e.scalar_tensor_tensor(
            out=t["bcb16"][:, :w], in0=t["ssm2"][:, o:o + w],
            scalar=t["bcln"][:, 0:1], in1=t["ps_x2"][:, :w],
            op0=ALU.mult, op1=ALU.mult), inc=("v", 1))
        p.mark(f"{pre}bcn", "v")
        p.wait("sync", f"{pre}bcn")
        p.do("sync", lambda e, o=o, w=w: e.dma_start(
            out=t["bc_o"][:, o:o + w], in_=t["bcb16"][:, :w]), inc=("sbc", 16))
        p.mark(f"{pre}bcnd", "sbc")
        for dc in range(64):
            pp = dc % 2
            p.wait("tensor", f"{pre}dtrn")
            p.wait("tensor", f"{pre}dtev_{dc - 2}")
            if rt == 0 and dc < 2:
                p.wait("tensor", "ssm_done")
                p.wait("tensor", "dtpjld")
            for kt in range(2):
                p.do("tensor", lambda e, dc=dc, kt=kt, pp=pp, w=w: e.matmul(
                    t["ps"][pp][:, :w],
                    t["dtpj"][:, 0:2 * DIN].rearrange("q (a d) -> q a d", d=DIN)
                    [:, kt, dc * 128:(dc + 1) * 128],
                    t["dtrn"][:, kt * 512:kt * 512 + w],
                    start=(kt == 0), stop=(kt == 1)),
                    inc=("tn", 1) if kt == 1 else None)
            p.mark(f"{pre}dtmm_{dc}", "tn")
            po = pp * 512
            p.wait("scalar", f"{pre}dtmm_{dc}")
            p.wait("scalar", f"{pre}dtvd_{dc - 2}")
            if dc < 2:
                p.wait("scalar", f"R{rt - 1}.dtvd_{62 + dc}")
            p.do("scalar", lambda e, dc=dc, pp=pp, w=w, po=po: e.activation(
                out=t["sps"][0][:, po:po + w], in_=t["ps"][pp][:, :w], func=AF.Abs,
                bias=t["dtbias"][:, dc:dc + 1]), inc=("a", 1))
            p.wait_self("scalar", "a")
            p.do("scalar", lambda e, w=w, po=po: e.activation(
                out=t["sps"][1][:, po:po + w], in_=t["sps"][0][:, po:po + w],
                func=AF.Exp, scale=-1.0), inc=("a", 1))
            p.wait_self("scalar", "a")
            p.do("scalar", lambda e, w=w, po=po: e.activation(
                out=t["sps"][0][:, po:po + w], in_=t["sps"][1][:, po:po + w],
                func=AF.Ln, bias=1.0), inc=("a", 1))
            p.do("scalar", lambda e, dc=dc, pp=pp, w=w, po=po: e.activation(
                out=t["spr"][:, po:po + w], in_=t["ps"][pp][:, :w], func=AF.Relu,
                bias=t["dtbias"][:, dc:dc + 1]), inc=("a", 1))
            p.mark(f"{pre}dtev_{dc}", "a")
            p.mark(f"{pre}dtsp_{dc}", "a")
            p.wait("vector", f"{pre}dtsp_{dc}")
            p.wait("vector", f"{pre}dtvd_{dc - 2}")
            if dc < 2:
                p.wait("vector", f"R{rt - 1}.dtvd_{62 + dc}")
            p.do("vector", lambda e, w=w, po=po: e.tensor_add(
                out=t["sps"][1][:, po:po + w], in0=t["sps"][0][:, po:po + w],
                in1=t["spr"][:, po:po + w]), inc=("v", 1))
            p.mark(f"{pre}dtv_{dc}", "v")
            p.wait("sync", f"{pre}dtv_{dc}")
            p.do("sync", lambda e, dc=dc, o=o, w=w, po=po: e.dma_start(
                out=t["dt_o"][dc * 128:(dc + 1) * 128, o:o + w],
                in_=t["sps"][1][:, po:po + w]),
                inc=("sdt" if pp == 0 else "so", 16))
            p.mark(f"{pre}dtvd_{dc}", "sdt" if pp == 0 else "so")
        p.mark(f"{pre}fin", "sdt")
        p.mark(f"{pre}fin2", "so")

    finish(nc, p, stack)
    return p


def make_A(n_real=CHUNK):
    marks = {}
    build_A(marks, emit=False, n_real=n_real)
    nc = bass.Bass("TRN2", target_bir_lowering=False)
    with ExitStack() as stack:
        build_A(marks, emit=True, n_real=n_real, nc=nc, stack=stack)
    return nc



def build_B(marks, emit, n_t=16384, batch_starts=(0, 8192), nc=None, stack=None):
    TC = 512
    n_ch = n_t // TC
    p = P(marks, emit)
    t = {}
    if emit:
        dram, sb = nc.dram_tensor, nc.sbuf_tensor
        t["dt_d"] = dram("dt", [1024, n_t], F32, kind="ExternalInput").ap()
        t["xc_d"] = dram("xc", [1024, n_t], BF16, kind="ExternalInput").ap()
        t["sz_d"] = dram("sz", [1024, n_t], BF16, kind="ExternalInput").ap()
        t["bc_d"] = dram("bc", [32, n_t], BF16, kind="ExternalInput").ap()
        t["al_d"] = dram("al", [8, 128, 16], F32, kind="ExternalInput").ap()
        t["dp_d"] = dram("dp", [128, 8], F32, kind="ExternalInput").ap()
        t["y_o"] = dram("yo", [1024, n_t], BF16, kind="ExternalOutput").ap()
        e = stack.enter_context
        t["dt"] = [e(sb(f"dtt{i}", [128, TC], F32)) for i in range(2)]
        t["xcf"] = [e(sb(f"xcf{i}", [128, TC], F32)) for i in range(2)]
        t["szf"] = [e(sb(f"szf{i}", [128, TC], F32)) for i in range(2)]
        t["xcs"] = [e(sb(f"xcs{i}", [128, TC], BF16)) for i in range(2)]
        t["szs"] = [e(sb(f"szs{i}", [128, TC], BF16)) for i in range(2)]
        t["u"] = e(sb("ubuf", [128, TC], BF16))
        t["amat"] = e(sb("amat", [128, 8 * 16], F32))
        t["dmat"] = e(sb("dmat", [128, 8], F32))
        t["Bb"] = [e(sb("Bb0", [128, 16 * TC], BF16))]
        t["Cb"] = [e(sb("Cb0", [128, 16 * TC], BF16))]
        t["av"] = [e(sb(f"av{i}", [128, 8 * TC], BF16)) for i in range(4)]
        t["bv"] = [e(sb(f"bv{i}", [128, 8 * TC], BF16)) for i in range(4)]
        t["hv"] = [e(sb(f"hv{i}", [128, 8 * TC], BF16)) for i in range(4)]
        t["hc"] = e(sb("hcbuf", [128, 8 * TC], BF16))
        t["tr"] = e(sb("trbuf", [128, 4 * TC], BF16))
        t["yh"] = e(sb("yhbuf", [128, 2 * TC], BF16))
        t["yt"] = e(sb("ytbuf", [128, TC], F32))
        t["dxc"] = e(sb("dxcbuf", [128, TC], F32))
        t["yb"] = e(sb("ybbuf", [128, TC], BF16))
        t["hlast"] = e(sb("hlast", [128, 8 * 16], F32))
        p.sems = _mk_sems(nc, stack)

    emit_consts(p, t, nc, stack, (EPS_LN, EPS_RMS, 1.0, -0.5, -1.0, 0.5))
    p.do("sync", lambda e: e.dma_start(
        out=t["amat"][:].rearrange("q (b n) -> q b n", n=16), in_=t["al_d"][:]),
        inc=("si", 16))
    p.do("sync", lambda e: e.dma_start(out=t["dmat"][:], in_=t["dp_d"][:]),
         inc=("si", 16))
    p.mark("smalls", "si")
    p.wait("scalar", "smalls")
    p.do("scalar", lambda e: e.activation(
        out=t["amat"][:], in_=t["amat"][:], func=AF.Exp), inc=("a", 1))
    p.wait_self("scalar", "a")
    p.do("scalar", lambda e: e.activation(
        out=t["amat"][:], in_=t["amat"][:], func=AF.Copy, scale=-1.0),
        inc=("a", 1))
    p.mark("aneg", "a")

    for ch in range(n_ch):
        tc0 = ch * TC
        cb = 0
        p.wait("sync", f"bcused_{ch - 1}")
        p.wait("sync", f"bcusedg_{ch - 1}")
        for (nm, roff) in (("Bb", 0), ("Cb", 16)):
            for hh in range(2):
                p.do("sync", lambda e, nm=nm, roff=roff, hh=hh, tc0=tc0, cb=cb:
                     e.dma_start(
                         out=t[nm][cb][:].rearrange("q (n c) -> q n c", c=TC)
                         [:, hh * 8:(hh + 1) * 8, :],
                         in_=t["bc_d"][roff + hh * 8:roff + (hh + 1) * 8,
                                       tc0:tc0 + TC][None, :, :]
                         .broadcast_to([128, 8, TC])), inc=(f"gb{cb}", 16))
        p.mark(f"bcb_{ch}", f"gb{cb}")
        for blk in range(8):
            it = ch * 8 + blk
            pb = it % 2
            p.wait("sync", f"u_{it - 2}")
            p.wait("sync", f"a_{it - 2}_1")
            p.wait("sync", f"ydone_{it - 2}")
            p.do("sync", lambda e, blk=blk, tc0=tc0, pb=pb: e.dma_start(
                out=t["dt"][pb][:], in_=t["dt_d"][blk * 128:(blk + 1) * 128,
                                                  tc0:tc0 + TC]), inc=(f"b{pb}", 16))
            p.mark(f"ld_{it}", f"b{pb}")
            p.wait("scalar", f"cast_{it - 2}")
            p.do("scalar", lambda e, blk=blk, tc0=tc0, pb=pb: e.dma_start(
                out=t["xcs"][pb][:], in_=t["xc_d"][blk * 128:(blk + 1) * 128,
                                                   tc0:tc0 + TC]), inc=(f"g{pb}", 16))
            p.do("scalar", lambda e, blk=blk, tc0=tc0, pb=pb: e.dma_start(
                out=t["szs"][pb][:], in_=t["sz_d"][blk * 128:(blk + 1) * 128,
                                                   tc0:tc0 + TC]), inc=(f"g{pb}", 16))
            p.mark(f"ldg_{it}", f"g{pb}")
            p.wait("scalar", f"ldg_{it}")
            p.wait("scalar", f"u_{it - 2}")
            p.wait("scalar", f"ydone_{it - 2}")
            p.do("scalar", lambda e, pb=pb: e.activation(
                out=t["xcf"][pb][:], in_=t["xcs"][pb][:], func=AF.Copy),
                inc=("a", 1))
            p.do("scalar", lambda e, pb=pb: e.activation(
                out=t["szf"][pb][:], in_=t["szs"][pb][:], func=AF.Copy),
                inc=("a", 1))
            p.mark(f"cast_{it}", "a")
            for hf in range(2):
                ab = pb * 2 + hf
                p.wait("scalar", f"ld_{it}")
                if hf == 0:
                    p.wait("scalar", "aneg")
                p.wait("scalar", f"scans_{it - 2}_{hf}")
                for n in range(8):
                    nn = hf * 8 + n
                    p.do("scalar", lambda e, blk=blk, nn=nn, n=n, ab=ab, pb=pb:
                         e.activation(
                             out=t["av"][ab][:, n * TC:(n + 1) * TC],
                             in_=t["dt"][pb][:], func=AF.Exp,
                             scale=t["amat"][:, blk * 16 + nn:blk * 16 + nn + 1]),
                         inc=("a", 1) if n == 7 else None)
                p.mark(f"a_{it}_{hf}", "a")
            p.mark(f"a_{it}_1", "a")
            p.wait("vector", f"ld_{it}")
            p.wait("vector", f"cast_{it}")
            p.wait("vector", f"b_{it - 1}_1")
            p.do("vector", lambda e, pb=pb: e.tensor_tensor(
                out=t["u"][:], in0=t["dt"][pb][:], in1=t["xcf"][pb][:],
                op=ALU.mult), inc=("v", 1))
            p.mark(f"u_{it}", "v")
            for hf in range(2):
                ab = pb * 2 + hf
                p.wait("vector", f"bcb_{ch}")
                p.wait("vector", f"scans_{it - 2}_{hf}")
                p.wait_self("vector", "v")
                p.do("vector", lambda e, ab=ab, cb=cb, hf=hf: e.tensor_tensor(
                    out=t["bv"][ab][:].rearrange("q (n c) -> q n c", c=TC),
                    in0=t["u"][:][:, None, :].broadcast_to([128, 8, TC]),
                    in1=t["Bb"][cb][:].rearrange("q (n c) -> q n c", c=TC)
                    [:, hf * 8:(hf + 1) * 8, :], op=ALU.mult), inc=("v", 1))
                p.mark(f"b_{it}_{hf}", "v")
                p.wait("vector", f"a_{it}_{hf}")
                p.wait("vector", f"hcd_{it - 2}_{hf}")
                p.wait_self("vector", "v")
                for n in range(8):
                    nn = hf * 8 + n
                    if tc0 in batch_starts:
                        p.do("vector", lambda e, ab=ab, n=n: e.tensor_tensor_scan(
                            out=t["hv"][ab][:, n * TC:(n + 1) * TC],
                            data0=t["av"][ab][:, n * TC:(n + 1) * TC],
                            data1=t["bv"][ab][:, n * TC:(n + 1) * TC],
                            initial=0.0, op0=ALU.mult, op1=ALU.add),
                            inc=("v", 1) if n == 7 else None)
                    else:
                        p.do("vector", lambda e, ab=ab, n=n, nn=nn, blk=blk:
                             e.tensor_tensor_scan(
                                 out=t["hv"][ab][:, n * TC:(n + 1) * TC],
                                 data0=t["av"][ab][:, n * TC:(n + 1) * TC],
                                 data1=t["bv"][ab][:, n * TC:(n + 1) * TC],
                                 initial=t["hlast"][:, blk * 16 + nn:
                                                    blk * 16 + nn + 1],
                                 op0=ALU.mult, op1=ALU.add),
                             inc=("v", 1) if n == 7 else None)
                p.mark(f"scans_{it}_{hf}", "v")
                p.wait_self("vector", "v")
                p.do("vector", lambda e, ab=ab, blk=blk, hf=hf: e.tensor_copy(
                    out=t["hlast"][:, blk * 16 + hf * 8:blk * 16 + hf * 8 + 8],
                    in_=t["hv"][ab][:].rearrange("q (n c) -> q n c", c=TC)
                    [:, :, TC - 1]), inc=("v", 1))
                p.mark(f"hl_{it}_{hf}", "v")
                p.wait_self("vector", "v")
                p.do("vector", lambda e, ab=ab, cb=cb, hf=hf: e.tensor_tensor(
                    out=t["hc"][:].rearrange("q (n c) -> q n c", c=TC),
                    in0=t["hv"][ab][:].rearrange("q (n c) -> q n c", c=TC),
                    in1=t["Cb"][cb][:].rearrange("q (n c) -> q n c", c=TC)
                    [:, hf * 8:(hf + 1) * 8, :], op=ALU.mult), inc=("v", 1))
                p.mark(f"hcd_{it}_{hf}", "v")
                p.wait_self("vector", "v")
                p.do("vector", lambda e: e.tensor_add(
                    out=t["tr"][:], in0=t["hc"][:, 0:4 * TC],
                    in1=t["hc"][:, 4 * TC:8 * TC]), inc=("v", 1))
                p.wait_self("vector", "v")
                p.do("vector", lambda e: e.tensor_add(
                    out=t["tr"][:, 0:2 * TC], in0=t["tr"][:, 0:2 * TC],
                    in1=t["tr"][:, 2 * TC:4 * TC]), inc=("v", 1))
                p.wait_self("vector", "v")
                p.do("vector", lambda e, hf=hf: e.tensor_add(
                    out=t["yh"][:, hf * TC:(hf + 1) * TC], in0=t["tr"][:, 0:TC],
                    in1=t["tr"][:, TC:2 * TC]), inc=("v", 1))
                p.mark(f"tree_{it}_{hf}", "v")
            p.wait_self("vector", "v")
            p.do("vector", lambda e: e.tensor_add(
                out=t["yt"][:], in0=t["yh"][:, 0:TC], in1=t["yh"][:, TC:2 * TC]),
                inc=("v", 1))
            p.do("vector", lambda e, pb=pb, blk=blk: e.tensor_scalar(
                out=t["dxc"][:], in0=t["xcf"][pb][:],
                scalar1=t["dmat"][:, blk:blk + 1], scalar2=None, op0=ALU.mult),
                inc=("v", 1))
            p.wait_self("vector", "v")
            p.do("vector", lambda e: e.tensor_add(
                out=t["yt"][:], in0=t["yt"][:], in1=t["dxc"][:]), inc=("v", 1))
            p.wait_self("vector", "v")
            p.wait("vector", f"ydma_{it - 1}")
            p.do("vector", lambda e, pb=pb: e.tensor_tensor(
                out=t["yb"][:], in0=t["yt"][:], in1=t["szf"][pb][:],
                op=ALU.mult), inc=("v", 1))
            p.mark(f"ydone_{it}", "v")
            p.wait("sync", f"ydone_{it}")
            p.do("sync", lambda e, blk=blk, tc0=tc0: e.dma_start(
                out=t["y_o"][blk * 128:(blk + 1) * 128, tc0:tc0 + TC],
                in_=t["yb"][:]), inc=("sy", 16))
            p.mark(f"ydma_{it}", "sy")
        p.mark(f"bcused_{ch}", "v")
        p.mark(f"bcusedg_{ch}", "g")

    finish(nc, p, stack)
    return p


def make_B(n_t=16384, batch_starts=(0, 8192)):
    marks = {}
    build_B(marks, emit=False, n_t=n_t, batch_starts=batch_starts)
    nc = bass.Bass("TRN2", target_bir_lowering=False)
    with ExitStack() as stack:
        build_B(marks, emit=True, n_t=n_t, batch_starts=batch_starts,
                nc=nc, stack=stack)
    return nc



def build_C(marks, emit, n_real=CHUNK, nc=None, stack=None):
    tts = ttiles(n_real)
    p = P(marks, emit)
    t = {}
    if emit:
        dram, sb, psa = nc.dram_tensor, nc.sbuf_tensor, nc.psum_tensor
        t["y_d"] = dram("y", [DIN, n_real], BF16, kind="ExternalInput").ap()
        t["x1_d"] = dram("x1", [H, n_real], F32, kind="ExternalInput").ap()
        t["wop_d"] = dram("wop", [DIN, H], F32, kind="ExternalInput").ap()
        t["qw_d"] = dram("qw", [H, 3072], I32, kind="ExternalInput").ap()
        t["qscale_d"] = dram("qscale", [128, 32], F32, kind="ExternalInput").ap()
        t["qzp_d"] = dram("qzp", [128, 32], F32, kind="ExternalInput").ap()
        t["fbias_d"] = dram("fbias", [128, 32], F32, kind="ExternalInput").ap()
        t["out_o"] = dram("outo", [H, n_real], F32, kind="ExternalOutput").ap()
        t["wt_dram"] = dram("wt_dram", [H, H], BF16, kind="Internal").ap()
        t["wop16"] = dram("wop16", [DIN, H], BF16, kind="Internal").ap()
        t["x2f"] = dram("x2f", [H, n_real], F32, kind="Internal").ap()
        e = stack.enter_context
        t["sqw"] = [e(sb(f"sqw{i}", [128, 3072], I32)) for i in range(2)]
        t["wvi"] = e(sb("wvi", [128, 2048], I32))
        t["wvf"] = e(sb("wvf", [128, 2048], F32))
        t["wnb"] = e(sb("wnb", [128, 4096], BF16))
        t["wtt"] = [e(sb(f"wtt{i}", [128, 128], BF16)) for i in range(4)]
        t["qscale"] = e(sb("qscales", [128, 32], F32))
        t["qzp"] = e(sb("qzps", [128, 32], F32))
        t["hb"] = e(sb("hbs", [128, 32], F32))
        t["onesb"] = e(sb("onesb", [128, 1], BF16))
        t["ones1"] = e(sb("ones1", [1, 128], BF16))
        t["xsrc"] = [e(sb(f"xsrc{i}", [128, 512], F32)) for i in range(2)]
        t["xb"] = e(sb("xbuf", [128, 512], BF16))
        t["sqb"] = e(sb("sqbuf", [128, 512], BF16))
        t["tmp32"] = e(sb("tmp32", [128, 512], F32))
        t["rows"] = e(sb("rows", [1, 1024], F32))
        t["rowb"] = e(sb("rowb", [1, 1024], BF16))
        t["bc32"] = e(sb("bc32", [128, 1024], F32))
        t["wstr"] = [e(sb(f"wstr{i}", [128, 512], BF16)) for i in range(4)]
        t["wcf32"] = [e(sb(f"wcf32{i}", [128, 1024], F32)) for i in range(2)]
        t["wcb16"] = [e(sb(f"wcb16{i}", [128, 1024], BF16)) for i in range(2)]
        t["xnsl"] = e(sb("xnsl", [128, 32 * 512], BF16))
        t["ysl"] = e(sb("ysl", [128, 64 * 512], BF16))
        t["t1"] = e(sb("t1buf", [128, 512], F32))
        t["x1t"] = e(sb("x1t", [128, 512], F32))
        t["ps"] = [e(psa(f"psum{i}", [128, 512], F32)) for i in range(5)]
        t["ps_row"] = e(psa("ps_row", [2, 1024], F32))
        t["ps_e"] = t["ps"][4]
        p.sems = _mk_sems(nc, stack)

    emit_consts(p, t, nc, stack, (EPS_LN, EPS_RMS, 1.0, -0.5, -1.0, 0.5))
    for (name, dst) in (("qscale_d", "qscale"), ("qzp_d", "qzp"),
                        ("fbias_d", "hb")):
        p.do("sync", lambda e, name=name, dst=dst: e.dma_start(
            out=t[dst][:], in_=t[name][:]), inc=("si", 16))
    p.mark("smalls", "si")
    p.do("gpsimd", lambda e: e.memset(t["onesb"][:], 1.0))
    p.do("gpsimd", lambda e: e.memset(t["ones1"][:], 1.0), inc=("g", 1))
    p.mark("ones", "g")
    p.wait("vector", "smalls")
    p.do("vector", lambda e: e.tensor_scalar(
        out=t["hb"][:], in0=t["hb"][:], scalar1=0.5, scalar2=None,
        op0=ALU.mult), inc=("v", 1))
    p.mark("hb", "v")
    emit_dequant(p, t)
    for ci in range(256):
        kt, ct = ci // 4, ci % 4
        pb = ci % 2
        sname = "gw" if pb == 0 else "gw2"
        p.wait("sync", f"wipc_{ci - 2}")
        p.do("sync", lambda e, kt=kt, ct=ct, pb=pb: e.dma_start(
            out=t["wcf32"][pb][:], in_=t["wop_d"][kt * 128:(kt + 1) * 128,
                                                  ct * 1024:(ct + 1) * 1024]),
            inc=(sname, 16))
        p.mark(f"wipl_{ci}", sname)
        p.wait("vector", f"wipl_{ci}")
        p.wait("vector", f"wipst_{ci - 2}")
        p.do("vector", lambda e, pb=pb: e.tensor_copy(
            out=t["wcb16"][pb][:], in_=t["wcf32"][pb][:]), inc=("v", 1))
        p.mark(f"wipc_{ci}", "v")
        p.wait("scalar", f"wipc_{ci}")
        p.do("scalar", lambda e, kt=kt, ct=ct, pb=pb: e.dma_start(
            out=t["wop16"][kt * 128:(kt + 1) * 128, ct * 1024:(ct + 1) * 1024],
            in_=t["wcb16"][pb][:]), inc=(f"gb{pb}", 16))
        p.mark(f"wipst_{ci}", f"gb{pb}")
    p.mark("wop16", "gb0")
    p.mark("wop16b", "gb1")

    p.wait("tensor", "ones")
    p.wait("vector", "ones")
    for ti, (tt0, w) in enumerate(tts):
        pre = f"T{ti}."
        p.wait("sync", f"T{ti - 1}.opdone")
        for kt in range(64):
            p.do("sync", lambda e, kt=kt, tt0=tt0, w=w: e.dma_start(
                out=t["ysl"][:, kt * 512:kt * 512 + w],
                in_=t["y_d"][kt * 128:(kt + 1) * 128, tt0:tt0 + w]),
                inc=("sy", 16))
        p.mark(f"{pre}yld", "sy")
        for hq in range(8):
            for k in range(64):
                pb = k % 4
                qe = "sync" if k % 2 == 0 else "scalar"
                if p.lastw.get(pb):
                    p.wait(qe, p.lastw[pb])
                if ti == 0 and hq == 0 and k < 4:
                    p.wait(qe, "wop16")
                    p.wait(qe, "wop16b")
                p.do(qe, lambda e, hq=hq, k=k, pb=pb: e.dma_start(
                    out=t["wstr"][pb][:], in_=t["wop16"][k * 128:(k + 1) * 128,
                                                         hq * 512:(hq + 1) * 512]),
                    inc=(f"sw{pb}", 16))
                p.mark(f"{pre}op.w_{hq}_{k}", f"sw{pb}")
                p.wait("tensor", f"{pre}op.w_{hq}_{k}")
                if k == 0:
                    p.wait("tensor", f"{pre}yld")
                    if ti == 0 and hq == 0:
                        p.wait("tensor", "ones")
                    for ns in range(4):
                        p.wait("tensor", f"{pre}op.free_{hq - 1}_{ns}")
                for ns in range(4):
                    p.do("tensor", lambda e, k=k, ns=ns, pb=pb, w=w: e.matmul(
                        t["ps"][ns][:, :w], t["wstr"][pb][:, ns * 128:(ns + 1) * 128],
                        t["ysl"][:, k * 512:k * 512 + w],
                        start=(k == 0), stop=(k == 63)),
                        inc=("tn", 1) if ns == 3 else None)
                p.mark(f"{pre}op.c_{hq}_{k}", "tn")
                p.lastw[pb] = f"{pre}op.c_{hq}_{k}"
            p.mark(f"{pre}op.g_{hq}", "tn")
            for ns in range(4):
                ev = hq * 4 + ns
                p.wait("sync", f"{pre}op.x2_{ev - 1}")
                p.do("sync", lambda e, ev=ev, tt0=tt0, w=w: e.dma_start(
                    out=t["xsrc"][0][:, :w], in_=t["x1_d"][ev * 128:(ev + 1) * 128,
                                                           tt0:tt0 + w]),
                    inc=("x0", 16))
                p.mark(f"{pre}op.xl_{ev}", "x0")
                p.wait("vector", f"{pre}op.g_{hq}")
                p.wait("vector", f"{pre}op.xl_{ev}")
                p.wait("vector", f"{pre}op.x2d_{ev - 1}")
                p.do("vector", lambda e, ns=ns, w=w: e.tensor_add(
                    out=t["x1t"][:, :w], in0=t["ps"][ns][:, :w],
                    in1=t["xsrc"][0][:, :w]), inc=("v", 1))
                p.mark(f"{pre}op.free_{hq}_{ns}", "v")
                p.mark(f"{pre}op.x2_{ev}", "v")
                p.wait("sync", f"{pre}op.x2_{ev}")
                p.do("sync", lambda e, ev=ev, tt0=tt0, w=w: e.dma_start(
                    out=t["x2f"][ev * 128:(ev + 1) * 128, tt0:tt0 + w],
                    in_=t["x1t"][:, :w]), inc=("sx1", 16))
                p.mark(f"{pre}op.x2d_{ev}", "sx1")
                p.mark(f"{pre}op.x2dk_{ev}", "sx1")
        p.mark(f"{pre}opdone", "tn")

        emit_ln(p, t, pre + "ln3.", w, 32,
                lambda kt, tt0=tt0, w=w: t["x2f"][kt * 128:(kt + 1) * 128,
                                                  tt0:tt0 + w],
                lambda kt, w=w: t["xnsl"][:, kt * 512:kt * 512 + w],
                f"T{ti - 1}.f2.mmdone",
                f"T{ti - 1}.ln3.var", f"T{ti - 1}.ln3.bce1",
                src_guard_fn=lambda kt, pre=pre: f"{pre}op.x2dk_{kt}")

        def evac_ffn2(p, nq, ns, ev, pre=pre, tt0=tt0, w=w):
            n = ev
            p.wait("vector", f"{pre}f2.g_{nq}")
            p.wait("vector", f"{pre}f2.od_{ev - 1}")
            p.do("vector", lambda e, ns=ns, n=n, w=w: e.tensor_scalar(
                out=t["t1"][:, :w], in0=t["ps"][ns][:, :w],
                scalar1=0.5, scalar2=t["hb"][:, n:n + 1],
                op0=ALU.mult, op1=ALU.add), inc=("v", 1))
            p.mark(f"{pre}f2.free_{nq}_{ns}", "v")
            if ev == 0:
                p.wait("sync", f"{pre}ln3.done")
            p.wait("sync", f"{pre}f2.o_{ev - 1}")
            p.do("sync", lambda e, n=n, tt0=tt0, w=w: e.dma_start(
                out=t["xsrc"][0][:, :w], in_=t["x2f"][n * 128:(n + 1) * 128,
                                                      tt0:tt0 + w]),
                inc=("x0", 16))
            p.mark(f"{pre}f2.xl_{ev}", "x0")
            p.wait("vector", f"{pre}f2.xl_{ev}")
            p.wait_self("vector", "v")
            p.do("vector", lambda e, w=w: e.tensor_add(
                out=t["x1t"][:, :w], in0=t["t1"][:, :w], in1=t["xsrc"][0][:, :w]),
                inc=("v", 1))
            p.mark(f"{pre}f2.o_{ev}", "v")
            p.wait("sync", f"{pre}f2.o_{ev}")
            p.do("sync", lambda e, n=n, tt0=tt0, w=w: e.dma_start(
                out=t["out_o"][n * 128:(n + 1) * 128, tt0:tt0 + w],
                in_=t["x1t"][:, :w]), inc=("so", 16))
            p.mark(f"{pre}f2.od_{ev}", "so")

        if ti == 0:
            for wb in range(4):
                p.wait("tensor", f"dq_done{wb}")
            p.wait("tensor", "hb")
        emit_ffn(p, t, pre + "f2.", w,
                 lambda k, nq: t["wt_dram"][k * 128:(k + 1) * 128,
                                            nq * 512:(nq + 1) * 512],
                 evac_ffn2, f"{pre}ln3.done", (),
                 tuple(f"dq_done{i}" for i in range(4)) if ti == 0 else ())

    finish(nc, p, stack)
    return p


def make_C(n_real=CHUNK):
    marks = {}
    build_C(marks, emit=False, n_real=n_real)
    nc = bass.Bass("TRN2", target_bir_lowering=False)
    with ExitStack() as stack:
        build_C(marks, emit=True, n_real=n_real, nc=nc, stack=stack)
    return nc



def _col32(v):
    return np.ascontiguousarray(v.reshape(-1, 128).T)


def prep_host(inp):
    g = lambda k: np.asarray(inp[k])
    d = {}
    x0 = g("L0_attn_hidden_states")
    d["xt"] = np.ascontiguousarray(x0.reshape(2 * 8192, H).T)
    d["wip"] = np.ascontiguousarray(g("L0_attn_in_proj_w"))
    d["qw"] = np.ascontiguousarray(g("L0_ffn_qweight"))
    d["qscale"] = _col32(g("L0_ffn_weight_scale").reshape(-1))
    d["qzp"] = _col32(g("L0_ffn_weight_zero_point").reshape(-1))
    d["fbias"] = _col32(g("L0_ffn_bias"))
    d["xprojT"] = np.ascontiguousarray(g("L0_attn_x_proj_w").T)
    d["dtprojT"] = np.ascontiguousarray(g("L0_attn_dt_proj_w").T)
    d["dtbias"] = np.ascontiguousarray(g("L0_attn_dt_bias").reshape(64, 128).T)
    d["dtlnw"] = np.ascontiguousarray(g("L0_attn_dt_ln_w").reshape(2, 128).T)
    d["bclnw"] = np.concatenate([g("L0_attn_B_ln_w"),
                                 g("L0_attn_C_ln_w")]).reshape(32, 1)
    d["convw"] = np.ascontiguousarray(
        g("L0_attn_conv_w").reshape(64, 128, 4).transpose(1, 0, 2).reshape(128, 256))
    d["convb"] = np.ascontiguousarray(g("L0_attn_conv_b").reshape(64, 128).T)
    d["wopT"] = np.ascontiguousarray(g("L0_attn_out_proj_w").T)
    d["al"] = g("L0_attn_A_log")
    d["dp"] = g("L0_attn_D_param")
    return d


def a_inmap(d, c):
    lo = c * CHUNK
    if c % 4 == 0:
        xc = np.concatenate([np.zeros((H, 3), np.float32),
                             d["xt"][:, lo:lo + CHUNK]], axis=1)
    else:
        xc = np.ascontiguousarray(d["xt"][:, lo - 3:lo + CHUNK])
    bpat = np.zeros((32, 2), np.float32)
    bpat[0:16, 0] = 1.0
    bpat[16:32, 1] = 1.0
    hmask = np.ones((128, 4), np.float32)
    if c % 4 == 0:
        hmask[:, 0:3] = 0.0
    return dict(x=xc, hmask=hmask, wip=d["wip"], qw=d["qw"], qscale=d["qscale"],
                qzp=d["qzp"], fbias=d["fbias"], xprojT=d["xprojT"],
                dtprojT=d["dtprojT"], dtbias=d["dtbias"], dtlnw=d["dtlnw"],
                bclnw=d["bclnw"], convw=d["convw"], convb=d["convb"],
                bpat_i=bpat, bpat2_i=np.ascontiguousarray(bpat.T))


def b_inmap(d, ab, j):
    sl = slice(j * 1024, (j + 1) * 1024)
    return dict(dt=np.ascontiguousarray(ab["dt"][sl]),
                xc=np.ascontiguousarray(ab["xc"][sl]),
                sz=np.ascontiguousarray(ab["sz"][sl]),
                bc=ab["bc"],
                al=np.ascontiguousarray(d["al"][sl].reshape(8, 128, 16)),
                dp=np.ascontiguousarray(d["dp"][sl].reshape(8, 128).T))


def c_inmap(d, ab, y_all, c):
    cs = slice(c * CHUNK, (c + 1) * CHUNK)
    return dict(y=np.ascontiguousarray(y_all[:, cs]),
                x1=np.ascontiguousarray(ab["x1"][:, cs]),
                wop=d["wopT"], qw=d["qw"], qscale=d["qscale"],
                qzp=d["qzp"], fbias=d["fbias"])


def kernel(**inp):
    d = prep_host(inp)
    ncA = make_A()
    resA = run_bass_kernel_spmd(ncA, [a_inmap(d, c) for c in range(NCORE)],
                                core_ids=list(range(NCORE)))
    ab = dict(
        x1=np.concatenate([r["x1o"] for r in resA.results], axis=1),
        dt=np.concatenate([r["dto"] for r in resA.results], axis=1),
        xc=np.concatenate([r["xco"] for r in resA.results], axis=1),
        sz=np.concatenate([r["szo"] for r in resA.results], axis=1),
        bc=np.concatenate([r["bco"] for r in resA.results], axis=1),
    )
    ncB = make_B()
    resB = run_bass_kernel_spmd(ncB, [b_inmap(d, ab, j) for j in range(NCORE)],
                                core_ids=list(range(NCORE)))
    y_all = np.concatenate([r["yo"] for r in resB.results], axis=0)
    ncC = make_C()
    resC = run_bass_kernel_spmd(ncC, [c_inmap(d, ab, y_all, c)
                                      for c in range(NCORE)],
                                core_ids=list(range(NCORE)))
    out = np.concatenate([r["outo"] for r in resC.results], axis=1)
    return np.ascontiguousarray(out.T).reshape(2, 8192, H).astype(np.float32)


# version 06 (final)
# speedup vs baseline: 1.1204x; 1.0105x over previous
import numpy as np
import ml_dtypes
import sys
if '/opt/trn_rl_repo' not in sys.path:
    sys.path.insert(0, '/opt/trn_rl_repo')
from contextlib import ExitStack
import concourse.bass as bass
import concourse.mybir as mybir
from concourse.bass_utils import run_bass_kernel_spmd

F32 = mybir.dt.float32
BF16 = mybir.dt.bfloat16
I32 = mybir.dt.int32
AF = mybir.ActivationFunctionType
ALU = mybir.AluOpType
BF = ml_dtypes.bfloat16

H = 4096
DIN = 8192
NCORE = 8
CHUNK = 2048
EPS_LN = 1e-5
EPS_RMS = 1e-6

ENGS = ("sync", "gpsimd", "tensor", "vector", "scalar")
SEMS = ("si", "gj", "gw", "gw2", "q0", "q1",
        "t0", "t1", "t2", "t3", "u0", "u1", "u2", "u3",
        "x0", "x1", "sw0", "sw1", "sw2", "sw3", "sx1", "sxc", "sdt",
        "sbc", "sy", "so", "b0", "b1", "g0", "g1", "gb0", "gb1",
        "g", "tn", "v", "a")


class P:
    def __init__(self, marks, emit):
        self.marks = marks
        self.emit = emit
        self.sems = None
        self.cnt = {s: 0 for s in SEMS}
        self.streams = {e: [] for e in ENGS}
        self.lastw = {}

    def do(self, eng, fn, inc=None):
        incs = [inc] if isinstance(inc, tuple) else (inc or [])
        for (s, k) in incs:
            self.cnt[s] += k
        if self.emit:
            def run(e, fn=fn, incs=tuple(incs)):
                ins = fn(e)
                for (s, k) in incs:
                    ins.then_inc(self.sems[s], k)
            self.streams[eng].append(run)

    def mark(self, tag, sem):
        self.marks[tag] = (sem, self.cnt[sem])

    def wait(self, eng, tag):
        if self.emit and tag in self.marks:
            sem, val = self.marks[tag]
            if val > 0:
                self.streams[eng].append(
                    lambda e, s=sem, v=val: e.wait_ge(self.sems[s], v))

    def wait_val(self, eng, sem, val):
        if self.emit and val > 0:
            self.streams[eng].append(
                lambda e, s=sem, v=val: e.wait_ge(self.sems[s], v))

    def wait_self(self, eng, sem):
        self.wait_val(eng, sem, self.cnt[sem])

    def val(self, sem):
        return self.cnt[sem]


def ttiles(n):
    out, o = [], 0
    while o < n:
        w = min(512, n - o)
        out.append((o, w))
        o += w
    return out


def _mk_sems(nc, stack):
    return {n: stack.enter_context(nc.semaphore(n)) for n in SEMS}


def finish(nc, p, stack):
    for s in SEMS:
        if s not in ("g", "tn", "v", "a"):
            p.wait_val("gpsimd", s, p.val(s))
    if p.emit:
        block = stack.enter_context(nc.Block())
        for eng in ENGS:
            ops = p.streams[eng]
            if ops:
                getattr(block, eng)(lambda e, ops=tuple(ops): [f(e) for f in ops])



def emit_consts(p, t, nc, stack, values):
    if p.emit:
        cst = stack.enter_context(nc.sbuf_tensor("cst", [128, len(values)], F32))
        t["cst"] = cst
        for i, v in enumerate(values):
            nc.const_aps.register(F32, v, cst[:, i:i + 1])
    for i, v in enumerate(values):
        p.do("gpsimd", lambda e, i=i, v=v: e.memset(t["cst"][:, i:i + 1], v),
             inc=("g", 1) if i == len(values) - 1 else None)
    p.mark("csts", "g")
    p.wait("scalar", "csts")
    p.wait("vector", "csts")



def emit_dequant(p, t):
    for nt in range(32):
        pb = nt % 2
        p.wait("sync", f"dq_unp_{nt - 2}")
        p.do("sync", lambda e, nt=nt, pb=pb: e.dma_start(
            out=t["sqw"][pb][:], in_=t["qw_d"][nt * 128:(nt + 1) * 128, :]),
            inc=(f"q{pb}", 16))
        p.mark(f"dq_ld_{nt}", f"q{pb}")
        p.wait("vector", f"dq_ld_{nt}")
        for wb in range(4):
            p.wait("vector", f"dq_u{wb}_{nt - 1}")
        for hh in range(2):
            for (j, sj, op, sc) in ((0, 0, ALU.bitwise_and, 63),
                                    (1, 0, ALU.logical_shift_right, 6),
                                    (2, 1, ALU.bitwise_and, 63),
                                    (3, 2, ALU.bitwise_and, 63)):
                p.do("vector", lambda e, pb=pb, j=j, sj=sj, op=op, sc=sc, hh=hh:
                     e.tensor_single_scalar(
                         out=t["wvi"][:].rearrange("q (g f) -> q g f", f=4)[:, :, j],
                         in_=t["sqw"][pb][:, hh * 1536:(hh + 1) * 1536]
                         .rearrange("q (g f) -> q g f", f=3)[:, :, sj],
                         scalar=sc, op=op), inc=("v", 1))
            p.wait_self("vector", "v")
            p.do("vector", lambda e: e.tensor_copy(out=t["wvf"][:], in_=t["wvi"][:]),
                 inc=("v", 1))
            p.wait_self("vector", "v")
            p.do("vector", lambda e, nt=nt, hh=hh: e.tensor_scalar(
                out=t["wnb"][:, hh * 2048:(hh + 1) * 2048], in0=t["wvf"][:],
                scalar1=t["qzp"][:, nt:nt + 1], scalar2=t["qscale"][:, nt:nt + 1],
                op0=ALU.subtract, op1=ALU.mult), inc=("v", 1))
        p.mark(f"dq_unp_{nt}", "v")
        p.mark(f"dq_w_{nt}", "v")
        p.wait("sync", f"dq_w_{nt}")
        for kb in range(32):
            si = nt * 32 + kb
            wb = si % 4
            p.wait_val("sync", f"u{wb}", p.val(f"u{wb}"))
            p.do("sync", lambda e, kb=kb, wb=wb: e.dma_start_transpose(
                out=t["wtt"][wb][:], in_=t["wnb"][:, kb * 128:(kb + 1) * 128]),
                inc=(f"t{wb}", 16))
            p.wait_val("scalar", f"t{wb}", p.val(f"t{wb}"))
            p.do("scalar", lambda e, kb=kb, nt=nt, wb=wb: e.dma_start(
                out=t["wt_dram"][kb * 128:(kb + 1) * 128, nt * 128:(nt + 1) * 128],
                in_=t["wtt"][wb][:]), inc=(f"u{wb}", 16))
        for wb in range(4):
            p.mark(f"dq_u{wb}_{nt}", f"u{wb}")
    for wb in range(4):
        p.mark(f"dq_done{wb}", f"u{wb}")



def emit_ln(p, t, pre, W, n_kt, src_fn, out_fn, norm_guard, row_guard,
            e_guard, src_guard_fn=None):
    for kt in range(n_kt):
        pb = kt % 2
        p.wait("sync", f"{pre}mm_{kt - 2}")
        if src_guard_fn is not None:
            p.wait("sync", src_guard_fn(kt))
        p.do("sync", lambda e, kt=kt, pb=pb, W=W: e.dma_start(
            out=t["xsrc"][pb][:, :W], in_=src_fn(kt)), inc=(f"x{pb}", 16))
        p.mark(f"{pre}ld_{kt}", f"x{pb}")
        p.wait("scalar", f"{pre}ld_{kt}")
        p.wait("scalar", f"{pre}mm_{kt - 1}")
        p.do("scalar", lambda e, pb=pb, W=W: e.activation(
            out=t["xb"][:, :W], in_=t["xsrc"][pb][:, :W], func=AF.Copy),
            inc=("a", 1))
        p.mark(f"{pre}xb_{kt}", "a")
        p.wait("vector", f"{pre}xb_{kt}")
        p.wait("vector", f"{pre}mm_{kt - 1}")
        p.do("vector", lambda e, W=W: e.tensor_mul(
            out=t["sqb"][:, :W], in0=t["xb"][:, :W], in1=t["xb"][:, :W]),
            inc=("v", 1))
        p.mark(f"{pre}sq_{kt}", "v")
        if kt == 0:
            p.wait("tensor", row_guard)
        p.wait("tensor", f"{pre}xb_{kt}")
        p.wait("tensor", f"{pre}sq_{kt}")
        p.do("tensor", lambda e, kt=kt, W=W: e.matmul(
            t["ps_row"][0:1, 0:W], t["onesb"][:], t["xb"][:, :W],
            start=(kt == 0), stop=(kt == n_kt - 1)))
        p.do("tensor", lambda e, kt=kt, W=W: e.matmul(
            t["ps_row"][0:1, 512:512 + W], t["onesb"][:], t["sqb"][:, :W],
            start=(kt == 0), stop=(kt == n_kt - 1)), inc=("tn", 1))
        p.mark(f"{pre}mm_{kt}", "tn")
    p.mark(f"{pre}stats", "tn")
    p.wait("vector", f"{pre}stats")
    p.do("vector", lambda e, W=W: e.tensor_scalar(
        out=t["rows"][0:1, :W], in0=t["ps_row"][0:1, 0:W],
        scalar1=1.0 / (128 * n_kt), scalar2=None, op0=ALU.mult), inc=("v", 1))
    p.wait_self("vector", "v")
    p.do("vector", lambda e, W=W: e.tensor_mul(
        out=t["rows"][0:1, 512:512 + W], in0=t["rows"][0:1, :W],
        in1=t["rows"][0:1, :W]), inc=("v", 1))
    p.wait_self("vector", "v")
    p.do("vector", lambda e, W=W: e.scalar_tensor_tensor(
        out=t["rows"][0:1, 512:512 + W], in0=t["ps_row"][0:1, 512:512 + W],
        scalar=1.0 / (128 * n_kt), in1=t["rows"][0:1, 512:512 + W],
        op0=ALU.mult, op1=ALU.subtract), inc=("v", 1))
    p.mark(f"{pre}var", "v")
    p.wait("scalar", f"{pre}var")
    p.do("scalar", lambda e, W=W: e.activation(
        out=t["rows"][0:1, 512:512 + W], in_=t["rows"][0:1, 512:512 + W],
        func=AF.Ln, bias=EPS_LN), inc=("a", 1))
    p.wait_self("scalar", "a")
    p.do("scalar", lambda e, W=W: e.activation(
        out=t["rows"][0:1, 512:512 + W], in_=t["rows"][0:1, 512:512 + W],
        func=AF.Exp, scale=-0.5), inc=("a", 1))
    p.mark(f"{pre}rstd", "a")
    p.wait("vector", f"{pre}rstd")
    p.do("vector", lambda e, W=W: e.tensor_mul(
        out=t["rows"][0:1, :W], in0=t["rows"][0:1, :W],
        in1=t["rows"][0:1, 512:512 + W]), inc=("v", 1))
    p.wait_self("vector", "v")
    p.do("vector", lambda e, W=W: e.tensor_copy(
        out=t["rowb"][0:1, 0:W], in_=t["rows"][0:1, 512:512 + W]))
    p.do("vector", lambda e, W=W: e.tensor_copy(
        out=t["rowb"][0:1, 512:512 + W], in_=t["rows"][0:1, :W]), inc=("v", 1))
    p.mark(f"{pre}rowb", "v")
    p.wait("tensor", f"{pre}rowb")
    p.wait("tensor", e_guard)
    p.do("tensor", lambda e, W=W: e.matmul(
        t["ps_e"][:, 0:W], t["ones1"][:], t["rowb"][0:1, 0:W],
        start=True, stop=True), inc=("tn", 1))
    p.mark(f"{pre}bc0", "tn")
    p.wait("scalar", f"{pre}bc0")
    p.do("scalar", lambda e, W=W: e.activation(
        out=t["bc32"][:, 0:W], in_=t["ps_e"][:, 0:W], func=AF.Copy), inc=("a", 1))
    p.mark(f"{pre}bce0", "a")
    p.wait("tensor", f"{pre}bce0")
    p.do("tensor", lambda e, W=W: e.matmul(
        t["ps_e"][:, 0:W], t["ones1"][:], t["rowb"][0:1, 512:512 + W],
        start=True, stop=True), inc=("tn", 1))
    p.mark(f"{pre}bc1", "tn")
    p.wait("scalar", f"{pre}bc1")
    p.do("scalar", lambda e, W=W: e.activation(
        out=t["bc32"][:, 512:512 + W], in_=t["ps_e"][:, 0:W], func=AF.Copy),
        inc=("a", 1))
    p.mark(f"{pre}bce1", "a")
    for kt in range(n_kt):
        pb = kt % 2
        p.wait("sync", f"{pre}n_{kt - 2}")
        if kt < 2:
            p.wait("sync", f"{pre}xb_{n_kt - 1}")
        if src_guard_fn is not None:
            p.wait("sync", src_guard_fn(kt))
        p.do("sync", lambda e, kt=kt, pb=pb, W=W: e.dma_start(
            out=t["xsrc"][pb][:, :W], in_=src_fn(kt)), inc=(f"x{pb}", 16))
        p.mark(f"{pre}l2_{kt}", f"x{pb}")
        p.wait("vector", f"{pre}l2_{kt}")
        p.wait("vector", f"{pre}bce1")
        if kt == 0 and norm_guard:
            p.wait("vector", norm_guard)
        p.wait("vector", f"{pre}n_{kt - 1}")
        p.do("vector", lambda e, pb=pb, W=W: e.tensor_mul(
            out=t["tmp32"][:, :W], in0=t["xsrc"][pb][:, :W], in1=t["bc32"][:, 0:W]),
            inc=("v", 1))
        p.wait_self("vector", "v")
        p.do("vector", lambda e, kt=kt, W=W: e.tensor_sub(
            out=out_fn(kt), in0=t["tmp32"][:, :W], in1=t["bc32"][:, 512:512 + W]),
            inc=("v", 1))
        p.mark(f"{pre}n_{kt}", "v")
    p.mark(f"{pre}done", "v")



def emit_ffn(p, t, pre, W, wdram, evac_fn, rhs_guard, war_guards=(),
             wt_guards=()):
    for nq in range(8):
        for k in range(32):
            pb = k % 4
            qe = "sync" if k % 2 == 0 else "scalar"
            if p.lastw.get(pb):
                p.wait(qe, p.lastw[pb])
            if nq == 0 and k < 4:
                for wg in wt_guards:
                    p.wait(qe, wg)
            p.do(qe, lambda e, nq=nq, k=k, pb=pb: e.dma_start(
                out=t["wstr"][pb][:], in_=wdram(k, nq)), inc=(f"sw{pb}", 16))
            p.mark(f"{pre}w_{nq}_{k}", f"sw{pb}")
            p.wait("tensor", f"{pre}w_{nq}_{k}")
            if k == 0:
                p.wait("tensor", rhs_guard)
                if nq == 0:
                    for wg in war_guards:
                        p.wait("tensor", wg)
                for ns in range(4):
                    p.wait("tensor", f"{pre}free_{nq - 1}_{ns}")
            for ns in range(4):
                p.do("tensor", lambda e, k=k, ns=ns, pb=pb, W=W: e.matmul(
                    t["ps"][ns][:, :W], t["wstr"][pb][:, ns * 128:(ns + 1) * 128],
                    t["xnsl"][:, k * 512:k * 512 + W],
                    start=(k == 0), stop=(k == 31)),
                    inc=("tn", 1) if ns == 3 else None)
            p.mark(f"{pre}c_{nq}_{k}", "tn")
            p.lastw[pb] = f"{pre}c_{nq}_{k}"
        p.mark(f"{pre}g_{nq}", "tn")
        for ns in range(4):
            evac_fn(p, nq, ns, nq * 4 + ns)
    p.mark(f"{pre}mmdone", "tn")



def build_A(marks, emit, n_real=CHUNK, nc=None, stack=None):
    n_tok = n_real + 3
    tts = ttiles(n_tok)
    p = P(marks, emit)
    t = {}
    if emit:
        dram, sb, psa = nc.dram_tensor, nc.sbuf_tensor, nc.psum_tensor
        t["x_d"] = dram("x", [H, n_tok], F32, kind="ExternalInput").ap()
        t["wip_d"] = dram("wip", [H, 2 * DIN], F32, kind="ExternalInput").ap()
        t["qw_d"] = dram("qw", [H, 3072], I32, kind="ExternalInput").ap()
        t["qscale_d"] = dram("qscale", [128, 32], F32, kind="ExternalInput").ap()
        t["qzp_d"] = dram("qzp", [128, 32], F32, kind="ExternalInput").ap()
        t["fbias_d"] = dram("fbias", [128, 32], F32, kind="ExternalInput").ap()
        t["xprojT_d"] = dram("xprojT", [DIN, 288], F32, kind="ExternalInput").ap()
        t["dtprojT_d"] = dram("dtprojT", [256, DIN], F32, kind="ExternalInput").ap()
        t["dtbias_d"] = dram("dtbias", [128, 64], F32, kind="ExternalInput").ap()
        t["dtlnw_d"] = dram("dtlnw", [128, 2], F32, kind="ExternalInput").ap()
        t["bclnw_d"] = dram("bclnw", [32, 1], F32, kind="ExternalInput").ap()
        t["convw_d"] = dram("convw", [128, 256], F32, kind="ExternalInput").ap()
        t["convb_d"] = dram("convb", [128, 64], F32, kind="ExternalInput").ap()
        t["bpat_d"] = dram("bpat_i", [32, 2], F32, kind="ExternalInput").ap()
        t["hmask_d"] = dram("hmask", [128, 4], F32, kind="ExternalInput").ap()
        t["bpat2_d"] = dram("bpat2_i", [2, 32], F32, kind="ExternalInput").ap()
        t["x1_o"] = dram("x1o", [H, n_real], F32, kind="ExternalOutput").ap()
        t["dt_o"] = dram("dto", [DIN, n_real], F32, kind="ExternalOutput").ap()
        t["xc_o"] = dram("xco", [DIN, n_real], BF16, kind="ExternalOutput").ap()
        t["sz_o"] = dram("szo", [DIN, n_real], BF16, kind="ExternalOutput").ap()
        t["bc_o"] = dram("bco", [32, n_real], BF16, kind="ExternalOutput").ap()
        t["wt_dram"] = dram("wt_dram", [H, H], BF16, kind="Internal").ap()
        t["wip16"] = dram("wip16", [H, 2 * DIN], BF16, kind="Internal").ap()
        t["x1f"] = dram("x1f", [H, n_tok], F32, kind="Internal").ap()
        e = stack.enter_context
        t["sqw"] = [e(sb(f"sqw{i}", [128, 3072], I32)) for i in range(2)]
        t["wvi"] = e(sb("wvi", [128, 2048], I32))
        t["wvf"] = e(sb("wvf", [128, 2048], F32))
        t["wnb"] = e(sb("wnb", [128, 4096], BF16))
        t["wtt"] = [e(sb(f"wtt{i}", [128, 128], BF16)) for i in range(4)]
        t["qscale"] = e(sb("qscales", [128, 32], F32))
        t["qzp"] = e(sb("qzps", [128, 32], F32))
        t["hb"] = e(sb("hbs", [128, 32], F32))
        t["onesb"] = e(sb("onesb", [128, 1], BF16))
        t["ones1"] = e(sb("ones1", [1, 128], BF16))
        t["bpat"] = e(sb("bpat", [32, 2], BF16))
        t["hmask"] = e(sb("hmask_s", [128, 4], F32))
        t["bpat2"] = e(sb("bpat2", [2, 32], BF16))
        t["xsrc"] = [e(sb(f"xsrc{i}", [128, 512], F32)) for i in range(2)]
        t["xb"] = e(sb("xbuf", [128, 512], BF16))
        t["sqb"] = e(sb("sqbuf", [128, 512], BF16))
        t["tmp32"] = e(sb("tmp32", [128, 512], F32))
        t["rows"] = e(sb("rows", [1, 1024], F32))
        t["rows2"] = e(sb("rows2", [2, 512], F32))
        t["rowb"] = e(sb("rowb", [1, 1024], BF16))
        t["rows2b"] = e(sb("rows2b", [2, 512], BF16))
        t["bc32"] = e(sb("bc32", [128, 1024], F32))
        t["wstr"] = [e(sb(f"wstr{i}", [128, 512], BF16)) for i in range(4)]
        t["wcf32"] = [e(sb(f"wcf32{i}", [128, 1024], F32)) for i in range(2)]
        t["wcb16"] = [e(sb(f"wcb16{i}", [128, 1024], BF16)) for i in range(2)]
        t["xnsl"] = e(sb("xnsl", [128, 32 * 512], BF16))
        t["t1"] = e(sb("t1buf", [128, 512], F32))
        t["x1t"] = e(sb("x1t", [128, 512], F32))
        t["xtb"] = e(sb("xtb", [128, 515], F32))
        t["tails"] = e(sb("tails", [128, 64 * 3], F32))
        t["accs"] = [e(sb(f"accs{i}", [128, 512], F32)) for i in range(2)]
        t["xcb"] = e(sb("xcbuf", [128, 512], BF16))
        t["ljar"] = e(sb("ljar", [128, 64 * 288], BF16))
        t["xpj"] = t["ljar"]
        t["convw"] = e(sb("convw_s", [128, 256], F32))
        t["convb"] = e(sb("convb_s", [128, 64], F32))
        t["ssm0"] = e(sb("ssm0", [128, n_real], F32))
        t["ssm1"] = e(sb("ssm1", [128, n_real], F32))
        t["ssm2"] = e(sb("ssm2", [32, n_real], F32))
        t["dtln"] = e(sb("dtln", [128, 2], F32))
        t["bcln"] = e(sb("bcln", [32, 1], F32))
        t["dtrn"] = e(sb("dtrn", [128, 2 * 512], BF16))
        t["bcb16"] = e(sb("bcb16", [32, 512], BF16))
        t["dtpj"] = t["ljar"]
        t["dtbias"] = e(sb("dtbias_s", [128, 64], F32))
        t["sps"] = [e(sb(f"sps{i}", [128, 1024], F32)) for i in range(2)]
        t["spr"] = e(sb("spr", [128, 1024], F32))
        t["ps"] = [e(psa(f"psum{i}", [128, 512], F32)) for i in range(5)]
        t["ps_x2"] = e(psa("ps_x2", [32, 512], F32))
        t["ps_row"] = e(psa("ps_row", [2, 1024], F32))
        t["ps_e"] = t["ps"][4]
        p.sems = _mk_sems(nc, stack)

    emit_consts(p, t, nc, stack, (EPS_LN, EPS_RMS, 1.0, -0.5, -1.0, 0.5))
    for (name, dst) in (("qscale_d", "qscale"), ("qzp_d", "qzp"),
                        ("fbias_d", "hb"), ("dtbias_d", "dtbias"),
                        ("dtlnw_d", "dtln"), ("bclnw_d", "bcln"),
                        ("convw_d", "convw"), ("convb_d", "convb")):
        p.do("sync", lambda e, name=name, dst=dst: e.dma_start(
            out=t[dst][:], in_=t[name][:]), inc=("si", 16))
    p.mark("smalls", "si")
    p.do("gpsimd", lambda e: e.dma_start(
        out=t["xpj"][:].rearrange("q (c r) -> q c r", r=288),
        in_=t["xprojT_d"][:].rearrange("(c q) r -> q c r", q=128)), inc=("gj", 16))
    p.do("gpsimd", lambda e: e.memset(t["onesb"][:], 1.0))
    p.do("gpsimd", lambda e: e.memset(t["ones1"][:], 1.0), inc=("g", 1))
    p.do("gpsimd", lambda e: e.dma_start(out=t["bpat"][:], in_=t["bpat_d"][:]),
         inc=("gj", 16))
    p.do("gpsimd", lambda e: e.dma_start(out=t["bpat2"][:], in_=t["bpat2_d"][:]),
         inc=("gj", 16))
    p.do("gpsimd", lambda e: e.dma_start(out=t["hmask"][:], in_=t["hmask_d"][:]),
         inc=("gj", 16))
    p.mark("hmaskld", "gj")
    p.mark("ones", "g")
    p.mark("ljt", "gj")
    p.mark("bpats", "gj")
    p.wait("vector", "smalls")
    p.do("vector", lambda e: e.tensor_scalar(
        out=t["hb"][:], in0=t["hb"][:], scalar1=0.5, scalar2=None,
        op0=ALU.mult), inc=("v", 1))
    p.mark("hb", "v")
    emit_dequant(p, t)
    for ci in range(512):
        kt, ct = ci // 16, ci % 16
        pb = ci % 2
        sname = "gw" if pb == 0 else "gw2"
        p.wait("sync", f"wipc_{ci - 2}")
        p.do("sync", lambda e, kt=kt, ct=ct, pb=pb: e.dma_start(
            out=t["wcf32"][pb][:], in_=t["wip_d"][kt * 128:(kt + 1) * 128,
                                                  ct * 1024:(ct + 1) * 1024]),
            inc=(sname, 16))
        p.mark(f"wipl_{ci}", sname)
        p.wait("vector", f"wipl_{ci}")
        p.wait("vector", f"wipst_{ci - 2}")
        p.do("vector", lambda e, pb=pb: e.tensor_copy(
            out=t["wcb16"][pb][:], in_=t["wcf32"][pb][:]), inc=("v", 1))
        p.mark(f"wipc_{ci}", "v")
        p.wait("scalar", f"wipc_{ci}")
        p.do("scalar", lambda e, kt=kt, ct=ct, pb=pb: e.dma_start(
            out=t["wip16"][kt * 128:(kt + 1) * 128, ct * 1024:(ct + 1) * 1024],
            in_=t["wcb16"][pb][:]), inc=(f"gb{pb}", 16))
        p.mark(f"wipst_{ci}", f"gb{pb}")
    p.mark("wip16", "gb0")
    p.mark("wip16b", "gb1")

    p.wait("tensor", "ones")
    p.wait("vector", "ones")
    for ti, (tt0, w) in enumerate(tts):
        pre = f"T{ti}."
        lo = max(tt0, 3)
        wo = tt0 + w - lo

        emit_ln(p, t, pre + "ln1.", w, 32,
                lambda kt, tt0=tt0, w=w: t["x_d"][kt * 128:(kt + 1) * 128,
                                                  tt0:tt0 + w],
                lambda kt, w=w: t["xnsl"][:, kt * 512:kt * 512 + w],
                f"T{ti - 1}.ipdone",
                f"T{ti - 1}.ln2.var", f"T{ti - 1}.ln2.bce1")

        if ti == 0:
            for wb in range(4):
                p.wait("tensor", f"dq_done{wb}")
            p.wait("tensor", "hb")

        def evac_ffn1(p, nq, ns, ev, pre=pre, tt0=tt0, w=w, lo=lo, wo=wo):
            n = ev
            p.wait("vector", f"{pre}f1.g_{nq}")
            p.wait("vector", f"{pre}f1.std_{ev - 1}")
            if ev == 0:
                p.wait("vector", f"T{ti - 1}.f1.std_31")
            p.do("vector", lambda e, ns=ns, n=n, w=w: e.tensor_scalar(
                out=t["t1"][:, :w], in0=t["ps"][ns][:, :w],
                scalar1=0.5, scalar2=t["hb"][:, n:n + 1],
                op0=ALU.mult, op1=ALU.add), inc=("v", 1))
            p.mark(f"{pre}f1.free_{nq}_{ns}", "v")
            p.mark(f"{pre}f1.t1_{ev}", "v")
            xp = ev % 2
            if ev < 2:
                p.wait("sync", f"{pre}ln1.done")
            p.wait("sync", f"{pre}f1.x1_{ev - 2}")
            p.do("sync", lambda e, n=n, tt0=tt0, w=w, xp=xp: e.dma_start(
                out=t["xsrc"][xp][:, :w], in_=t["x_d"][n * 128:(n + 1) * 128,
                                                       tt0:tt0 + w]),
                inc=(f"x{xp}", 16))
            p.mark(f"{pre}f1.xl_{ev}", f"x{xp}")
            p.wait("vector", f"{pre}f1.xl_{ev}")
            p.wait_self("vector", "v")
            p.do("vector", lambda e, w=w, xp=xp: e.tensor_add(
                out=t["x1t"][:, :w], in0=t["t1"][:, :w], in1=t["xsrc"][xp][:, :w]),
                inc=("v", 1))
            p.mark(f"{pre}f1.x1_{ev}", "v")
            p.wait("sync", f"{pre}f1.x1_{ev}")
            p.do("sync", lambda e, n=n, tt0=tt0, w=w: e.dma_start(
                out=t["x1f"][n * 128:(n + 1) * 128, tt0:tt0 + w],
                in_=t["x1t"][:, :w]), inc=("sx1", 16))
            if wo > 0:
                p.do("sync", lambda e, n=n, lo=lo, wo=wo, tt0=tt0: e.dma_start(
                    out=t["x1_o"][n * 128:(n + 1) * 128, lo - 3:lo - 3 + wo],
                    in_=t["x1t"][:, lo - tt0:lo - tt0 + wo]), inc=("sx1", 16))
            p.mark(f"{pre}f1.std_{ev}", "sx1")
            p.mark(f"{pre}f1.stdk_{ev}", "sx1")

        emit_ffn(p, t, pre + "f1.", w,
                 lambda k, nq: t["wt_dram"][k * 128:(k + 1) * 128,
                                            nq * 512:(nq + 1) * 512],
                 evac_ffn1, f"{pre}ln1.done",
                 (f"T{ti - 1}.ipdone", f"T{ti - 1}.ssmev"),
                 tuple(f"dq_done{i}" for i in range(4)) if ti == 0 else ())
        p.mark(f"{pre}ffn1t", "tn")

        emit_ln(p, t, pre + "ln2.", w, 32,
                lambda kt, tt0=tt0, w=w: t["x1f"][kt * 128:(kt + 1) * 128,
                                                  tt0:tt0 + w],
                lambda kt, w=w: t["xnsl"][:, kt * 512:kt * 512 + w],
                f"{pre}ffn1t",
                f"{pre}ln1.var", f"{pre}ln1.bce1",
                src_guard_fn=lambda kt, pre=pre: f"{pre}f1.stdk_{kt}")

        for cq in range(32):
            for k in range(32):
                pb = k % 4
                qe = "sync" if k % 2 == 0 else "scalar"
                if p.lastw.get(pb):
                    p.wait(qe, p.lastw[pb])
                if cq == 0 and k < 4 and ti == 0:
                    p.wait(qe, "wip16")
                    p.wait(qe, "wip16b")
                p.do(qe, lambda e, cq=cq, k=k, pb=pb: e.dma_start(
                    out=t["wstr"][pb][:],
                    in_=t["wip16"][k * 128:(k + 1) * 128,
                                   cq * 512:(cq + 1) * 512]), inc=(f"sw{pb}", 16))
                p.mark(f"{pre}ip.w_{cq}_{k}", f"sw{pb}")
                p.wait("tensor", f"{pre}ip.w_{cq}_{k}")
                if k == 0:
                    if cq == 0:
                        p.wait("tensor", f"{pre}ln2.done")
                        if ti == 0:
                            p.wait("tensor", "ljt")
                    for cs in range(4):
                        p.wait("tensor", f"{pre}ip.free_{cq - 1}_{cs}")
                for cs in range(4):
                    p.do("tensor", lambda e, k=k, cs=cs, pb=pb, w=w: e.matmul(
                        t["ps"][cs][:, :w], t["wstr"][pb][:, cs * 128:(cs + 1) * 128],
                        t["xnsl"][:, k * 512:k * 512 + w],
                        start=(k == 0), stop=(k == 31)),
                        inc=("tn", 1) if cs == 3 else None)
                p.mark(f"{pre}ip.c_{cq}_{k}", "tn")
                p.lastw[pb] = f"{pre}ip.c_{cq}_{k}"
            p.mark(f"{pre}ip.g_{cq}", "tn")
            for cs in range(4):
                c = cq * 4 + cs
                if c < 64:
                    p.wait("scalar", f"{pre}ip.g_{cq}")
                    if c == 0:
                        p.wait("scalar", f"T{ti - 1}.xcd_127")
                    p.wait("scalar", f"{pre}cv.done_{c - 1}")
                    p.do("scalar", lambda e, cs=cs, w=w: e.activation(
                        out=t["xtb"][:, 3:3 + w], in_=t["ps"][cs][:, :w],
                        func=AF.Copy), inc=("a", 1))
                    p.mark(f"{pre}ip.free_{cq}_{cs}", "a")
                    if ti > 0:
                        p.wait_self("scalar", "a")
                        p.do("scalar", lambda e, c=c: e.activation(
                            out=t["xtb"][:, 0:3], in_=t["tails"][:, c * 3:c * 3 + 3],
                            func=AF.Copy), inc=("a", 1))
                    p.mark(f"{pre}xt_{c}", "a")
                    if ti == 0:
                        p.wait("vector", f"{pre}xt_{c}")
                        p.wait("vector", "hmaskld")
                        p.do("vector", lambda e: e.tensor_mul(
                            out=t["xtb"][:, 3:6], in0=t["xtb"][:, 3:6],
                            in1=t["hmask"][:, 0:3]), inc=("v", 1))
                        p.mark(f"{pre}xt_{c}", "v")
                        p.wait("scalar", f"{pre}xt_{c}")
                    b0 = lo - tt0
                    p.wait("scalar", f"{pre}cv.acc_{c - 1}")
                    p.wait_self("scalar", "a")
                    p.do("scalar", lambda e, c=c, b0=b0, wo=wo: e.activation(
                        out=t["accs"][0][:, :wo], in_=t["xtb"][:, b0:b0 + wo],
                        func=AF.Copy, scale=t["convw"][:, c * 4:c * 4 + 1]),
                        inc=("a", 1))
                    p.mark(f"{pre}cv0_{c}", "a")
                    p.wait("vector", f"{pre}cv0_{c}")
                    p.wait("vector", f"{pre}cv.vdone_{c - 1}")
                    for j in (1, 2, 3):
                        jsrc, jdst = (j + 1) % 2, j % 2
                        if j > 1:
                            p.wait_self("vector", "v")
                        p.do("vector", lambda e, c=c, j=j, b0=b0, wo=wo,
                             jsrc=jsrc, jdst=jdst: e.scalar_tensor_tensor(
                                 out=t["accs"][jdst][:, :wo],
                                 in0=t["xtb"][:, b0 + j:b0 + j + wo],
                                 scalar=t["convw"][:, c * 4 + j:c * 4 + j + 1],
                                 in1=t["accs"][jsrc][:, :wo],
                                 op0=ALU.mult, op1=ALU.add), inc=("v", 1))
                    p.mark(f"{pre}cvt_{c}", "v")
                    p.mark(f"{pre}cv.vdone_{c}", "v")
                    p.wait("scalar", f"{pre}cvt_{c}")
                    if c == 0:
                        p.wait("scalar", f"T{ti - 1}.xcd_127")
                    p.wait("scalar", f"{pre}xcd_{c - 1}")
                    p.wait("scalar", f"{pre}xpj_{c - 1}")
                    p.do("scalar", lambda e, c=c, wo=wo: e.activation(
                        out=t["xcb"][:, :wo], in_=t["accs"][1][:, :wo],
                        func=AF.Silu, bias=t["convb"][:, c:c + 1]), inc=("a", 1))
                    p.mark(f"{pre}xc_{c}", "a")
                    p.mark(f"{pre}cv.acc_{c}", "a")
                    p.wait("sync", f"{pre}xc_{c}")
                    p.do("sync", lambda e, c=c, lo=lo, wo=wo: e.dma_start(
                        out=t["xc_o"][c * 128:(c + 1) * 128, lo - 3:lo - 3 + wo],
                        in_=t["xcb"][:, :wo]), inc=("sxc", 16))
                    p.mark(f"{pre}xcdma_{c}", "sxc")
                    p.wait("tensor", f"{pre}xc_{c}")
                    for (ei, dst, pst, elo, ehi) in (
                            (0, "ssm0", "ps_e", 0, 128),
                            (1, "ssm1", "ps_e", 128, 256),
                            (2, "ssm2", "ps_x2", 256, 288)):
                        if ei == 0:
                            p.wait("tensor", f"{pre}xa1_{c - 1}")
                        elif ei == 1:
                            p.wait("tensor", f"{pre}xa0_{c}")
                        else:
                            p.wait("tensor", f"{pre}xa2_{c - 1}")
                        p.do("tensor", lambda e, c=c, wo=wo, pst=pst, elo=elo,
                             ehi=ehi: e.matmul(
                                 t[pst][:, :wo] if pst == "ps_x2" else
                                 t[pst][:, :wo],
                                 t["xpj"][:].rearrange("q (c r) -> q c r", r=288)
                                 [:, c, elo:ehi],
                                 t["xcb"][:, :wo], start=True, stop=True),
                             inc=("tn", 1))
                        p.mark(f"{pre}xm{ei}_{c}", "tn")
                        p.wait("vector", f"{pre}xm{ei}_{c}")
                        if c > 0:
                            p.wait_self("vector", "v")
                        if c == 0:
                            p.do("vector", lambda e, dst=dst, lo=lo, wo=wo,
                                 pst=pst: e.tensor_copy(
                                     out=t[dst][:, lo - 3:lo - 3 + wo],
                                     in_=t[pst][:, :wo]), inc=("v", 1))
                        else:
                            p.do("vector", lambda e, dst=dst, lo=lo, wo=wo,
                                 pst=pst: e.tensor_tensor(
                                     out=t[dst][:, lo - 3:lo - 3 + wo],
                                     in0=t[pst][:, :wo],
                                     in1=t[dst][:, lo - 3:lo - 3 + wo],
                                     op=ALU.add), inc=("v", 1))
                        p.mark(f"{pre}xa{ei}_{c}", "v")
                    p.mark(f"{pre}xpj_{c}", "v")
                    if ti < len(tts) - 1:
                        p.wait("vector", f"{pre}xt_{c}")
                        p.do("vector", lambda e, c=c, w=w: e.tensor_copy(
                            out=t["tails"][:, c * 3:c * 3 + 3],
                            in_=t["xtb"][:, w:w + 3]), inc=("v", 1))
                    p.mark(f"{pre}cv.done_{c}", "v")
                else:
                    p.wait("scalar", f"{pre}ip.g_{cq}")
                    p.wait("scalar", f"{pre}xcd_{c - 1}")
                    p.wait("scalar", f"{pre}xpj_{c - 1}")
                    p.do("scalar", lambda e, cs=cs, lo=lo, tt0=tt0, wo=wo:
                         e.activation(out=t["xcb"][:, :wo],
                                      in_=t["ps"][cs][:, lo - tt0:lo - tt0 + wo],
                                      func=AF.Silu), inc=("a", 1))
                    p.mark(f"{pre}ip.free_{cq}_{cs}", "a")
                    p.mark(f"{pre}z_{c}", "a")
                    p.wait("sync", f"{pre}z_{c}")
                    p.do("sync", lambda e, c=c, lo=lo, wo=wo: e.dma_start(
                        out=t["sz_o"][(c - 64) * 128:(c - 63) * 128,
                                      lo - 3:lo - 3 + wo],
                        in_=t["xcb"][:, :wo]), inc=("sxc", 16))
                p.mark(f"{pre}xcd_{c}", "sxc")
        p.mark(f"{pre}ipdone", "tn")
        p.mark(f"T{ti}.ssmev", "v")
    p.mark("ssm_done", "v")

    n_tt_last = len(tts) - 1
    p.wait("gpsimd", f"T{n_tt_last}.ipdone")
    p.do("gpsimd", lambda e: e.dma_start(
        out=t["dtpj"][:, 0:2 * DIN].rearrange("q (c d) -> q c d", d=DIN),
        in_=t["dtprojT_d"][:].rearrange("(c q) d -> q c d", q=128)), inc=("gj", 16))
    p.mark("dtpjld", "gj")

    n_rt = (n_real + 511) // 512
    for rt in range(n_rt):
        o = rt * 512
        w = min(512, n_real - o)
        pre = f"R{rt}."
        p.wait("scalar", "ssm_done")
        p.wait("scalar", f"R{rt - 1}.m2")
        p.do("scalar", lambda e, o=o, w=w: e.activation(
            out=t["xb"][:, :w], in_=t["ssm0"][:, o:o + w], func=AF.Square),
            inc=("a", 1))
        p.mark(f"{pre}s0", "a")
        if rt == 0:
            p.wait("tensor", "ssm_done")
            p.wait("tensor", f"T{len(tts) - 1}.ln2.var")
            p.wait("tensor", f"T{len(tts) - 1}.ln2.bce1")
        p.wait("tensor", f"{pre}s0")
        p.wait("tensor", f"R{rt - 1}.mean")
        p.do("tensor", lambda e, w=w: e.matmul(
            t["ps_row"][0:1, 0:w], t["onesb"][:], t["xb"][:, :w],
            start=True, stop=False), inc=("tn", 1))
        p.mark(f"{pre}m0", "tn")
        p.wait("scalar", f"{pre}m0")
        p.do("scalar", lambda e, o=o, w=w: e.activation(
            out=t["xb"][:, :w], in_=t["ssm1"][:, o:o + w], func=AF.Square),
            inc=("a", 1))
        p.mark(f"{pre}s1", "a")
        p.wait("tensor", f"{pre}s1")
        p.do("tensor", lambda e, w=w: e.matmul(
            t["ps_row"][0:1, 0:w], t["onesb"][:], t["xb"][:, :w],
            start=False, stop=True), inc=("tn", 1))
        p.mark(f"{pre}m1", "tn")
        p.wait("scalar", f"{pre}m1")
        p.do("scalar", lambda e, o=o, w=w: e.activation(
            out=t["xb"][0:32, :w], in_=t["ssm2"][:, o:o + w], func=AF.Square),
            inc=("a", 1))
        p.mark(f"{pre}s2", "a")
        p.wait("tensor", f"{pre}s2")
        if rt == 0:
            p.wait("tensor", "bpats")
        p.do("tensor", lambda e, w=w: e.matmul(
            t["ps_row"][0:2, 512:512 + w], t["bpat"][:], t["xb"][0:32, :w],
            start=True, stop=True), inc=("tn", 1))
        p.mark(f"{pre}m2", "tn")
        p.wait("vector", f"{pre}m1")
        p.wait("vector", f"{pre}m2")
        p.wait("vector", f"R{rt - 1}.dtrn")
        p.do("vector", lambda e, w=w: e.tensor_scalar(
            out=t["rows"][0:1, :w], in0=t["ps_row"][0:1, 0:w],
            scalar1=1.0 / 256, scalar2=None, op0=ALU.mult), inc=("v", 1))
        p.do("vector", lambda e, w=w: e.tensor_scalar(
            out=t["rows2"][0:2, :w], in0=t["ps_row"][0:2, 512:512 + w],
            scalar1=1.0 / 16, scalar2=None, op0=ALU.mult), inc=("v", 1))
        p.mark(f"{pre}mean", "v")
        p.wait("scalar", f"{pre}mean")
        p.do("scalar", lambda e, w=w: e.activation(
            out=t["rows"][0:1, :w], in_=t["rows"][0:1, :w], func=AF.Ln,
            bias=EPS_RMS), inc=("a", 1))
        p.do("scalar", lambda e, w=w: e.activation(
            out=t["rows2"][0:2, :w], in_=t["rows2"][0:2, :w], func=AF.Ln,
            bias=EPS_RMS), inc=("a", 1))
        p.wait_self("scalar", "a")
        p.do("scalar", lambda e, w=w: e.activation(
            out=t["rows"][0:1, :w], in_=t["rows"][0:1, :w], func=AF.Exp,
            scale=-0.5), inc=("a", 1))
        p.do("scalar", lambda e, w=w: e.activation(
            out=t["rows2"][0:2, :w], in_=t["rows2"][0:2, :w], func=AF.Exp,
            scale=-0.5), inc=("a", 1))
        p.mark(f"{pre}rstd", "a")
        p.wait("vector", f"{pre}rstd")
        p.do("vector", lambda e, w=w: e.tensor_copy(
            out=t["rowb"][0:1, 0:w], in_=t["rows"][0:1, :w]))
        p.do("vector", lambda e, w=w: e.tensor_copy(
            out=t["rows2b"][:, :w], in_=t["rows2"][0:2, :w]), inc=("v", 1))
        p.mark(f"{pre}rowb", "v")
        p.wait("tensor", f"{pre}rowb")
        p.wait("tensor", f"R{rt - 1}.dtrn")
        p.do("tensor", lambda e, w=w: e.matmul(
            t["ps_e"][:, 0:w], t["ones1"][:], t["rowb"][0:1, 0:w],
            start=True, stop=True), inc=("tn", 1))
        p.mark(f"{pre}bcmm", "tn")
        p.wait("scalar", f"{pre}bcmm")
        p.wait("scalar", f"R{rt - 1}.dtrn")
        p.do("scalar", lambda e, w=w: e.activation(
            out=t["bc32"][:, 0:w], in_=t["ps_e"][:, 0:w], func=AF.Copy),
            inc=("a", 1))
        p.mark(f"{pre}bcev", "a")
        for kt in range(2):
            p.wait("vector", f"{pre}bcev")
            p.do("vector", lambda e, kt=kt, o=o, w=w: e.scalar_tensor_tensor(
                out=t["dtrn"][:, kt * 512:kt * 512 + w],
                in0=t["ssm0" if kt == 0 else "ssm1"][:, o:o + w],
                scalar=t["dtln"][:, kt:kt + 1], in1=t["bc32"][:, 0:w],
                op0=ALU.mult, op1=ALU.mult), inc=("v", 1))
        p.mark(f"{pre}dtrn", "v")
        p.wait("tensor", f"{pre}rowb")
        p.wait("tensor", f"R{rt - 1}.bcn")
        p.do("tensor", lambda e, w=w: e.matmul(
            t["ps_x2"][:, :w], t["bpat2"][:], t["rows2b"][:, :w],
            start=True, stop=True), inc=("tn", 1))
        p.mark(f"{pre}bcbc", "tn")
        p.wait("vector", f"{pre}bcbc")
        p.wait("vector", f"R{rt - 1}.bcnd")
        p.do("vector", lambda e, o=o, w=w: e.scalar_tensor_tensor(
            out=t["bcb16"][:, :w], in0=t["ssm2"][:, o:o + w],
            scalar=t["bcln"][:, 0:1], in1=t["ps_x2"][:, :w],
            op0=ALU.mult, op1=ALU.mult), inc=("v", 1))
        p.mark(f"{pre}bcn", "v")
        p.wait("sync", f"{pre}bcn")
        p.do("sync", lambda e, o=o, w=w: e.dma_start(
            out=t["bc_o"][:, o:o + w], in_=t["bcb16"][:, :w]), inc=("sbc", 16))
        p.mark(f"{pre}bcnd", "sbc")
        for dc in range(64):
            pp = dc % 2
            p.wait("tensor", f"{pre}dtrn")
            p.wait("tensor", f"{pre}dtev_{dc - 2}")
            if rt == 0 and dc < 2:
                p.wait("tensor", "ssm_done")
                p.wait("tensor", "dtpjld")
            for kt in range(2):
                p.do("tensor", lambda e, dc=dc, kt=kt, pp=pp, w=w: e.matmul(
                    t["ps"][pp][:, :w],
                    t["dtpj"][:, 0:2 * DIN].rearrange("q (a d) -> q a d", d=DIN)
                    [:, kt, dc * 128:(dc + 1) * 128],
                    t["dtrn"][:, kt * 512:kt * 512 + w],
                    start=(kt == 0), stop=(kt == 1)),
                    inc=("tn", 1) if kt == 1 else None)
            p.mark(f"{pre}dtmm_{dc}", "tn")
            po = pp * 512
            p.wait("scalar", f"{pre}dtmm_{dc}")
            p.wait("scalar", f"{pre}dtvd_{dc - 2}")
            if dc < 2:
                p.wait("scalar", f"R{rt - 1}.dtvd_{62 + dc}")
            p.do("scalar", lambda e, dc=dc, pp=pp, w=w, po=po: e.activation(
                out=t["sps"][0][:, po:po + w], in_=t["ps"][pp][:, :w], func=AF.Abs,
                bias=t["dtbias"][:, dc:dc + 1]), inc=("a", 1))
            p.wait_self("scalar", "a")
            p.do("scalar", lambda e, w=w, po=po: e.activation(
                out=t["sps"][1][:, po:po + w], in_=t["sps"][0][:, po:po + w],
                func=AF.Exp, scale=-1.0), inc=("a", 1))
            p.wait_self("scalar", "a")
            p.do("scalar", lambda e, w=w, po=po: e.activation(
                out=t["sps"][0][:, po:po + w], in_=t["sps"][1][:, po:po + w],
                func=AF.Ln, bias=1.0), inc=("a", 1))
            p.do("scalar", lambda e, dc=dc, pp=pp, w=w, po=po: e.activation(
                out=t["spr"][:, po:po + w], in_=t["ps"][pp][:, :w], func=AF.Relu,
                bias=t["dtbias"][:, dc:dc + 1]), inc=("a", 1))
            p.mark(f"{pre}dtev_{dc}", "a")
            p.mark(f"{pre}dtsp_{dc}", "a")
            p.wait("vector", f"{pre}dtsp_{dc}")
            p.wait("vector", f"{pre}dtvd_{dc - 2}")
            if dc < 2:
                p.wait("vector", f"R{rt - 1}.dtvd_{62 + dc}")
            p.do("vector", lambda e, w=w, po=po: e.tensor_add(
                out=t["sps"][1][:, po:po + w], in0=t["sps"][0][:, po:po + w],
                in1=t["spr"][:, po:po + w]), inc=("v", 1))
            p.mark(f"{pre}dtv_{dc}", "v")
            p.wait("sync", f"{pre}dtv_{dc}")
            p.do("sync", lambda e, dc=dc, o=o, w=w, po=po: e.dma_start(
                out=t["dt_o"][dc * 128:(dc + 1) * 128, o:o + w],
                in_=t["sps"][1][:, po:po + w]),
                inc=("sdt" if pp == 0 else "so", 16))
            p.mark(f"{pre}dtvd_{dc}", "sdt" if pp == 0 else "so")
        p.mark(f"{pre}fin", "sdt")
        p.mark(f"{pre}fin2", "so")

    finish(nc, p, stack)
    return p


def make_A(n_real=CHUNK):
    marks = {}
    build_A(marks, emit=False, n_real=n_real)
    nc = bass.Bass("TRN2", target_bir_lowering=False)
    with ExitStack() as stack:
        build_A(marks, emit=True, n_real=n_real, nc=nc, stack=stack)
    return nc



def build_B(marks, emit, n_t=16384, batch_starts=(0, 8192), nc=None, stack=None):
    TC = 512
    n_ch = n_t // TC
    p = P(marks, emit)
    t = {}
    if emit:
        dram, sb = nc.dram_tensor, nc.sbuf_tensor
        t["dt_d"] = dram("dt", [1024, n_t], F32, kind="ExternalInput").ap()
        t["xc_d"] = dram("xc", [1024, n_t], BF16, kind="ExternalInput").ap()
        t["sz_d"] = dram("sz", [1024, n_t], BF16, kind="ExternalInput").ap()
        t["bc_d"] = dram("bc", [32, n_t], BF16, kind="ExternalInput").ap()
        t["al_d"] = dram("al", [8, 128, 16], F32, kind="ExternalInput").ap()
        t["dp_d"] = dram("dp", [128, 8], F32, kind="ExternalInput").ap()
        t["y_o"] = dram("yo", [1024, n_t], BF16, kind="ExternalOutput").ap()
        e = stack.enter_context
        t["dt"] = [e(sb(f"dtt{i}", [128, TC], F32)) for i in range(2)]
        t["xcf"] = [e(sb(f"xcf{i}", [128, TC], F32)) for i in range(2)]
        t["szf"] = [e(sb(f"szf{i}", [128, TC], F32)) for i in range(2)]
        t["xcs"] = [e(sb(f"xcs{i}", [128, TC], BF16)) for i in range(2)]
        t["szs"] = [e(sb(f"szs{i}", [128, TC], BF16)) for i in range(2)]
        t["u"] = e(sb("ubuf", [128, TC], BF16))
        t["amat"] = e(sb("amat", [128, 8 * 16], F32))
        t["dmat"] = e(sb("dmat", [128, 8], F32))
        t["Bb"] = [e(sb("Bb0", [128, 16 * TC], BF16))]
        t["Cb"] = [e(sb("Cb0", [128, 16 * TC], BF16))]
        t["av"] = [e(sb(f"av{i}", [128, 8 * TC], BF16)) for i in range(4)]
        t["bv"] = [e(sb(f"bv{i}", [128, 8 * TC], BF16)) for i in range(4)]
        t["hv"] = [e(sb(f"hv{i}", [128, 8 * TC], BF16)) for i in range(4)]
        t["hc"] = e(sb("hcbuf", [128, 8 * TC], BF16))
        t["tr"] = e(sb("trbuf", [128, 4 * TC], BF16))
        t["yh"] = e(sb("yhbuf", [128, 2 * TC], BF16))
        t["yt"] = e(sb("ytbuf", [128, TC], F32))
        t["dxc"] = e(sb("dxcbuf", [128, TC], F32))
        t["yb"] = e(sb("ybbuf", [128, TC], BF16))
        t["hlast"] = e(sb("hlast", [128, 8 * 16], F32))
        p.sems = _mk_sems(nc, stack)

    emit_consts(p, t, nc, stack, (EPS_LN, EPS_RMS, 1.0, -0.5, -1.0, 0.5))
    p.do("sync", lambda e: e.dma_start(
        out=t["amat"][:].rearrange("q (b n) -> q b n", n=16), in_=t["al_d"][:]),
        inc=("si", 16))
    p.do("sync", lambda e: e.dma_start(out=t["dmat"][:], in_=t["dp_d"][:]),
         inc=("si", 16))
    p.mark("smalls", "si")
    p.wait("scalar", "smalls")
    p.do("scalar", lambda e: e.activation(
        out=t["amat"][:], in_=t["amat"][:], func=AF.Exp), inc=("a", 1))
    p.wait_self("scalar", "a")
    p.do("scalar", lambda e: e.activation(
        out=t["amat"][:], in_=t["amat"][:], func=AF.Copy, scale=-1.0),
        inc=("a", 1))
    p.mark("aneg", "a")

    for ch in range(n_ch):
        tc0 = ch * TC
        cb = 0
        p.wait("sync", f"bcused_{ch - 1}")
        p.wait("sync", f"bcusedg_{ch - 1}")
        for (nm, roff) in (("Bb", 0), ("Cb", 16)):
            for hh in range(2):
                p.do("sync", lambda e, nm=nm, roff=roff, hh=hh, tc0=tc0, cb=cb:
                     e.dma_start(
                         out=t[nm][cb][:].rearrange("q (n c) -> q n c", c=TC)
                         [:, hh * 8:(hh + 1) * 8, :],
                         in_=t["bc_d"][roff + hh * 8:roff + (hh + 1) * 8,
                                       tc0:tc0 + TC][None, :, :]
                         .broadcast_to([128, 8, TC])), inc=(f"gb{cb}", 16))
        p.mark(f"bcb_{ch}", f"gb{cb}")
        for blk in range(8):
            it = ch * 8 + blk
            pb = it % 2
            p.wait("sync", f"u_{it - 2}")
            p.wait("sync", f"a_{it - 2}_1")
            p.wait("sync", f"ydone_{it - 2}")
            p.do("sync", lambda e, blk=blk, tc0=tc0, pb=pb: e.dma_start(
                out=t["dt"][pb][:], in_=t["dt_d"][blk * 128:(blk + 1) * 128,
                                                  tc0:tc0 + TC]), inc=(f"b{pb}", 16))
            p.mark(f"ld_{it}", f"b{pb}")
            p.wait("scalar", f"cast_{it - 2}")
            p.do("scalar", lambda e, blk=blk, tc0=tc0, pb=pb: e.dma_start(
                out=t["xcs"][pb][:], in_=t["xc_d"][blk * 128:(blk + 1) * 128,
                                                   tc0:tc0 + TC]), inc=(f"g{pb}", 16))
            p.do("scalar", lambda e, blk=blk, tc0=tc0, pb=pb: e.dma_start(
                out=t["szs"][pb][:], in_=t["sz_d"][blk * 128:(blk + 1) * 128,
                                                   tc0:tc0 + TC]), inc=(f"g{pb}", 16))
            p.mark(f"ldg_{it}", f"g{pb}")
            p.wait("scalar", f"ldg_{it}")
            p.wait("scalar", f"u_{it - 2}")
            p.wait("scalar", f"ydone_{it - 2}")
            p.do("scalar", lambda e, pb=pb: e.activation(
                out=t["xcf"][pb][:], in_=t["xcs"][pb][:], func=AF.Copy),
                inc=("a", 1))
            p.do("scalar", lambda e, pb=pb: e.activation(
                out=t["szf"][pb][:], in_=t["szs"][pb][:], func=AF.Copy),
                inc=("a", 1))
            p.mark(f"cast_{it}", "a")
            for hf in range(2):
                ab = pb * 2 + hf
                p.wait("scalar", f"ld_{it}")
                if hf == 0:
                    p.wait("scalar", "aneg")
                p.wait("scalar", f"scans_{it - 2}_{hf}")
                for n in range(8):
                    nn = hf * 8 + n
                    p.do("scalar", lambda e, blk=blk, nn=nn, n=n, ab=ab, pb=pb:
                         e.activation(
                             out=t["av"][ab][:, n * TC:(n + 1) * TC],
                             in_=t["dt"][pb][:], func=AF.Exp,
                             scale=t["amat"][:, blk * 16 + nn:blk * 16 + nn + 1]),
                         inc=("a", 1) if n == 7 else None)
                p.mark(f"a_{it}_{hf}", "a")
            p.mark(f"a_{it}_1", "a")
            p.wait("vector", f"ld_{it}")
            p.wait("vector", f"cast_{it}")
            p.wait("vector", f"b_{it - 1}_1")
            p.do("vector", lambda e, pb=pb: e.tensor_tensor(
                out=t["u"][:], in0=t["dt"][pb][:], in1=t["xcf"][pb][:],
                op=ALU.mult), inc=("v", 1))
            p.mark(f"u_{it}", "v")
            for hf in range(2):
                ab = pb * 2 + hf
                p.wait("vector", f"bcb_{ch}")
                p.wait("vector", f"scans_{it - 2}_{hf}")
                p.wait_self("vector", "v")
                p.do("vector", lambda e, ab=ab, cb=cb, hf=hf: e.tensor_tensor(
                    out=t["bv"][ab][:].rearrange("q (n c) -> q n c", c=TC),
                    in0=t["u"][:][:, None, :].broadcast_to([128, 8, TC]),
                    in1=t["Bb"][cb][:].rearrange("q (n c) -> q n c", c=TC)
                    [:, hf * 8:(hf + 1) * 8, :], op=ALU.mult), inc=("v", 1))
                p.mark(f"b_{it}_{hf}", "v")
                p.wait("vector", f"a_{it}_{hf}")
                p.wait("vector", f"hcd_{it - 2}_{hf}")
                p.wait_self("vector", "v")
                for n in range(8):
                    nn = hf * 8 + n
                    if tc0 in batch_starts:
                        p.do("vector", lambda e, ab=ab, n=n: e.tensor_tensor_scan(
                            out=t["hv"][ab][:, n * TC:(n + 1) * TC],
                            data0=t["av"][ab][:, n * TC:(n + 1) * TC],
                            data1=t["bv"][ab][:, n * TC:(n + 1) * TC],
                            initial=0.0, op0=ALU.mult, op1=ALU.add),
                            inc=("v", 1) if n == 7 else None)
                    else:
                        p.do("vector", lambda e, ab=ab, n=n, nn=nn, blk=blk:
                             e.tensor_tensor_scan(
                                 out=t["hv"][ab][:, n * TC:(n + 1) * TC],
                                 data0=t["av"][ab][:, n * TC:(n + 1) * TC],
                                 data1=t["bv"][ab][:, n * TC:(n + 1) * TC],
                                 initial=t["hlast"][:, blk * 16 + nn:
                                                    blk * 16 + nn + 1],
                                 op0=ALU.mult, op1=ALU.add),
                             inc=("v", 1) if n == 7 else None)
                p.mark(f"scans_{it}_{hf}", "v")
                p.wait_self("vector", "v")
                p.do("vector", lambda e, ab=ab, blk=blk, hf=hf: e.tensor_copy(
                    out=t["hlast"][:, blk * 16 + hf * 8:blk * 16 + hf * 8 + 8],
                    in_=t["hv"][ab][:].rearrange("q (n c) -> q n c", c=TC)
                    [:, :, TC - 1]), inc=("v", 1))
                p.mark(f"hl_{it}_{hf}", "v")
                p.wait_self("vector", "v")
                p.do("vector", lambda e, ab=ab, cb=cb, hf=hf: e.tensor_tensor(
                    out=t["hc"][:].rearrange("q (n c) -> q n c", c=TC),
                    in0=t["hv"][ab][:].rearrange("q (n c) -> q n c", c=TC),
                    in1=t["Cb"][cb][:].rearrange("q (n c) -> q n c", c=TC)
                    [:, hf * 8:(hf + 1) * 8, :], op=ALU.mult), inc=("v", 1))
                p.mark(f"hcd_{it}_{hf}", "v")
                p.wait_self("vector", "v")
                p.do("vector", lambda e: e.tensor_add(
                    out=t["tr"][:], in0=t["hc"][:, 0:4 * TC],
                    in1=t["hc"][:, 4 * TC:8 * TC]), inc=("v", 1))
                p.wait_self("vector", "v")
                p.do("vector", lambda e: e.tensor_add(
                    out=t["tr"][:, 0:2 * TC], in0=t["tr"][:, 0:2 * TC],
                    in1=t["tr"][:, 2 * TC:4 * TC]), inc=("v", 1))
                p.wait_self("vector", "v")
                p.do("vector", lambda e, hf=hf: e.tensor_add(
                    out=t["yh"][:, hf * TC:(hf + 1) * TC], in0=t["tr"][:, 0:TC],
                    in1=t["tr"][:, TC:2 * TC]), inc=("v", 1))
                p.mark(f"tree_{it}_{hf}", "v")
            p.wait_self("vector", "v")
            p.do("vector", lambda e: e.tensor_add(
                out=t["yt"][:], in0=t["yh"][:, 0:TC], in1=t["yh"][:, TC:2 * TC]),
                inc=("v", 1))
            p.do("vector", lambda e, pb=pb, blk=blk: e.tensor_scalar(
                out=t["dxc"][:], in0=t["xcf"][pb][:],
                scalar1=t["dmat"][:, blk:blk + 1], scalar2=None, op0=ALU.mult),
                inc=("v", 1))
            p.wait_self("vector", "v")
            p.do("vector", lambda e: e.tensor_add(
                out=t["yt"][:], in0=t["yt"][:], in1=t["dxc"][:]), inc=("v", 1))
            p.wait_self("vector", "v")
            p.wait("vector", f"ydma_{it - 1}")
            p.do("vector", lambda e, pb=pb: e.tensor_tensor(
                out=t["yb"][:], in0=t["yt"][:], in1=t["szf"][pb][:],
                op=ALU.mult), inc=("v", 1))
            p.mark(f"ydone_{it}", "v")
            p.wait("sync", f"ydone_{it}")
            p.do("sync", lambda e, blk=blk, tc0=tc0: e.dma_start(
                out=t["y_o"][blk * 128:(blk + 1) * 128, tc0:tc0 + TC],
                in_=t["yb"][:]), inc=("sy", 16))
            p.mark(f"ydma_{it}", "sy")
        p.mark(f"bcused_{ch}", "v")
        p.mark(f"bcusedg_{ch}", "g")

    finish(nc, p, stack)
    return p


def make_B(n_t=16384, batch_starts=(0, 8192)):
    marks = {}
    build_B(marks, emit=False, n_t=n_t, batch_starts=batch_starts)
    nc = bass.Bass("TRN2", target_bir_lowering=False)
    with ExitStack() as stack:
        build_B(marks, emit=True, n_t=n_t, batch_starts=batch_starts,
                nc=nc, stack=stack)
    return nc



def build_C(marks, emit, n_real=CHUNK, nc=None, stack=None):
    tts = ttiles(n_real)
    p = P(marks, emit)
    t = {}
    if emit:
        dram, sb, psa = nc.dram_tensor, nc.sbuf_tensor, nc.psum_tensor
        t["y_d"] = dram("y", [DIN, n_real], BF16, kind="ExternalInput").ap()
        t["x1_d"] = dram("x1", [H, n_real], F32, kind="ExternalInput").ap()
        t["wop_d"] = dram("wop", [DIN, H], F32, kind="ExternalInput").ap()
        t["qw_d"] = dram("qw", [H, 3072], I32, kind="ExternalInput").ap()
        t["qscale_d"] = dram("qscale", [128, 32], F32, kind="ExternalInput").ap()
        t["qzp_d"] = dram("qzp", [128, 32], F32, kind="ExternalInput").ap()
        t["fbias_d"] = dram("fbias", [128, 32], F32, kind="ExternalInput").ap()
        t["out_o"] = dram("outo", [H, n_real], F32, kind="ExternalOutput").ap()
        t["wt_dram"] = dram("wt_dram", [H, H], BF16, kind="Internal").ap()
        t["wop16"] = dram("wop16", [DIN, H], BF16, kind="Internal").ap()
        t["x2f"] = dram("x2f", [H, n_real], F32, kind="Internal").ap()
        e = stack.enter_context
        t["sqw"] = [e(sb(f"sqw{i}", [128, 3072], I32)) for i in range(2)]
        t["wvi"] = e(sb("wvi", [128, 2048], I32))
        t["wvf"] = e(sb("wvf", [128, 2048], F32))
        t["wnb"] = e(sb("wnb", [128, 4096], BF16))
        t["wtt"] = [e(sb(f"wtt{i}", [128, 128], BF16)) for i in range(4)]
        t["qscale"] = e(sb("qscales", [128, 32], F32))
        t["qzp"] = e(sb("qzps", [128, 32], F32))
        t["hb"] = e(sb("hbs", [128, 32], F32))
        t["onesb"] = e(sb("onesb", [128, 1], BF16))
        t["ones1"] = e(sb("ones1", [1, 128], BF16))
        t["xsrc"] = [e(sb(f"xsrc{i}", [128, 512], F32)) for i in range(2)]
        t["xb"] = e(sb("xbuf", [128, 512], BF16))
        t["sqb"] = e(sb("sqbuf", [128, 512], BF16))
        t["tmp32"] = e(sb("tmp32", [128, 512], F32))
        t["rows"] = e(sb("rows", [1, 1024], F32))
        t["rowb"] = e(sb("rowb", [1, 1024], BF16))
        t["bc32"] = e(sb("bc32", [128, 1024], F32))
        t["wstr"] = [e(sb(f"wstr{i}", [128, 512], BF16)) for i in range(4)]
        t["wcf32"] = [e(sb(f"wcf32{i}", [128, 1024], F32)) for i in range(2)]
        t["wcb16"] = [e(sb(f"wcb16{i}", [128, 1024], BF16)) for i in range(2)]
        t["xnsl"] = e(sb("xnsl", [128, 32 * 512], BF16))
        t["ysl"] = e(sb("ysl", [128, 64 * 512], BF16))
        t["t1"] = e(sb("t1buf", [128, 512], F32))
        t["x1t"] = e(sb("x1t", [128, 512], F32))
        t["ps"] = [e(psa(f"psum{i}", [128, 512], F32)) for i in range(5)]
        t["ps_row"] = e(psa("ps_row", [2, 1024], F32))
        t["ps_e"] = t["ps"][4]
        p.sems = _mk_sems(nc, stack)

    emit_consts(p, t, nc, stack, (EPS_LN, EPS_RMS, 1.0, -0.5, -1.0, 0.5))
    for (name, dst) in (("qscale_d", "qscale"), ("qzp_d", "qzp"),
                        ("fbias_d", "hb")):
        p.do("sync", lambda e, name=name, dst=dst: e.dma_start(
            out=t[dst][:], in_=t[name][:]), inc=("si", 16))
    p.mark("smalls", "si")
    p.do("gpsimd", lambda e: e.memset(t["onesb"][:], 1.0))
    p.do("gpsimd", lambda e: e.memset(t["ones1"][:], 1.0), inc=("g", 1))
    p.mark("ones", "g")
    p.wait("vector", "smalls")
    p.do("vector", lambda e: e.tensor_scalar(
        out=t["hb"][:], in0=t["hb"][:], scalar1=0.5, scalar2=None,
        op0=ALU.mult), inc=("v", 1))
    p.mark("hb", "v")
    emit_dequant(p, t)
    for ci in range(256):
        kt, ct = ci // 4, ci % 4
        pb = ci % 2
        sname = "gw" if pb == 0 else "gw2"
        p.wait("sync", f"wipc_{ci - 2}")
        p.do("sync", lambda e, kt=kt, ct=ct, pb=pb: e.dma_start(
            out=t["wcf32"][pb][:], in_=t["wop_d"][kt * 128:(kt + 1) * 128,
                                                  ct * 1024:(ct + 1) * 1024]),
            inc=(sname, 16))
        p.mark(f"wipl_{ci}", sname)
        p.wait("vector", f"wipl_{ci}")
        p.wait("vector", f"wipst_{ci - 2}")
        p.do("vector", lambda e, pb=pb: e.tensor_copy(
            out=t["wcb16"][pb][:], in_=t["wcf32"][pb][:]), inc=("v", 1))
        p.mark(f"wipc_{ci}", "v")
        p.wait("scalar", f"wipc_{ci}")
        p.do("scalar", lambda e, kt=kt, ct=ct, pb=pb: e.dma_start(
            out=t["wop16"][kt * 128:(kt + 1) * 128, ct * 1024:(ct + 1) * 1024],
            in_=t["wcb16"][pb][:]), inc=(f"gb{pb}", 16))
        p.mark(f"wipst_{ci}", f"gb{pb}")
    p.mark("wop16", "gb0")
    p.mark("wop16b", "gb1")

    p.wait("tensor", "ones")
    p.wait("vector", "ones")
    for ti, (tt0, w) in enumerate(tts):
        pre = f"T{ti}."
        p.wait("sync", f"T{ti - 1}.opdone")
        for kt in range(64):
            p.do("sync", lambda e, kt=kt, tt0=tt0, w=w: e.dma_start(
                out=t["ysl"][:, kt * 512:kt * 512 + w],
                in_=t["y_d"][kt * 128:(kt + 1) * 128, tt0:tt0 + w]),
                inc=("sy", 16))
        p.mark(f"{pre}yld", "sy")
        for hq in range(8):
            for k in range(64):
                pb = k % 4
                qe = "sync" if k % 2 == 0 else "scalar"
                if p.lastw.get(pb):
                    p.wait(qe, p.lastw[pb])
                if ti == 0 and hq == 0 and k < 4:
                    p.wait(qe, "wop16")
                    p.wait(qe, "wop16b")
                p.do(qe, lambda e, hq=hq, k=k, pb=pb: e.dma_start(
                    out=t["wstr"][pb][:], in_=t["wop16"][k * 128:(k + 1) * 128,
                                                         hq * 512:(hq + 1) * 512]),
                    inc=(f"sw{pb}", 16))
                p.mark(f"{pre}op.w_{hq}_{k}", f"sw{pb}")
                p.wait("tensor", f"{pre}op.w_{hq}_{k}")
                if k == 0:
                    p.wait("tensor", f"{pre}yld")
                    if ti == 0 and hq == 0:
                        p.wait("tensor", "ones")
                    for ns in range(4):
                        p.wait("tensor", f"{pre}op.free_{hq - 1}_{ns}")
                for ns in range(4):
                    p.do("tensor", lambda e, k=k, ns=ns, pb=pb, w=w: e.matmul(
                        t["ps"][ns][:, :w], t["wstr"][pb][:, ns * 128:(ns + 1) * 128],
                        t["ysl"][:, k * 512:k * 512 + w],
                        start=(k == 0), stop=(k == 63)),
                        inc=("tn", 1) if ns == 3 else None)
                p.mark(f"{pre}op.c_{hq}_{k}", "tn")
                p.lastw[pb] = f"{pre}op.c_{hq}_{k}"
            p.mark(f"{pre}op.g_{hq}", "tn")
            for ns in range(4):
                ev = hq * 4 + ns
                p.wait("sync", f"{pre}op.x2_{ev - 1}")
                p.do("sync", lambda e, ev=ev, tt0=tt0, w=w: e.dma_start(
                    out=t["xsrc"][0][:, :w], in_=t["x1_d"][ev * 128:(ev + 1) * 128,
                                                           tt0:tt0 + w]),
                    inc=("x0", 16))
                p.mark(f"{pre}op.xl_{ev}", "x0")
                p.wait("vector", f"{pre}op.g_{hq}")
                p.wait("vector", f"{pre}op.xl_{ev}")
                p.wait("vector", f"{pre}op.x2d_{ev - 1}")
                if ev == 0:
                    p.wait("vector", f"T{ti - 1}.f2.od_31")
                p.do("vector", lambda e, ns=ns, w=w: e.tensor_add(
                    out=t["x1t"][:, :w], in0=t["ps"][ns][:, :w],
                    in1=t["xsrc"][0][:, :w]), inc=("v", 1))
                p.mark(f"{pre}op.free_{hq}_{ns}", "v")
                p.mark(f"{pre}op.x2_{ev}", "v")
                p.wait("sync", f"{pre}op.x2_{ev}")
                p.do("sync", lambda e, ev=ev, tt0=tt0, w=w: e.dma_start(
                    out=t["x2f"][ev * 128:(ev + 1) * 128, tt0:tt0 + w],
                    in_=t["x1t"][:, :w]), inc=("sx1", 16))
                p.mark(f"{pre}op.x2d_{ev}", "sx1")
                p.mark(f"{pre}op.x2dk_{ev}", "sx1")
        p.mark(f"{pre}opdone", "tn")

        emit_ln(p, t, pre + "ln3.", w, 32,
                lambda kt, tt0=tt0, w=w: t["x2f"][kt * 128:(kt + 1) * 128,
                                                  tt0:tt0 + w],
                lambda kt, w=w: t["xnsl"][:, kt * 512:kt * 512 + w],
                f"T{ti - 1}.f2.mmdone",
                f"T{ti - 1}.ln3.var", f"T{ti - 1}.ln3.bce1",
                src_guard_fn=lambda kt, pre=pre: f"{pre}op.x2dk_{kt}")

        def evac_ffn2(p, nq, ns, ev, pre=pre, tt0=tt0, w=w):
            n = ev
            p.wait("vector", f"{pre}f2.g_{nq}")
            p.wait("vector", f"{pre}f2.od_{ev - 1}")
            if ev == 0:
                p.wait("vector", f"{pre}op.x2d_31")
            p.do("vector", lambda e, ns=ns, n=n, w=w: e.tensor_scalar(
                out=t["t1"][:, :w], in0=t["ps"][ns][:, :w],
                scalar1=0.5, scalar2=t["hb"][:, n:n + 1],
                op0=ALU.mult, op1=ALU.add), inc=("v", 1))
            p.mark(f"{pre}f2.free_{nq}_{ns}", "v")
            if ev == 0:
                p.wait("sync", f"{pre}ln3.done")
            p.wait("sync", f"{pre}f2.o_{ev - 1}")
            p.do("sync", lambda e, n=n, tt0=tt0, w=w: e.dma_start(
                out=t["xsrc"][0][:, :w], in_=t["x2f"][n * 128:(n + 1) * 128,
                                                      tt0:tt0 + w]),
                inc=("x0", 16))
            p.mark(f"{pre}f2.xl_{ev}", "x0")
            p.wait("vector", f"{pre}f2.xl_{ev}")
            p.wait_self("vector", "v")
            p.do("vector", lambda e, w=w: e.tensor_add(
                out=t["x1t"][:, :w], in0=t["t1"][:, :w], in1=t["xsrc"][0][:, :w]),
                inc=("v", 1))
            p.mark(f"{pre}f2.o_{ev}", "v")
            p.wait("sync", f"{pre}f2.o_{ev}")
            p.do("sync", lambda e, n=n, tt0=tt0, w=w: e.dma_start(
                out=t["out_o"][n * 128:(n + 1) * 128, tt0:tt0 + w],
                in_=t["x1t"][:, :w]), inc=("so", 16))
            p.mark(f"{pre}f2.od_{ev}", "so")

        if ti == 0:
            for wb in range(4):
                p.wait("tensor", f"dq_done{wb}")
            p.wait("tensor", "hb")
        emit_ffn(p, t, pre + "f2.", w,
                 lambda k, nq: t["wt_dram"][k * 128:(k + 1) * 128,
                                            nq * 512:(nq + 1) * 512],
                 evac_ffn2, f"{pre}ln3.done", (),
                 tuple(f"dq_done{i}" for i in range(4)) if ti == 0 else ())

    finish(nc, p, stack)
    return p


def make_C(n_real=CHUNK):
    marks = {}
    build_C(marks, emit=False, n_real=n_real)
    nc = bass.Bass("TRN2", target_bir_lowering=False)
    with ExitStack() as stack:
        build_C(marks, emit=True, n_real=n_real, nc=nc, stack=stack)
    return nc



def _col32(v):
    return np.ascontiguousarray(v.reshape(-1, 128).T)


def prep_host(inp):
    g = lambda k: np.asarray(inp[k])
    d = {}
    x0 = g("L0_attn_hidden_states")
    d["xt"] = np.ascontiguousarray(x0.reshape(2 * 8192, H).T)
    d["wip"] = np.ascontiguousarray(g("L0_attn_in_proj_w"))
    d["qw"] = np.ascontiguousarray(g("L0_ffn_qweight"))
    d["qscale"] = _col32(g("L0_ffn_weight_scale").reshape(-1))
    d["qzp"] = _col32(g("L0_ffn_weight_zero_point").reshape(-1))
    d["fbias"] = _col32(g("L0_ffn_bias"))
    d["xprojT"] = np.ascontiguousarray(g("L0_attn_x_proj_w").T)
    d["dtprojT"] = np.ascontiguousarray(g("L0_attn_dt_proj_w").T)
    d["dtbias"] = np.ascontiguousarray(g("L0_attn_dt_bias").reshape(64, 128).T)
    d["dtlnw"] = np.ascontiguousarray(g("L0_attn_dt_ln_w").reshape(2, 128).T)
    d["bclnw"] = np.concatenate([g("L0_attn_B_ln_w"),
                                 g("L0_attn_C_ln_w")]).reshape(32, 1)
    d["convw"] = np.ascontiguousarray(
        g("L0_attn_conv_w").reshape(64, 128, 4).transpose(1, 0, 2).reshape(128, 256))
    d["convb"] = np.ascontiguousarray(g("L0_attn_conv_b").reshape(64, 128).T)
    d["wopT"] = np.ascontiguousarray(g("L0_attn_out_proj_w").T)
    d["al"] = g("L0_attn_A_log")
    d["dp"] = g("L0_attn_D_param")
    return d


def a_inmap(d, c):
    lo = c * CHUNK
    if c % 4 == 0:
        xc = np.concatenate([np.zeros((H, 3), np.float32),
                             d["xt"][:, lo:lo + CHUNK]], axis=1)
    else:
        xc = np.ascontiguousarray(d["xt"][:, lo - 3:lo + CHUNK])
    bpat = np.zeros((32, 2), np.float32)
    bpat[0:16, 0] = 1.0
    bpat[16:32, 1] = 1.0
    hmask = np.ones((128, 4), np.float32)
    if c % 4 == 0:
        hmask[:, 0:3] = 0.0
    return dict(x=xc, hmask=hmask, wip=d["wip"], qw=d["qw"], qscale=d["qscale"],
                qzp=d["qzp"], fbias=d["fbias"], xprojT=d["xprojT"],
                dtprojT=d["dtprojT"], dtbias=d["dtbias"], dtlnw=d["dtlnw"],
                bclnw=d["bclnw"], convw=d["convw"], convb=d["convb"],
                bpat_i=bpat, bpat2_i=np.ascontiguousarray(bpat.T))


def b_inmap(d, ab, j):
    sl = slice(j * 1024, (j + 1) * 1024)
    return dict(dt=np.ascontiguousarray(ab["dt"][sl]),
                xc=np.ascontiguousarray(ab["xc"][sl]),
                sz=np.ascontiguousarray(ab["sz"][sl]),
                bc=ab["bc"],
                al=np.ascontiguousarray(d["al"][sl].reshape(8, 128, 16)),
                dp=np.ascontiguousarray(d["dp"][sl].reshape(8, 128).T))


def c_inmap(d, ab, y_all, c):
    cs = slice(c * CHUNK, (c + 1) * CHUNK)
    return dict(y=np.ascontiguousarray(y_all[:, cs]),
                x1=np.ascontiguousarray(ab["x1"][:, cs]),
                wop=d["wopT"], qw=d["qw"], qscale=d["qscale"],
                qzp=d["qzp"], fbias=d["fbias"])


def kernel(**inp):
    d = prep_host(inp)
    ncA = make_A()
    resA = run_bass_kernel_spmd(ncA, [a_inmap(d, c) for c in range(NCORE)],
                                core_ids=list(range(NCORE)))
    ab = dict(
        x1=np.concatenate([r["x1o"] for r in resA.results], axis=1),
        dt=np.concatenate([r["dto"] for r in resA.results], axis=1),
        xc=np.concatenate([r["xco"] for r in resA.results], axis=1),
        sz=np.concatenate([r["szo"] for r in resA.results], axis=1),
        bc=np.concatenate([r["bco"] for r in resA.results], axis=1),
    )
    ncB = make_B()
    resB = run_bass_kernel_spmd(ncB, [b_inmap(d, ab, j) for j in range(NCORE)],
                                core_ids=list(range(NCORE)))
    y_all = np.concatenate([r["yo"] for r in resB.results], axis=0)
    ncC = make_C()
    resC = run_bass_kernel_spmd(ncC, [c_inmap(d, ab, y_all, c)
                                      for c in range(NCORE)],
                                core_ids=list(range(NCORE)))
    out = np.concatenate([r["outo"] for r in resC.results], axis=1)
    return np.ascontiguousarray(out.T).reshape(2, 8192, H).astype(np.float32)
